# version 04
# baseline (speedup 1.0000x reference)
import contextlib
import numpy as np
import ml_dtypes
import concourse.bass as bass
import concourse.mybir as mybir
from concourse.bass_utils import run_bass_kernel_spmd
from concourse.alu_op_type import AluOpType as OP

F32 = mybir.dt.float32
BF16 = mybir.dt.bfloat16
F16 = mybir.dt.float16
U16 = mybir.dt.uint16
U32 = mybir.dt.uint32
I32 = mybir.dt.int32
AF = mybir.ActivationFunctionType
AX = mybir.AxisListType.X

T = 4096
D = 2048
DK = 1024
NH = 8
II = 5632
I2 = 2 * II
TOWN = 2048
NT = T // 128
NTO = TOWN // 128

MAG_MASKS = [(0x0007, 6, True), (0x0070, 2, True), (0x0700, 2, False), (0x7000, 6, False)]
SGN_MASKS = [(0x0008, 12), (0x0080, 8), (0x0800, 4), (0x8000, 0)]

ENGS = ["sync", "vector", "scalar", "tensor", "gpsimd"]
OWNER = {"pe": "tensor", "ve": "vector", "se": "scalar", "ge": "gpsimd"}


def _pi4(n):
    q = n // 4
    j = np.arange(n)
    return 4 * (j % q) + j // q


class Sched:

    def __init__(self):
        self.ops = {e: [] for e in ENGS}
        self.waited = {}
        self.c = {s: 0 for s in ["pe", "ve", "se", "ge", "dS", "dA", "dG",
                                 "dL0", "dL1", "cc"]}

    def add(self, eng, fn, deps=(), incs=()):
        waits = {}
        for d in deps:
            if d is None:
                continue
            items = d.items() if isinstance(d, dict) else [d]
            for s, v in items:
                waits[s] = max(waits.get(s, 0), v)
        tok = {}
        for s, n in incs:
            self.c[s] += n
            tok[s] = self.c[s]
        self.ops[eng].append((waits, fn, dict(incs)))
        return tok

    def snap(self):
        return dict(self.c)

    def flush(self, block, sems, name=None):
        import contextlib as _cl
        scope = (block.bass.named_scope(name) if name is not None
                 else _cl.nullcontext())
        with scope:
            self._flush_inner(block, sems)

    def _flush_inner(self, block, sems):
        for ename in ENGS:
            oplist = self.ops[ename]
            if not oplist:
                continue

            def body(eng, oplist=oplist, ename=ename):
                waited = self.waited.setdefault(ename, {})
                for waits, fn, dec in oplist:
                    for s in sorted(waits):
                        v = waits[s]
                        if waited.get(s, -1) >= v:
                            continue
                        eng.wait_ge(sems[s], v)
                        waited[s] = v
                    before = dict(ATTACH)
                    fn(eng)
                    for sname, n in dec.items():
                        got = ATTACH[sname] - before[sname]
                        if got != n:
                            import traceback
                            print(f"*** INC MISMATCH {ename} {sname}: declared {n} attached {got} fn={fn}")
                    for sname in ATTACH:
                        if sname not in dec and ATTACH[sname] != before[sname]:
                            print(f"*** UNDECLARED ATTACH {ename} {sname} fn={fn}")
            getattr(block, ename)(body)
        self.ops = {e: [] for e in ENGS}


ATTACH = {s: 0 for s in ["pe", "ve", "se", "ge", "dS", "dA", "dG",
                         "dL0", "dL1", "cc"]}


def build(dev=0):
    for k in ATTACH:
        ATTACH[k] = 0
    nc = bass.Bass("TRN2", target_bir_lowering=False)

    x = nc.dram_tensor("x", [T, D], F32, kind="ExternalInput").ap()
    xown = nc.dram_tensor("xown", [TOWN, D], F32, kind="ExternalInput").ap()
    wq = nc.dram_tensor("wq", [D, DK], F32, kind="ExternalInput").ap()
    wk = nc.dram_tensor("wk", [D, DK], F32, kind="ExternalInput").ap()
    wv = nc.dram_tensor("wv", [D, DK], F32, kind="ExternalInput").ap()
    wo = nc.dram_tensor("wo", [DK, D], F32, kind="ExternalInput").ap()
    lam = nc.dram_tensor("lam", [1, NH], F32, kind="ExternalInput").ap()
    gu16 = nc.dram_tensor("gu16", [I2, D // 4], U16, kind="ExternalInput").ap()
    dn16 = nc.dram_tensor("dn16", [D, II // 4], U16, kind="ExternalInput").ap()
    gus = nc.dram_tensor("gus", [I2, 16], F16, kind="ExternalInput").ap()
    dns = nc.dram_tensor("dns", [D, 44], F16, kind="ExternalInput").ap()
    ofs = nc.dram_tensor("ofs", [1, 1], I32, kind="ExternalInput").ap()
    out = nc.dram_tensor("out", [TOWN, D], F32, kind="ExternalOutput").ap()

    def scratch(name, shape, dt):
        kind = "ExternalOutput" if dev >= 2 else "Internal"
        return nc.dram_tensor(name, shape, dt, kind=kind).ap()

    xn_d = scratch("xn_d", [T, D], BF16)
    xn2_d = scratch("xn2_d", [T + TOWN, D], BF16)
    k_d = scratch("k_d", [DK, T], BF16)
    v_d = scratch("v_d", [T, DK], BF16)
    q_d = scratch("q_d", [DK, T], BF16)
    o_d = scratch("o_d", [DK, T], BF16)
    wgu_d = scratch("wgu_d", [I2, D], BF16)
    wdn_d = scratch("wdn_d", [D, II], BF16)
    h_d = scratch("h_d", [II, TOWN], BF16)
    y0_d = scratch("y0_d", [TOWN, 1024], F32)
    cc_in = nc.dram_tensor("cc_in", [T, D], BF16, kind="Internal").ap()
    cc_out = nc.dram_tensor("cc_out", [TOWN, D], BF16, kind="Internal").ap()
    att_dump = (nc.dram_tensor("att_dump", [T, D], BF16, kind="ExternalOutput").ap()
                if dev >= 2 else None)
    cc_dump = (nc.dram_tensor("cc_dump", [TOWN, D], BF16, kind="ExternalOutput").ap()
               if dev >= 2 else None)
    rstd_dump = (nc.dram_tensor("rstd_dump", [TOWN, 1], F32, kind="ExternalOutput").ap()
                 if dev >= 2 else None)
    y1_dump = (nc.dram_tensor("y1_dump", [TOWN, 1024], F32, kind="ExternalOutput").ap()
               if dev >= 2 else None)
    att7_dump = (nc.dram_tensor("att7_dump", [TOWN, D], BF16, kind="ExternalOutput").ap()
                 if dev >= 2 else None)
    y0l_dump = (nc.dram_tensor("y0l_dump", [TOWN, 1024], F32, kind="ExternalOutput").ap()
                if dev >= 2 else None)
    rstd_dump2 = (nc.dram_tensor("rstd_dump2", [TOWN, 1], F32, kind="ExternalOutput").ap()
                  if dev >= 2 else None)

    S = Sched()
    es = contextlib.ExitStack()
    with es:
        E = es.enter_context
        sems = {n: E(nc.semaphore(n)) for n in
                ["pe", "ve", "se", "ge", "dS", "dA", "dG", "dL0", "dL1", "cc"]}
        block = E(nc.Block())

        def DS(instr):
            ATTACH["dS"] += 16
            return instr.then_inc(sems["dS"], 16)

        def DA(instr):
            ATTACH["dA"] += 16
            return instr.then_inc(sems["dA"], 16)

        def DG(instr):
            ATTACH["dG"] += 16
            return instr.then_inc(sems["dG"], 16)

        def DL(instr, pb):
            ATTACH[f"dL{pb}"] += 16
            return instr.then_inc(sems[f"dL{pb}"], 16)

        def PE(instr):
            ATTACH["pe"] += 1
            return instr.then_inc(sems["pe"], 1)

        def VE(instr):
            ATTACH["ve"] += 1
            return instr.then_inc(sems["ve"], 1)

        def SE(instr):
            ATTACH["se"] += 1
            return instr.then_inc(sems["se"], 1)

        def GE(instr):
            ATTACH["ge"] += 1
            return instr.then_inc(sems["ge"], 1)

        c25 = E(nc.sbuf_tensor("c25", [128, 1], F32))
        cn8 = E(nc.sbuf_tensor("cn8", [128, 1], F32))
        ceps1 = E(nc.sbuf_tensor("ceps1", [128, 1], F32))
        ceps2 = E(nc.sbuf_tensor("ceps2", [128, 1], F32))
        ones_rf = E(nc.sbuf_tensor("ones_rf", [1, 128], F32))
        ones_rb = E(nc.sbuf_tensor("ones_rb", [1, 128], BF16))
        ones_kb = E(nc.sbuf_tensor("ones_kb", [128, 1], BF16))
        dmask = E(nc.sbuf_tensor("dmask", [128, 128], F32))
        lam_sb = E(nc.sbuf_tensor("lam_sb", [1, NH], F32))
        lam_sg = E(nc.sbuf_tensor("lam_sg", [1, NH], F32))
        nlam = E(nc.sbuf_tensor("nlam", [128, NH], F32))
        ofs_sb = E(nc.sbuf_tensor("ofs_sb", [1, 1], I32))
        es0 = contextlib.ExitStack()
        lam_ps = es0.enter_context(nc.psum_tensor("lam_ps", [128, NH], F32))

        def p0_consts(v):
            v.memset(c25[:], 0.25)
            v.memset(cn8[:], -8.0)
            v.memset(ceps1[:], 1e-5)
            v.memset(ceps2[:], 1e-6)
            v.memset(ones_rf[:], 1.0)
            v.memset(ones_rb[:], 1.0)
            VE(v.memset(ones_kb[:], 1.0))
        tk_consts = S.add("vector", p0_consts, incs=[("ve", 1)])

        def p0_mask(g):
            g.memset(dmask[:], 0.0)
            GE(g.affine_select(out=dmask[:], in_=dmask[:], pattern=[[1, 128]],
                               compare_op=OP.is_ge, fill=-1e30, base=0,
                               channel_multiplier=-1))
        tk_mask = S.add("gpsimd", p0_mask, incs=[("ge", 1)])
        tk_ofs = S.add("gpsimd", lambda g: DG(g.dma_start(out=ofs_sb[:], in_=ofs)),
                       incs=[("dG", 16)])
        tk_lam = S.add("gpsimd", lambda g: DG(g.dma_start(out=lam_sb[:], in_=lam)),
                       incs=[("dG", 16)])
        tk_sig = S.add("scalar", lambda s: SE(s.activation(
            out=lam_sg[:], in_=lam_sb[:], func=AF.Sigmoid)),
            deps=[tk_lam], incs=[("se", 1)])
        tk_lmm = S.add("tensor", lambda t: PE(t.matmul(
            lam_ps[:], ones_rf[:], lam_sg[:], start=True, stop=True)),
            deps=[tk_sig, tk_consts], incs=[("pe", 1)])
        tk_nlam = S.add("vector", lambda v: VE(v.tensor_scalar(
            out=nlam[:], in0=lam_ps[:], scalar1=-1.0, scalar2=None, op0=OP.mult)),
            deps=[tk_lmm], incs=[("ve", 1)])
        es0.close()

        with contextlib.ExitStack() as es1:
            E1 = es1.enter_context
            xf = [E1(nc.sbuf_tensor(f"xf{i}", [128, D], F32)) for i in range(2)]
            xnb = [E1(nc.sbuf_tensor(f"xnb{i}", [128, D], BF16)) for i in range(2)]
            negm = [E1(nc.sbuf_tensor(f"negm{i}", [128, 1], F32)) for i in range(2)]
            rstd1 = [E1(nc.sbuf_tensor(f"rstd1{i}", [128, 1], F32)) for i in range(2)]
            ssq1 = [E1(nc.sbuf_tensor(f"ssq1{i}", [128, 1], F32)) for i in range(2)]
            sxs = [E1(nc.sbuf_tensor(f"sxs{i}", [128, 1], F32)) for i in range(2)]
            vart = [E1(nc.sbuf_tensor(f"vart{i}", [128, 1], F32)) for i in range(2)]
            sqd = E1(nc.sbuf_tensor("sqd", [128, D], BF16))

            regs = {}

            def p1_reg(g):
                regs["rt"] = g.alloc_register("rt")
                g.reg_load(regs["rt"], ofs_sb[:1, :1])
            S.add("gpsimd", p1_reg, deps=[tk_ofs])

            tk_xn = [None] * NT
            tk_st = [None] * NT
            for t in range(NT):
                b = t % 2
                tk_ld = S.add("sync", lambda sy, t=t, b=b: DS(sy.dma_start(
                    out=xf[b][:], in_=x[t * 128:(t + 1) * 128, :])),
                    deps=[tk_xn[t - 2] if t >= 2 else None], incs=[("dS", 16)])

                tk_red = S.add("vector", lambda v, b=b: VE(
                    v.reduce_sum(sxs[b][:], xf[b][:], axis=AX)),
                    deps=[tk_ld], incs=[("ve", 1)])
                tk_sum = S.add("vector", lambda v, b=b: VE(
                    v.tensor_scalar(out=negm[b][:], in0=sxs[b][:],
                                    scalar1=-1.0 / D, scalar2=None, op0=OP.mult)),
                    deps=[tk_red], incs=[("ve", 1)])

                def p1_sq(s, b=b):
                    s.memzero(ssq1[b][:])
                    SE(s.activation(out=sqd[:], in_=xf[b][:], func=AF.Square,
                                    accum_out=ssq1[b][:]))
                tk_sq = S.add("scalar", p1_sq, deps=[tk_ld], incs=[("se", 1)])

                tk_va = S.add("vector", lambda v, b=b: VE(
                    v.tensor_scalar(out=vart[b][:], in0=ssq1[b][:],
                                    scalar1=1.0 / D, scalar2=None, op0=OP.mult)),
                    deps=[tk_sq], incs=[("ve", 1)])
                tk_vb = S.add("vector", lambda v, b=b: VE(
                    v.tensor_tensor(out=sxs[b][:], in0=negm[b][:],
                                    in1=negm[b][:], op=OP.mult)),
                    deps=[tk_sum], incs=[("ve", 1)])
                tk_var = S.add("vector", lambda v, b=b: VE(
                    v.scalar_tensor_tensor(out=vart[b][:], in0=sxs[b][:],
                                           scalar=-1.0, in1=vart[b][:],
                                           op0=OP.mult, op1=OP.add)),
                    deps=[tk_va, tk_vb], incs=[("ve", 1)])

                tk_ln1 = S.add("scalar", lambda s, b=b: SE(
                    s.activation(out=ssq1[b][:], in_=vart[b][:], func=AF.Ln,
                                 bias=ceps1[:], scale=1.0)),
                    deps=[tk_var], incs=[("se", 1)])
                tk_rstd = S.add("scalar", lambda s, b=b: SE(
                    s.activation(out=rstd1[b][:], in_=ssq1[b][:], func=AF.Exp,
                                 scale=-0.5)),
                    deps=[tk_ln1], incs=[("se", 1)])

                def p1_xn(v, b=b):
                    xo = bass.AP(xnb[b][:].tensor, xnb[b][:].offset,
                                 [xnb[b][:].ap[0], [1, 512], [512, 4]])
                    xi = bass.AP(xf[b][:].tensor, xf[b][:].offset,
                                 [xf[b][:].ap[0], [4, 512], [1, 4]])
                    VE(v.tensor_scalar(out=xo, in0=xi, scalar1=negm[b][:],
                                       scalar2=rstd1[b][:], op0=OP.add, op1=OP.mult))
                tk_xn[t] = S.add("vector", p1_xn,
                                 deps=[tk_rstd, tk_st[t - 2] if t >= 2 else None],
                                 incs=[("ve", 1)])

                def p1_st(g, t=t, b=b):
                    DG(g.dma_start(out=xn_d[t * 128:(t + 1) * 128, :], in_=xnb[b][:]))
                tk_st[t] = S.add("gpsimd", p1_st, deps=[tk_xn[t]],
                                 incs=[("dG", 16)])

            def p1_xn2(g):
                dyn = bass.AP(xn2_d.tensor, regs["rt"], [[D, T], [1, D]])
                DG(g.dma_start(out=dyn, in_=xn_d))
            tk_p1_done = S.add("gpsimd", p1_xn2, deps=[tk_st[NT - 1]],
                               incs=[("dG", 16)])
            S.flush(block, sems, "P1")

        tk_dq_store = {"gu": [None] * 88, "dn": [[None] * 4 for _ in range(16)]}
        dqs = {"wbuf": 0, "inbuf": 0, "last_w_store": [None, None],
               "last_v": None, "last_din_user": None, "last_sc_user": None}

        def emit_gu_chunk(ci):
            wb = dqs["wbuf"]; dqs["wbuf"] ^= 1
            ib = dqs["inbuf"]; dqs["inbuf"] ^= 1
            tk_ldp = S.add("gpsimd", lambda g, ci=ci, ib=ib: DG(g.dma_start(
                out=dq_in[ib][:], in_=gu16[ci * 128:(ci + 1) * 128, :])),
                deps=[dqs["last_v"]], incs=[("dG", 16)])
            tk_lds = S.add("gpsimd", lambda g, ci=ci: DG(g.dma_start(
                out=dq_scf[:, 0:16], in_=gus[ci * 128:(ci + 1) * 128, :])),
                deps=[dqs["last_sc_user"]], incs=[("dG", 16)])
            tk_scb = S.add("vector", lambda v: VE(v.tensor_scalar(
                out=dq_scb[:, 0:16], in0=dq_scf[:, 0:16], scalar1=4.0,
                scalar2=None, op0=OP.mult)), deps=[tk_lds], incs=[("ve", 1)])

            def s32f(g):
                lo = dq_s32[:, 0:16].bitcast(U16).rearrange(
                    "p (f two) -> p f two", two=2)[:, :, 0]
                hi = dq_s32[:, 0:16].bitcast(U16).rearrange(
                    "p (f two) -> p f two", two=2)[:, :, 1]
                g.tensor_copy(lo, dq_scb[:, 0:16].bitcast(U16))
                GE(g.tensor_copy(hi, dq_scb[:, 0:16].bitcast(U16)))
            tk_s32 = S.add("gpsimd", s32f, deps=[tk_scb], incs=[("ge", 1)])

            def vex(v, ib=ib):
                u32i = dq_in[ib][:].bitcast(U32)
                for k in range(4):
                    m, sh, left = MAG_MASKS[k]
                    v.tensor_scalar(
                        out=dq_nib[:, k * 512:(k + 1) * 512].bitcast(U32),
                        in0=u32i, scalar1=m | (m << 16), scalar2=sh,
                        op0=OP.bitwise_and,
                        op1=OP.logical_shift_left if left else OP.logical_shift_right)
                for k in range(4):
                    m, sh = SGN_MASKS[k]
                    v.tensor_scalar(
                        out=dq_sf[:, k * 512:(k + 1) * 512].bitcast(U32),
                        in0=u32i, scalar1=m | (m << 16), scalar2=sh,
                        op0=OP.bitwise_and, op1=OP.logical_shift_left)
                VE(v.tensor_scalar(out=dq_nib[:].bitcast(U32),
                                   in0=dq_nib[:].bitcast(U32),
                                   scalar1=0x3E003E00, scalar2=None,
                                   op0=OP.bitwise_or))
            tk_ext = S.add("vector", vex, deps=[tk_ldp], incs=[("ve", 1)])
            tk_cor = tk_ext

            def vfin(v, wb=wb):
                v.tensor_scalar(out=dq_corr[:], in0=dq_nib[:].bitcast(BF16),
                                scalar1=2.0, scalar2=-0.25,
                                op0=OP.mult, op1=OP.add)
                v.tensor_tensor(out=dq_mag[:], in0=dq_corr[:],
                                in1=dq_nib[:].bitcast(BF16), op=OP.min)
                sc4 = bass.AP(dq_s32[:].tensor, dq_s32[:].offset,
                              [dq_s32[:].ap[0], [0, 4], [1, 16], [0, 16]])
                v.tensor_tensor(
                    out=dq_sf[:].bitcast(U32).rearrange("p (a g r) -> p a g r",
                                                        a=4, r=16),
                    in0=dq_sf[:].bitcast(U32).rearrange("p (a g r) -> p a g r",
                                                        a=4, r=16),
                    in1=sc4, op=OP.bitwise_or)
                VE(v.tensor_tensor(out=dq_w[wb][:], in0=dq_mag[:],
                                   in1=dq_sf[:].bitcast(BF16), op=OP.mult))
            tk_fin = S.add("vector", vfin,
                           deps=[tk_cor, tk_s32, dqs["last_w_store"][wb]],
                           incs=[("ve", 1)])
            dqs["last_v"] = tk_fin
            dqs["last_sc_user"] = tk_fin
            tk_stw = S.add("gpsimd", lambda g, ci=ci, wb=wb: DG(g.dma_start(
                out=wgu_d[ci * 128:(ci + 1) * 128, :], in_=dq_w[wb][:])),
                deps=[tk_fin], incs=[("dG", 16)])
            dqs["last_w_store"][wb] = tk_stw
            tk_dq_store["gu"][ci] = tk_stw

        def emit_dn_block(db):
            tk_ldp = S.add("gpsimd", lambda g, db=db: DG(g.dma_start(
                out=dq_din[:], in_=dn16[db * 128:(db + 1) * 128, :])),
                deps=[dqs["last_din_user"]], incs=[("dG", 16)])
            tk_lds = S.add("gpsimd", lambda g, db=db: DG(g.dma_start(
                out=dq_scf[:], in_=dns[db * 128:(db + 1) * 128, :])),
                deps=[dqs["last_sc_user"]], incs=[("dG", 16)])
            tk_scb = S.add("vector", lambda v: VE(v.tensor_scalar(
                out=dq_scb[:], in0=dq_scf[:], scalar1=4.0, scalar2=None,
                op0=OP.mult)), deps=[tk_lds], incs=[("ve", 1)])

            def s32f(g):
                lo = dq_s32[:].bitcast(U16).rearrange(
                    "p (f two) -> p f two", two=2)[:, :, 0]
                hi = dq_s32[:].bitcast(U16).rearrange(
                    "p (f two) -> p f two", two=2)[:, :, 1]
                g.tensor_copy(lo, dq_scb[:].bitcast(U16))
                GE(g.tensor_copy(hi, dq_scb[:].bitcast(U16)))
            tk_s32 = S.add("gpsimd", s32f, deps=[tk_scb], incs=[("ge", 1)])

            last_fin = None
            for kp4 in range(4):
                wb = dqs["wbuf"]; dqs["wbuf"] ^= 1
                m, sh, left = MAG_MASKS[kp4]
                ms, shs = SGN_MASKS[kp4]

                def vex(v, m=m, sh=sh, left=left, ms=ms, shs=shs):
                    u32i = dq_din[:].bitcast(U32)
                    v.tensor_scalar(out=dq_nib[:, 0:1408].bitcast(U32), in0=u32i,
                                    scalar1=m | (m << 16), scalar2=sh,
                                    op0=OP.bitwise_and,
                                    op1=OP.logical_shift_left if left
                                    else OP.logical_shift_right)
                    v.tensor_scalar(out=dq_sf[:, 0:1408].bitcast(U32), in0=u32i,
                                    scalar1=ms | (ms << 16), scalar2=shs,
                                    op0=OP.bitwise_and, op1=OP.logical_shift_left)
                    VE(v.tensor_scalar(out=dq_nib[:, 0:1408].bitcast(U32),
                                       in0=dq_nib[:, 0:1408].bitcast(U32),
                                       scalar1=0x3E003E00, scalar2=None,
                                       op0=OP.bitwise_or))
                tk_ext = S.add("vector", vex, deps=[tk_ldp], incs=[("ve", 1)])
                tk_cor = tk_ext

                def vfin(v, wb=wb):
                    v.tensor_scalar(out=dq_corr[:, 0:1408],
                                    in0=dq_nib[:, 0:1408].bitcast(BF16),
                                    scalar1=2.0, scalar2=-0.25,
                                    op0=OP.mult, op1=OP.add)
                    v.tensor_tensor(out=dq_mag[:, 0:1408],
                                    in0=dq_corr[:, 0:1408],
                                    in1=dq_nib[:, 0:1408].bitcast(BF16),
                                    op=OP.min)
                    sc4 = bass.AP(dq_s32[:].tensor, dq_s32[:].offset,
                                  [dq_s32[:].ap[0], [1, 44], [0, 16]])
                    v.tensor_tensor(
                        out=dq_sf[:, 0:1408].bitcast(U32).rearrange(
                            "p (g r) -> p g r", g=44),
                        in0=dq_sf[:, 0:1408].bitcast(U32).rearrange(
                            "p (g r) -> p g r", g=44),
                        in1=sc4, op=OP.bitwise_or)
                    VE(v.tensor_tensor(out=dq_w[wb][:, 0:1408],
                                       in0=dq_mag[:, 0:1408],
                                       in1=dq_sf[:, 0:1408].bitcast(BF16),
                                       op=OP.mult))
                tk_fin = S.add("vector", vfin,
                               deps=[tk_cor, tk_s32, dqs["last_w_store"][wb]],
                               incs=[("ve", 1)])
                dqs["last_v"] = tk_fin
                dqs["last_sc_user"] = tk_fin
                last_fin = tk_fin
                tk_stw = S.add("gpsimd", lambda g, db=db, kp4=kp4, wb=wb: DG(
                    g.dma_start(out=wdn_d[db * 128:(db + 1) * 128,
                                          kp4 * 1408:(kp4 + 1) * 1408],
                                in_=dq_w[wb][:, 0:1408])),
                    deps=[tk_fin], incs=[("dG", 16)])
                dqs["last_w_store"][wb] = tk_stw
                tk_dq_store["dn"][db][kp4] = tk_stw
            dqs["last_din_user"] = last_fin

        dq_units = [lambda ci=ci: emit_gu_chunk(ci) for ci in range(88)]
        dq_units += [lambda db=db: emit_dn_block(db) for db in range(16)]
        dq_iter = iter(dq_units)
        n_dq_per_slot = (len(dq_units) + 63) // 64

        with contextlib.ExitStack() as es2:
            E2 = es2.enter_context
            dq_in = [E2(nc.sbuf_tensor(f"dq_in{i}", [128, 512], U16))
                     for i in range(2)]
            dq_din = E2(nc.sbuf_tensor("dq_din", [128, 1408], U16))
            dq_nib = E2(nc.sbuf_tensor("dq_nib", [128, 2048], U16))
            dq_sf = E2(nc.sbuf_tensor("dq_sf", [128, 2048], U16))
            dq_corr = E2(nc.sbuf_tensor("dq_corr", [128, 2048], BF16))
            dq_mag = E2(nc.sbuf_tensor("dq_mag", [128, 2048], BF16))
            dq_w = [E2(nc.sbuf_tensor(f"dq_w{i}", [128, 2048], BF16))
                    for i in range(2)]
            dq_scf = E2(nc.sbuf_tensor("dq_scf", [128, 44], F16))
            dq_scb = E2(nc.sbuf_tensor("dq_scb", [128, 44], BF16))
            dq_s32 = E2(nc.sbuf_tensor("dq_s32", [128, 44], U32))
            wkb = E2(nc.sbuf_tensor("wkb", [128, 16 * DK], BF16))
            wvb = E2(nc.sbuf_tensor("wvb", [128, 16 * DK], BF16))
            wqb = E2(nc.sbuf_tensor("wqb", [128, 16 * DK], BF16))
            tk_w = {}
            for nm, src, dst in [("k", wk, wkb), ("v", wv, wvb), ("q", wq, wqb)]:
                tk_w[nm] = S.add("gpsimd", lambda g, src=src, dst=dst: DG(
                    g.dma_start(
                        out=dst[:].rearrange("p (a n) -> p a n", a=16),
                        in_=src.rearrange("(a p) n -> p a n", p=128))),
                    incs=[("dG", 16)])
            xnt = [E2(nc.sbuf_tensor(f"xnt{i}", [128, 16 * 512], BF16))
                   for i in range(2)]
            evb = {k: [E2(nc.sbuf_tensor(f"{k}ev{i}", [128, 512], BF16))
                       for i in range(2)] for k in "kqv"}
            pp = {k: [E2(nc.psum_tensor(f"{k}p{i}", [128, 512], F32))
                      for i in range(2)] for k in "kqv"}

            tk_chunk = [None] * 8
            tk_ev = {}
            tk_stc = {}
            for c in range(8):
                cb = c % 2

                def p2_ldx(sy, c=c, cb=cb):
                    for dt in range(16):
                        DS(sy.dma_start(
                            out=xnt[cb][:, dt * 512:(dt + 1) * 512],
                            in_=xn_d[c * 512:(c + 1) * 512, dt * 128:(dt + 1) * 128],
                            transpose=True))
                tk_ldx = S.add("sync", p2_ldx,
                               deps=[tk_p1_done, tk_chunk[c - 2] if c >= 2 else None],
                               incs=[("dS", 256)])

                for kind, wsb, dst in [("k", wkb, k_d), ("q", wqb, q_d),
                                       ("v", wvb, v_d)]:
                    for j in range(8):
                        jb = j % 2
                        key = (kind, c, j)
                        prev = (kind, c, j - 2) if j >= 2 else (kind, c - 1, 6 + j)
                        dep_bank = tk_ev.get(prev)

                        def mms(t, j=j, jb=jb, wsb=wsb, kind=kind, cb=cb):
                            ps = pp[kind][jb]
                            for dt in range(16):
                                if kind in ("k", "q"):
                                    lhs = wsb[:, dt * DK + j * 128:
                                              dt * DK + (j + 1) * 128]
                                    rhs = xnt[cb][:, dt * 512:(dt + 1) * 512]
                                else:
                                    ts4, dc = divmod(j, 2)
                                    lhs = xnt[cb][:, dt * 512 + ts4 * 128:
                                                  dt * 512 + (ts4 + 1) * 128]
                                    rhs = wsb[:, dt * DK + dc * 512:
                                              dt * DK + (dc + 1) * 512]
                                ins = t.matmul(ps[:], lhs, rhs,
                                               start=(dt == 0), stop=(dt == 15))
                            PE(ins)
                        tk_mm = S.add("tensor", mms,
                                      deps=[tk_ldx, dep_bank, tk_w[kind]],
                                      incs=[("pe", 1)])
                        dep_stb = tk_stc.get(prev)

                        def ev(se_, jb=jb, kind=kind):
                            SE(se_.copy(evb[kind][jb][:], pp[kind][jb][:]))
                        tk_ev[key] = S.add("scalar", ev, deps=[tk_mm, dep_stb],
                                           incs=[("se", 1)])

                        def st(s2, j=j, jb=jb, kind=kind, c=c, dst=dst):
                            if kind in ("k", "q"):
                                d = dst[j * 128:(j + 1) * 128, c * 512:(c + 1) * 512]
                            else:
                                ts4, dc = divmod(j, 2)
                                d = dst[c * 512 + ts4 * 128:
                                        c * 512 + (ts4 + 1) * 128,
                                        dc * 512:(dc + 1) * 512]
                            DA(s2.dma_start(out=d, in_=evb[kind][jb][:]))
                        tk_stc[key] = S.add("scalar", st, deps=[tk_ev[key]],
                                            incs=[("dA", 16)])
                        u = next(dq_iter, None)
                        if u is not None:
                            u()
                tk_chunk[c] = tk_ev[("v", c, 7)]
            for u in dq_iter:
                u()
            S.flush(block, sems, "P2")
            p2_gate = S.snap()

        with contextlib.ExitStack() as es4:
            E4 = es4.enter_context
            k_sb = E4(nc.sbuf_tensor("k_sb", [128, NH * T], BF16))
            v_sb = E4(nc.sbuf_tensor("v_sb", [128, NT * DK], BF16))
            qm_sb = [E4(nc.sbuf_tensor(f"qm_sb{i}", [128, NH * 512], BF16))
                     for i in range(2)]
            p12 = [E4(nc.sbuf_tensor(f"p12_{i}", [128, 1024], BF16))
                   for i in range(2)]
            oh_sb = E4(nc.sbuf_tensor("oh_sb", [128, NH * 512], BF16))
            rl_f = E4(nc.sbuf_tensor("rl_f", [1, 1024], F32))
            rl_b = E4(nc.sbuf_tensor("rl_b", [1, 1024], BF16))
            rlb1 = E4(nc.sbuf_tensor("rlb1", [128, 512], BF16))
            rlb2 = E4(nc.sbuf_tensor("rlb2", [128, 512], BF16))
            tcmb = [E4(nc.sbuf_tensor(f"tcmb{i}", [128, 512], BF16))
                    for i in range(2)]
            s_ps = [E4(nc.psum_tensor(f"s_ps{i}", [128, 1024], F32))
                    for i in range(2)]
            o_ps = [E4(nc.psum_tensor(f"o_ps{i}", [128, 512], F32))
                    for i in range(2)]
            l_ps = E4(nc.psum_tensor("l_ps", [128, 1024], F32))

            def p4_ldk(sy):
                for h in range(NH):
                    DS(sy.dma_start(out=k_sb[:, h * T:(h + 1) * T],
                                    in_=k_d[h * 128:(h + 1) * 128, :]))
            tk_ldk = S.add("sync", p4_ldk, deps=[p2_gate], incs=[("dS", 128)])

            def p4_ldv(sy):
                for tt in range(NT):
                    DS(sy.dma_start(out=v_sb[:, tt * DK:(tt + 1) * DK],
                                    in_=v_d[tt * 128:(tt + 1) * 128, :]))
            tk_ldv = S.add("sync", p4_ldv, deps=[p2_gate], incs=[("dS", 512)])

            tk_ldq = [None] * 8
            tk_qm_done = [None] * 8

            def emit_ldq(qm):
                def f(sy, qm=qm):
                    src = bass.AP(q_d.tensor, qm * 512,
                                  [[T, 128], [128 * T, NH], [1, 512]])
                    DS(sy.dma_start(out=qm_sb[qm % 2][:], in_=src))
                tk_ldq[qm] = S.add(
                    "sync", f,
                    deps=[p2_gate, tk_qm_done[qm - 2] if qm >= 2 else None],
                    incs=[("dS", 16)])

            emit_ldq(0)
            emit_ldq(1)

            tk_exp = {}
            tk_avl = {}
            tk_cmb_prev = None
            tk_bce_prev = None
            tk_bc_prev = None
            tk_oh_dma = {}
            for qm in range(8):
                n_kt = 4 * qm + 4
                for hi in range(NH):
                    def emit_skt(kt, hi=hi, qm=qm):
                        par = kt % 2
                        qoff = max(0, (kt - 4 * qm) * 128)

                        def smm(t, hi=hi, kt=kt, qm=qm, par=par, qoff=qoff):
                            l1 = k_sb[0:64, hi * T + kt * 128:
                                      hi * T + (kt + 1) * 128]
                            r1 = qm_sb[qm % 2][0:64, hi * 512 + qoff:
                                               (hi + 1) * 512]
                            t.matmul(s_ps[par][:, qoff:512], l1, r1,
                                     start=True, stop=True)
                            l2 = k_sb[64:128, hi * T + kt * 128:
                                      hi * T + (kt + 1) * 128]
                            r2 = qm_sb[qm % 2][64:128, hi * 512 + qoff:
                                               (hi + 1) * 512]
                            PE(t.matmul(s_ps[par][:, 512 + qoff:1024], l2, r2,
                                        start=True, stop=True))
                        tk_s = S.add("tensor", smm,
                                     deps=[tk_ldq[qm], tk_ldk,
                                           tk_exp.get((qm, hi, kt - 2)),
                                           tk_cmb_prev if kt <= 1 else None,
                                           tk_bce_prev if kt <= 1 else None],
                                     incs=[("pe", 1)])

                        if kt >= 4 * qm:
                            def msk(v, par=par, qoff=qoff):
                                v.tensor_tensor(
                                    out=s_ps[par][:, qoff:qoff + 128],
                                    in0=s_ps[par][:, qoff:qoff + 128],
                                    in1=dmask[:], op=OP.add)
                                VE(v.tensor_tensor(
                                    out=s_ps[par][:, 512 + qoff:512 + qoff + 128],
                                    in0=s_ps[par][:, 512 + qoff:512 + qoff + 128],
                                    in1=dmask[:], op=OP.add))
                            tk_m = S.add("vector", msk, deps=[tk_s, tk_mask],
                                         incs=[("ve", 1)])
                        else:
                            tk_m = tk_s

                        def expf(s, par=par, qoff=qoff, kt=kt):
                            pb = p12[kt % 2]
                            if qoff == 0:
                                SE(s.activation(out=pb[:], in_=s_ps[par][:],
                                                func=AF.Exp, bias=cn8[:],
                                                scale=0.125))
                            else:
                                s.activation(out=pb[:, qoff:512],
                                             in_=s_ps[par][:, qoff:512],
                                             func=AF.Exp, bias=cn8[:],
                                             scale=0.125)
                                SE(s.activation(
                                    out=pb[:, 512 + qoff:1024],
                                    in_=s_ps[par][:, 512 + qoff:1024],
                                    func=AF.Exp, bias=cn8[:], scale=0.125))
                        tk_exp[(qm, hi, kt)] = S.add(
                            "scalar", expf,
                            deps=[tk_m, tk_avl.get((qm, hi, kt - 2))],
                            incs=[("se", 1)])

                    def emit_avkt(kt, hi=hi, qm=qm):
                        qoff = max(0, (kt - 4 * qm) * 128)

                        def avl(t, hi=hi, kt=kt, qoff=qoff, qm=qm):
                            pb = p12[kt % 2]
                            vt = v_sb[:, kt * DK + hi * 128:
                                      kt * DK + (hi + 1) * 128]
                            st0 = (kt == 0)
                            sp0 = (kt == 4 * qm + 3)
                            t.matmul(o_ps[0][:, qoff:512], vt, pb[:, qoff:512],
                                     start=st0, stop=sp0, skip_group_check=True)
                            t.matmul(o_ps[1][:, qoff:512], vt,
                                     pb[:, 512 + qoff:1024],
                                     start=st0, stop=sp0, skip_group_check=True)
                            t.matmul(l_ps[0:1, qoff:512], ones_kb[:],
                                     pb[:, qoff:512],
                                     start=st0, stop=sp0, skip_group_check=True)
                            PE(t.matmul(l_ps[0:1, 512 + qoff:1024], ones_kb[:],
                                        pb[:, 512 + qoff:1024],
                                        start=st0, stop=sp0,
                                        skip_group_check=True))
                        tk_avl[(qm, hi, kt)] = S.add(
                            "tensor", avl,
                            deps=[tk_exp[(qm, hi, kt)], tk_ldv,
                                  tk_cmb_prev if kt <= 1 else None],
                            incs=[("pe", 1)])

                    for kt in range(n_kt + 1):
                        if kt < n_kt:
                            emit_skt(kt)
                        if kt >= 1:
                            emit_avkt(kt - 1)

                    last_av = tk_avl[(qm, hi, n_kt - 1)]

                    tk_rla = S.add("scalar", lambda s: SE(
                        s.activation(out=rl_f[:], in_=l_ps[0:1, :], func=AF.Ln)),
                        deps=[last_av], incs=[("se", 1)])
                    tk_rl = S.add("scalar", lambda s: SE(
                        s.activation(out=rl_f[:], in_=rl_f[:], func=AF.Exp,
                                     scale=-1.0)),
                        deps=[tk_rla], incs=[("se", 1)])

                    tk_rn = S.add("vector", lambda v, hi=hi: VE(
                        v.tensor_scalar(out=rl_f[0:1, 512:1024],
                                        in0=rl_f[0:1, 512:1024],
                                        scalar1=nlam[0:1, hi:hi + 1],
                                        scalar2=None, op0=OP.mult)),
                        deps=[tk_rl, tk_nlam], incs=[("ve", 1)])
                    tk_rlb = S.add("vector", lambda v: VE(
                        v.tensor_copy(rl_b[:], rl_f[:])),
                        deps=[tk_rn, tk_bc_prev], incs=[("ve", 1)])

                    def bc(t):
                        t.matmul(s_ps[0][:, 0:512], ones_rb[:],
                                 rl_b[0:1, 0:512], start=True, stop=True)
                        PE(t.matmul(s_ps[0][:, 512:1024], ones_rb[:],
                                    rl_b[0:1, 512:1024], start=True, stop=True))
                    tk_bc = S.add("tensor", bc,
                                  deps=[tk_rlb, tk_exp[(qm, hi, n_kt - 1)]],
                                  incs=[("pe", 1)])
                    tk_bc_prev = tk_bc

                    def bce(s):
                        s.copy(rlb1[:], s_ps[0][:, 0:512])
                        SE(s.copy(rlb2[:], s_ps[0][:, 512:1024]))
                    tk_bce = S.add("scalar", bce, deps=[tk_bc], incs=[("se", 1)])
                    tk_bce_prev = tk_bce

                    tk_c1 = S.add("vector", lambda v: VE(
                        v.tensor_tensor(out=tcmb[0][:], in0=o_ps[0][:],
                                        in1=rlb1[:], op=OP.mult)),
                        deps=[tk_bce, last_av], incs=[("ve", 1)])
                    tk_c2 = S.add("vector", lambda v: VE(
                        v.tensor_tensor(out=tcmb[1][:], in0=o_ps[1][:],
                                        in1=rlb2[:], op=OP.mult)),
                        deps=[tk_bce, last_av], incs=[("ve", 1)])
                    tk_cmb = S.add("vector", lambda v, hi=hi: VE(
                        v.tensor_tensor(out=oh_sb[:, hi * 512:(hi + 1) * 512],
                                        in0=tcmb[0][:], in1=tcmb[1][:],
                                        op=OP.add)),
                        deps=[tk_c1, tk_c2, tk_oh_dma.get((qm - 1, hi))],
                        incs=[("ve", 1)])
                    tk_cmb_prev = tk_cmb
                    tk_oh_dma[(qm, hi)] = S.add(
                        "scalar", lambda s2, hi=hi, qm=qm: DA(s2.dma_start(
                            out=o_d[hi * 128:(hi + 1) * 128,
                                    qm * 512:(qm + 1) * 512],
                            in_=oh_sb[:, hi * 512:(hi + 1) * 512])),
                        deps=[tk_cmb], incs=[("dA", 16)])
                tk_qm_done[qm] = tk_cmb_prev
                if qm + 2 < 8:
                    emit_ldq(qm + 2)
            S.flush(block, sems, "P4")
            p4_gate = S.snap()

        es56 = contextlib.ExitStack()
        xnt6 = es56.enter_context(nc.sbuf_tensor("xnt6", [128, 16 * TOWN], BF16))
        with contextlib.ExitStack() as es5:
            E5 = es5.enter_context
            wob = E5(nc.sbuf_tensor("wob", [128, NH * D], BF16))
            otl = [E5(nc.sbuf_tensor(f"otl{i}", [128, NH * 128], BF16))
                   for i in range(2)]
            att_sb = [E5(nc.sbuf_tensor(f"att_sb{i}", [128, D], BF16))
                      for i in range(2)]
            ap_ps = [E5(nc.psum_tensor(f"ap_ps{i}", [128, D], F32))
                     for i in range(2)]

            tk_wo = S.add("gpsimd", lambda g: DG(g.dma_start(
                out=wob[:].rearrange("p (a n) -> p a n", a=NH),
                in_=wo.rearrange("(a p) n -> p a n", p=128))),
                deps=[p4_gate], incs=[("dG", 16)])

            tk_p5 = {}
            tk_ev5 = {}
            tk_attst = {}
            for tt in range(NT):
                tb = tt % 2

                def ldo(sy, tt=tt, tb=tb):
                    src = bass.AP(o_d.tensor, tt * 128,
                                  [[T, 128], [128 * T, NH], [1, 128]])
                    DS(sy.dma_start(out=otl[tb][:], in_=src))
                tk_ldo = S.add("sync", ldo, deps=[p4_gate, tk_p5.get(tt - 2)],
                               incs=[("dS", 16)])

                def p5mm(t, tb=tb):
                    for dv in range(NH):
                        for ch in range(4):
                            ins = t.matmul(
                                ap_ps[tb][:, ch * 512:(ch + 1) * 512],
                                otl[tb][:, dv * 128:(dv + 1) * 128],
                                wob[:, dv * D + ch * 512: dv * D + (ch + 1) * 512],
                                start=(dv == 0), stop=(dv == NH - 1))
                    PE(ins)
                tk_p5[tt] = S.add("tensor", p5mm,
                                  deps=[tk_ldo, tk_wo, tk_ev5.get(tt - 2)],
                                  incs=[("pe", 1)])
                tk_ev5[tt] = S.add("vector", lambda v, tb=tb: VE(
                    v.tensor_copy(att_sb[tb][:], ap_ps[tb][:])),
                    deps=[tk_p5[tt], tk_attst.get(tt - 2)], incs=[("ve", 1)])

                def stc(s2, tt=tt, tb=tb):
                    DA(s2.dma_start(out=cc_in[tt * 128:(tt + 1) * 128, :],
                                    in_=att_sb[tb][:]))
                    if dev >= 2:
                        DA(s2.dma_start(
                            out=att_dump[tt * 128:(tt + 1) * 128, :],
                            in_=att_sb[tb][:]))
                tk_attst[tt] = S.add("scalar", stc, deps=[tk_ev5[tt]],
                                     incs=[("dA", 32 if dev >= 2 else 16)])

            def _ccfn(g):
                ATTACH["cc"] += 1
                return g.collective_compute(
                "ReduceScatter", OP.add,
                replica_groups=[[0, 1], [2, 3], [4, 5], [6, 7]],
                    ins=[cc_in.opt()], outs=[cc_out.opt()]).then_inc(sems["cc"], 1)
            tk_cc = S.add("gpsimd", _ccfn,
                          deps=[tk_attst[NT - 1], {"dA": S.c["dA"]}], incs=[("cc", 1)])

            def ldxn6(sy):
                for dt in range(16):
                    DS(sy.dma_start(out=xnt6[:, dt * TOWN:(dt + 1) * TOWN],
                                    in_=xn2_d[TOWN:TOWN + TOWN,
                                              dt * 128:(dt + 1) * 128],
                                    transpose=True))
            tk_xn6 = S.add("sync", ldxn6, deps=[p4_gate, tk_p1_done],
                           incs=[("dS", 256)])
            if dev >= 2:
                S.add("gpsimd", lambda g: DG(g.dma_start(out=cc_dump, in_=cc_out)),
                      deps=[tk_cc], incs=[("dG", 16)])
            S.flush(block, sems, "P5")
            p5_gate = S.snap()

        with contextlib.ExitStack() as es6:
            E6 = es6.enter_context
            wgt = [E6(nc.sbuf_tensor(f"wgt{i}", [128, 16 * 512], BF16))
                   for i in range(2)]
            wup = [E6(nc.sbuf_tensor(f"wup{i}", [128, 16 * 512], BF16))
                   for i in range(2)]
            gsil = E6(nc.sbuf_tensor("gsil", [128, TOWN], BF16))
            h_sb = [E6(nc.sbuf_tensor(f"h_sb{i}", [128, TOWN], BF16))
                    for i in range(2)]
            gA = E6(nc.psum_tensor("gA", [128, TOWN], F32))
            gB = E6(nc.psum_tensor("gB", [128, TOWN], F32))

            tk_h = {}
            tk_hv = {}
            tk_sil = {}
            tk_blk = [None, None]
            for blk in range(11):
                bb = blk % 2

                def ldw(sy, blk=blk, bb=bb):
                    for dt in range(16):
                        DS(sy.dma_start(
                            out=wgt[bb][:, dt * 512:(dt + 1) * 512],
                            in_=wgu_d[blk * 512:(blk + 1) * 512,
                                      dt * 128:(dt + 1) * 128], transpose=True))
                    for dt in range(16):
                        DS(sy.dma_start(
                            out=wup[bb][:, dt * 512:(dt + 1) * 512],
                            in_=wgu_d[II + blk * 512: II + (blk + 1) * 512,
                                      dt * 128:(dt + 1) * 128], transpose=True))
                tk_ldw = S.add("sync", ldw,
                               deps=[tk_dq_store["gu"][4 * blk + 3],
                                     tk_dq_store["gu"][44 + 4 * blk + 3],
                                     tk_blk[bb], p5_gate],
                               incs=[("dS", 512)])

                for pos in range(4):
                    ot = blk * 4 + pos

                    def gmm(t, bb=bb, pos=pos):
                        for dt in range(16):
                            lhs = wgt[bb][:, dt * 512 + pos * 128:
                                          dt * 512 + (pos + 1) * 128]
                            for ch in range(4):
                                ins = t.matmul(
                                    gA[:, ch * 512:(ch + 1) * 512], lhs,
                                    xnt6[:, dt * TOWN + ch * 512:
                                         dt * TOWN + (ch + 1) * 512],
                                    start=(dt == 0), stop=(dt == 15))
                        PE(ins)
                    tk_g = S.add("tensor", gmm,
                                 deps=[tk_ldw, tk_xn6, tk_sil.get(ot - 1)],
                                 incs=[("pe", 1)])

                    def umm(t, bb=bb, pos=pos):
                        for dt in range(16):
                            lhs = wup[bb][:, dt * 512 + pos * 128:
                                          dt * 512 + (pos + 1) * 128]
                            for ch in range(4):
                                ins = t.matmul(
                                    gB[:, ch * 512:(ch + 1) * 512], lhs,
                                    xnt6[:, dt * TOWN + ch * 512:
                                         dt * TOWN + (ch + 1) * 512],
                                    start=(dt == 0), stop=(dt == 15))
                        PE(ins)
                    tk_u = S.add("tensor", umm,
                                 deps=[tk_ldw, tk_xn6, tk_hv.get(ot - 1)],
                                 incs=[("pe", 1)])

                    def sil(s):
                        for ch in range(4):
                            ins = s.activation(
                                out=gsil[:, ch * 512:(ch + 1) * 512],
                                in_=gA[:, ch * 512:(ch + 1) * 512], func=AF.Silu)
                        SE(ins)
                    tk_sil[ot] = S.add("scalar", sil, deps=[tk_g],
                                       incs=[("se", 1)])
                    hb = ot % 2

                    def hv(v, hb=hb):
                        for ch in range(4):
                            ins = v.tensor_tensor(
                                out=h_sb[hb][:, ch * 512:(ch + 1) * 512],
                                in0=gB[:, ch * 512:(ch + 1) * 512],
                                in1=gsil[:, ch * 512:(ch + 1) * 512], op=OP.mult)
                        VE(ins)
                    tk_hv[ot] = S.add("vector", hv,
                                      deps=[tk_u, tk_sil[ot], tk_h.get(ot - 2)],
                                      incs=[("ve", 1)])
                    tk_h[ot] = S.add("scalar", lambda s2, ot=ot, hb=hb: DA(
                        s2.dma_start(out=h_d[ot * 128:(ot + 1) * 128, :],
                                     in_=h_sb[hb][:])),
                        deps=[tk_hv[ot]], incs=[("dA", 16)])
                tk_blk[bb] = tk_h[blk * 4 + 3]
            S.flush(block, sems, "P6")
            p6_gate = S.snap()
        es56.close()

        with contextlib.ExitStack() as es7:
            E7 = es7.enter_context
            wdt = E7(nc.sbuf_tensor("wdt", [128, 44 * 1024], BF16))
            h7 = [E7(nc.sbuf_tensor(f"h7_{i}", [128, 44 * 128], BF16))
                  for i in range(2)]
            y0st = [E7(nc.sbuf_tensor(f"y0st{i}", [128, 1024], F32))
                    for i in range(2)]
            att7 = [E7(nc.sbuf_tensor(f"att7_{i}", [128, D], BF16))
                    for i in range(2)]
            xf7 = [E7(nc.sbuf_tensor(f"xf7_{i}", [128, D], F32)) for i in range(2)]
            y0l = [E7(nc.sbuf_tensor(f"y0l{i}", [128, 1024], F32))
                   for i in range(2)]
            outb = [E7(nc.sbuf_tensor(f"outb{i}", [128, D], F32))
                    for i in range(2)]
            ssqa = E7(nc.sbuf_tensor("ssqa", [128, NTO], F32))
            ssqh = [E7(nc.sbuf_tensor(f"ssqh{i}", [128, 1], F32)) for i in range(2)]
            ssqt = [E7(nc.sbuf_tensor(f"ssqt{i}", [128, 1], F32)) for i in range(2)]
            lnv7 = [E7(nc.sbuf_tensor(f"lnv7_{i}", [128, 1], F32))
                    for i in range(2)]
            rstd7 = [E7(nc.sbuf_tensor(f"rstd7_{i}", [128, 1], F32))
                     for i in range(2)]
            sq7 = E7(nc.sbuf_tensor("sq7", [128, 1024], BF16))
            y_ps = [E7(nc.psum_tensor(f"y_ps{i}", [128, 1024], F32))
                    for i in range(2)]

            tk_z = S.add("scalar", lambda s: SE(s.memzero(ssqa[:])),
                         deps=[p6_gate], incs=[("se", 1)])
            tk_ob0 = S.add("vector", lambda v: (
                v.memset(outb[0][:], 0.0),
                VE(v.memset(outb[1][:], 0.0)))[-1],
                deps=[p6_gate], incs=[("ve", 1)])

            tk_out = {}
            tk_y0 = {}
            tk_pe7 = {}
            for half in range(2):
                def ldwd(sy, half=half):
                    for it in range(44):
                        DS(sy.dma_start(
                            out=wdt[:, it * 1024:(it + 1) * 1024],
                            in_=wdn_d[half * 1024:(half + 1) * 1024,
                                      it * 128:(it + 1) * 128], transpose=True))
                dep_wd = [p6_gate] if half == 0 else [tk_pe7[(0, NTO - 1)]]
                dn_deps = [tk_dq_store["dn"][dbi][3] for dbi in range(16)]
                tk_ldwd = S.add("sync", ldwd, deps=dep_wd + dn_deps,
                                incs=[("dS", 704)])

                order = list(range(NTO))
                for pos, tt in enumerate(order):
                    tb = pos % 2
                    prev2 = order[pos - 2] if pos >= 2 else None

                    def ldh(sy, tt=tt, tb=tb):
                        for ggg in range(11):
                            src = bass.AP(h_d.tensor,
                                          ggg * 4 * 128 * TOWN + tt * 128,
                                          [[TOWN, 128], [128 * TOWN, 4], [1, 128]])
                            DS(sy.dma_start(
                                out=h7[tb][:, ggg * 512:(ggg + 1) * 512], in_=src))
                    if half == 0:
                        dep_h7 = tk_pe7.get((0, prev2)) if prev2 is not None else None
                    else:
                        dep_h7 = (tk_pe7.get((1, prev2)) if prev2 is not None
                                  else tk_pe7.get((0, NTO - 2 + pos)))
                    tk_ldh = S.add("sync", ldh, deps=[p6_gate, dep_h7],
                                   incs=[("dS", 176)])

                    def dmm(t, tb=tb):
                        for it in range(44):
                            for ch in range(2):
                                ins = t.matmul(
                                    y_ps[tb][:, ch * 512:(ch + 1) * 512],
                                    h7[tb][:, it * 128:(it + 1) * 128],
                                    wdt[:, it * 1024 + ch * 512:
                                        it * 1024 + (ch + 1) * 512],
                                    start=(it == 0), stop=(it == 43))
                        PE(ins)
                    if half == 0:
                        dep_bank = tk_y0.get(prev2) if prev2 is not None else None
                    else:
                        dep_bank = (tk_out.get(prev2) if prev2 is not None
                                    else tk_y0.get(NTO - 2 + pos))
                    tk_pe7[(half, tt)] = S.add("tensor", dmm,
                                               deps=[tk_ldh, tk_ldwd, dep_bank],
                                               incs=[("pe", 1)])
                    def sqacc(s, tb=tb):
                        s.memzero(ssqh[tb][:])
                        SE(s.activation(out=sq7[:], in_=y_ps[tb][:],
                                        func=AF.Square, accum_out=ssqh[tb][:]))
                    tk_ssq = S.add("scalar", sqacc,
                                   deps=[tk_pe7[(half, tt)], tk_z],
                                   incs=[("se", 1)])

                    if half == 0:
                        tk_ssqc = S.add("scalar", lambda s, tb=tb, tt=tt: SE(
                            s.copy(ssqa[:, tt:tt + 1], ssqh[tb][:])),
                            deps=[tk_ssq], incs=[("se", 1)])
                        tk_ev7 = S.add("vector", lambda v, tb=tb: VE(
                            v.tensor_copy(y0st[tb][:], y_ps[tb][:])),
                            deps=[tk_pe7[(half, tt)], tk_y0.get(tt - 2), tk_ssq],
                            incs=[("ve", 1)])
                        tk_y0[tt] = S.add("scalar", lambda s2, tt=tt, tb=tb: DA(
                            s2.dma_start(out=y0_d[tt * 128:(tt + 1) * 128, :],
                                         in_=y0st[tb][:])),
                            deps=[tk_ev7], incs=[("dA", 16)])
                    else:
                        tk_tot = S.add("vector", lambda v, tb=tb, tt=tt: VE(
                            v.tensor_tensor(out=ssqt[tb][:],
                                            in0=ssqa[:, tt:tt + 1],
                                            in1=ssqh[tb][:], op=OP.add)),
                            deps=[tk_ssq], incs=[("ve", 1)])
                        tk_l7 = S.add("scalar", lambda s, tb=tb: SE(
                            s.activation(out=lnv7[tb][:], in_=ssqt[tb][:],
                                         func=AF.Ln, bias=ceps2[:],
                                         scale=1.0 / D)),
                            deps=[tk_tot], incs=[("se", 1)])
                        tk_rs7 = S.add("scalar", lambda s, tb=tb: SE(
                            s.activation(out=rstd7[tb][:], in_=lnv7[tb][:],
                                         func=AF.Exp, scale=-0.5)),
                            deps=[tk_l7], incs=[("se", 1)])
                        if dev >= 2:
                            tk_y1e = S.add("vector", lambda v, tb=tb: VE(
                                v.tensor_copy(y0st[tb][:], y_ps[tb][:])),
                                deps=[tk_pe7[(half, tt)], tk_rs7],
                                incs=[("ve", 1)])
                            S.add("scalar", lambda s2, tt=tt, tb=tb: DA(
                                s2.dma_start(out=y1_dump[tt * 128:(tt + 1) * 128, :],
                                             in_=y0st[tb][:])),
                                deps=[tk_y1e], incs=[("dA", 16)])
                        dep_prev_out = (tk_out.get(prev2)
                                        if prev2 is not None else None)
                        pb = tb

                        def ld3(g, tt=tt, tb=tb, pb=pb):
                            DL(g.dma_start(
                                out=att7[tb][:],
                                in_=cc_out[tt * 128:(tt + 1) * 128, :]), pb)
                            DL(g.dma_start(
                                out=xf7[tb][:],
                                in_=xown[tt * 128:(tt + 1) * 128, :]), pb)
                            DL(g.dma_start(
                                out=y0l[tb][:],
                                in_=y0_d[tt * 128:(tt + 1) * 128, :]), pb)
                        tk_ld3 = S.add("gpsimd", ld3,
                                       deps=[tk_cc, tk_y0[tt], dep_prev_out],
                                       incs=[(f"dL{pb}", 48)])
                        tk_lda = tk_ldxo = tk_ldy0 = tk_ld3

                        deps_f = [tk_rs7, tk_lda, tk_ldxo, tk_ldy0,
                                  tk_pe7[(half, tt)], dep_prev_out]
                        tk_f1 = S.add("vector", lambda v, tb=tb: VE(
                            v.scalar_tensor_tensor(
                                out=outb[tb][:, 1024:2048], in0=y_ps[tb][:],
                                scalar=rstd7[tb][:], in1=att7[tb][:, 1024:2048],
                                op0=OP.mult, op1=OP.add)),
                            deps=deps_f, incs=[("ve", 1)])
                        tk_f2 = S.add("vector", lambda v, tb=tb: VE(
                            v.tensor_tensor(out=outb[tb][:, 1024:2048],
                                            in0=outb[tb][:, 1024:2048],
                                            in1=xf7[tb][:, 1024:2048],
                                            op=OP.add)),
                            deps=[tk_f1], incs=[("ve", 1)])
                        tk_f3 = S.add("vector", lambda v, tb=tb: VE(
                            v.scalar_tensor_tensor(
                                out=outb[tb][:, 0:1024], in0=y0l[tb][:],
                                scalar=rstd7[tb][:], in1=att7[tb][:, 0:1024],
                                op0=OP.mult, op1=OP.add)),
                            deps=deps_f, incs=[("ve", 1)])
                        tk_fin = S.add("vector", lambda v, tb=tb: VE(
                            v.tensor_tensor(out=outb[tb][:, 0:1024],
                                            in0=outb[tb][:, 0:1024],
                                            in1=xf7[tb][:, 0:1024], op=OP.add)),
                            deps=[tk_f2, tk_f3], incs=[("ve", 1)])
                        if dev >= 2:
                            def dumps7(s2, tt=tt, tb=tb):
                                DA(s2.dma_start(
                                    out=att7_dump[tt * 128:(tt + 1) * 128, :],
                                    in_=att7[tb][:]))
                                DA(s2.dma_start(
                                    out=y0l_dump[tt * 128:(tt + 1) * 128, :],
                                    in_=y0l[tb][:]))
                                DA(s2.dma_start(
                                    out=rstd_dump2[tt * 128:(tt + 1) * 128, :],
                                    in_=rstd7[tb][:]))
                            S.add("scalar", dumps7,
                                  deps=[tk_fin], incs=[("dA", 48)])
                        tk_out[tt] = S.add("scalar", lambda s2, tt=tt, tb=tb: DA(
                            s2.dma_start(out=out[tt * 128:(tt + 1) * 128, :],
                                         in_=outb[tb][:])),
                            deps=[tk_fin], incs=[("dA", 16)])

            S.add("sync", lambda sy: sy.nop(),
                  deps=[{"dA": S.c["dA"], "dG": S.c["dG"], "dS": S.c["dS"]}])
            S.flush(block, sems, "P7")
    return nc


_NC_CACHE = {}


def _get_nc(dev=0):
    if dev not in _NC_CACHE:
        _NC_CACHE[dev] = build(dev)
    return _NC_CACHE[dev]


def make_in_maps(L0_attn_x, L0_attn_Wq, L0_attn_Wk, L0_attn_Wv, L0_attn_Wo,
                 L0_attn_lambda_init, L0_ffn_gate_up_packed, L0_ffn_gate_up_scales,
                 L0_ffn_down_packed, L0_ffn_down_scales):
    pi_h = _pi4(D)
    pi_i = _pi4(II)
    asc = np.ascontiguousarray

    xq = np.asarray(L0_attn_x)
    wq_p = asc(np.asarray(L0_attn_Wq)[pi_h])
    wk_p = asc(np.asarray(L0_attn_Wk)[pi_h])
    wv_p = asc(np.asarray(L0_attn_Wv)[pi_h])
    wo_a = np.asarray(L0_attn_Wo)
    lam_a = np.asarray(L0_attn_lambda_init)
    gu_rows = np.concatenate([pi_i, II + pi_i])
    gu16 = asc(np.asarray(L0_ffn_gate_up_packed).view(np.uint16)[gu_rows])
    gus_t = asc(np.asarray(L0_ffn_gate_up_scales).T[gu_rows])
    dn16 = asc(np.asarray(L0_ffn_down_packed).view(np.uint16))
    dns_t = asc(np.asarray(L0_ffn_down_scales).T)

    in_maps = []
    for c in range(8):
        b, hg = divmod(c, 2)
        R = hg * TOWN
        in_maps.append({
            "x": asc(xq[b]),
            "xown": asc(xq[b][R:R + TOWN]),
            "wq": asc(wq_p[:, hg * DK:(hg + 1) * DK]),
            "wk": asc(wk_p[:, hg * DK:(hg + 1) * DK]),
            "wv": asc(wv_p[:, hg * DK:(hg + 1) * DK]),
            "wo": asc(wo_a[hg * DK:(hg + 1) * DK, :]),
            "lam": asc(lam_a[hg * NH:(hg + 1) * NH].reshape(1, NH)),
            "gu16": gu16,
            "dn16": dn16,
            "gus": gus_t,
            "dns": dns_t,
            "ofs": np.array([[(TOWN - R) * D]], dtype=np.int32),
        })
    return in_maps


def kernel(L0_attn_x, L0_attn_Wq, L0_attn_Wk, L0_attn_Wv, L0_attn_Wo,
           L0_attn_lambda_init, L0_ffn_gate_up_packed, L0_ffn_gate_up_scales,
           L0_ffn_down_packed, L0_ffn_down_scales):
    in_maps = make_in_maps(L0_attn_x, L0_attn_Wq, L0_attn_Wk, L0_attn_Wv,
                           L0_attn_Wo, L0_attn_lambda_init,
                           L0_ffn_gate_up_packed, L0_ffn_gate_up_scales,
                           L0_ffn_down_packed, L0_ffn_down_scales)
    nc = _get_nc(0)
    res = run_bass_kernel_spmd(nc, in_maps, core_ids=list(range(8)))
    outp = np.empty((4, T, D), dtype=np.float32)
    for c in range(8):
        b, hg = divmod(c, 2)
        R = hg * TOWN
        outp[b, R:R + TOWN, :] = res.results[c]["out"]
    return outp


# version 05
# speedup vs baseline: 1.1330x; 1.1330x over previous
import contextlib
import numpy as np
import ml_dtypes
import concourse.bass as bass
import concourse.mybir as mybir
from concourse.bass_utils import run_bass_kernel_spmd
from concourse.alu_op_type import AluOpType as OP

F32 = mybir.dt.float32
BF16 = mybir.dt.bfloat16
F16 = mybir.dt.float16
U16 = mybir.dt.uint16
U32 = mybir.dt.uint32
I32 = mybir.dt.int32
AF = mybir.ActivationFunctionType
AX = mybir.AxisListType.X

T = 4096
D = 2048
DK = 1024
NH = 8
II = 5632
I2 = 2 * II
TOWN = 2048
NT = T // 128
NTO = TOWN // 128

MAG_MASKS = [(0x0007, 6, True), (0x0070, 2, True), (0x0700, 2, False), (0x7000, 6, False)]
SGN_MASKS = [(0x0008, 12), (0x0080, 8), (0x0800, 4), (0x8000, 0)]

ENGS = ["sync", "vector", "scalar", "tensor", "gpsimd"]
OWNER = {"pe": "tensor", "ve": "vector", "se": "scalar", "ge": "gpsimd"}


def _pi4(n):
    q = n // 4
    j = np.arange(n)
    return 4 * (j % q) + j // q


class Sched:

    def __init__(self):
        self.ops = {e: [] for e in ENGS}
        self.waited = {}
        self.c = {s: 0 for s in ["pe", "ve", "se", "ge", "dS", "dA", "dG",
                                 "dL0", "dL1", "cc"]}

    def add(self, eng, fn, deps=(), incs=()):
        waits = {}
        for d in deps:
            if d is None:
                continue
            items = d.items() if isinstance(d, dict) else [d]
            for s, v in items:
                waits[s] = max(waits.get(s, 0), v)
        tok = {}
        for s, n in incs:
            self.c[s] += n
            tok[s] = self.c[s]
        self.ops[eng].append((waits, fn, dict(incs)))
        return tok

    def snap(self):
        return dict(self.c)

    def flush(self, block, sems, name=None):
        import contextlib as _cl
        scope = (block.bass.named_scope(name) if name is not None
                 else _cl.nullcontext())
        with scope:
            self._flush_inner(block, sems)

    def _flush_inner(self, block, sems):
        for ename in ENGS:
            oplist = self.ops[ename]
            if not oplist:
                continue

            def body(eng, oplist=oplist, ename=ename):
                waited = self.waited.setdefault(ename, {})
                for waits, fn, dec in oplist:
                    for s in sorted(waits):
                        v = waits[s]
                        if waited.get(s, -1) >= v:
                            continue
                        eng.wait_ge(sems[s], v)
                        waited[s] = v
                    before = dict(ATTACH)
                    fn(eng)
                    for sname, n in dec.items():
                        got = ATTACH[sname] - before[sname]
                        if got != n:
                            import traceback
                            print(f"*** INC MISMATCH {ename} {sname}: declared {n} attached {got} fn={fn}")
                    for sname in ATTACH:
                        if sname not in dec and ATTACH[sname] != before[sname]:
                            print(f"*** UNDECLARED ATTACH {ename} {sname} fn={fn}")
            getattr(block, ename)(body)
        self.ops = {e: [] for e in ENGS}


ATTACH = {s: 0 for s in ["pe", "ve", "se", "ge", "dS", "dA", "dG",
                         "dL0", "dL1", "cc"]}


def build(dev=0):
    for k in ATTACH:
        ATTACH[k] = 0
    nc = bass.Bass("TRN2", target_bir_lowering=False)

    x = nc.dram_tensor("x", [T, D], F32, kind="ExternalInput").ap()
    xown = nc.dram_tensor("xown", [TOWN, D], F32, kind="ExternalInput").ap()
    wq = nc.dram_tensor("wq", [D, DK], F32, kind="ExternalInput").ap()
    wk = nc.dram_tensor("wk", [D, DK], F32, kind="ExternalInput").ap()
    wv = nc.dram_tensor("wv", [D, DK], F32, kind="ExternalInput").ap()
    wo = nc.dram_tensor("wo", [DK, D], F32, kind="ExternalInput").ap()
    lam = nc.dram_tensor("lam", [1, NH], F32, kind="ExternalInput").ap()
    gu16 = nc.dram_tensor("gu16", [I2, D // 4], U16, kind="ExternalInput").ap()
    dn16 = nc.dram_tensor("dn16", [D, II // 4], U16, kind="ExternalInput").ap()
    gus = nc.dram_tensor("gus", [I2, 16], F16, kind="ExternalInput").ap()
    dns = nc.dram_tensor("dns", [D, 44], F16, kind="ExternalInput").ap()
    ofs = nc.dram_tensor("ofs", [1, 1], I32, kind="ExternalInput").ap()
    out = nc.dram_tensor("out", [TOWN, D], F32, kind="ExternalOutput").ap()

    def scratch(name, shape, dt):
        kind = "ExternalOutput" if dev >= 2 else "Internal"
        return nc.dram_tensor(name, shape, dt, kind=kind).ap()

    xn_d = scratch("xn_d", [T, D], BF16)
    xn2_d = scratch("xn2_d", [T + TOWN, D], BF16)
    k_d = scratch("k_d", [DK, T], BF16)
    v_d = scratch("v_d", [T, DK], BF16)
    q_d = scratch("q_d", [DK, T], BF16)
    o_d = scratch("o_d", [DK, T], BF16)
    wgu_d = scratch("wgu_d", [I2, D], BF16)
    wdn_d = scratch("wdn_d", [D, II], BF16)
    h_d = scratch("h_d", [II, TOWN], BF16)
    y0_d = scratch("y0_d", [TOWN, 1024], F32)
    cc_in = nc.dram_tensor("cc_in", [T, D], BF16, kind="Internal").ap()
    cc_out = nc.dram_tensor("cc_out", [TOWN, D], BF16, kind="Internal").ap()
    att_dump = (nc.dram_tensor("att_dump", [T, D], BF16, kind="ExternalOutput").ap()
                if dev >= 2 else None)
    cc_dump = (nc.dram_tensor("cc_dump", [TOWN, D], BF16, kind="ExternalOutput").ap()
               if dev >= 2 else None)
    rstd_dump = (nc.dram_tensor("rstd_dump", [TOWN, 1], F32, kind="ExternalOutput").ap()
                 if dev >= 2 else None)
    y1_dump = (nc.dram_tensor("y1_dump", [TOWN, 1024], F32, kind="ExternalOutput").ap()
               if dev >= 2 else None)
    att7_dump = (nc.dram_tensor("att7_dump", [TOWN, D], BF16, kind="ExternalOutput").ap()
                 if dev >= 2 else None)
    y0l_dump = (nc.dram_tensor("y0l_dump", [TOWN, 1024], F32, kind="ExternalOutput").ap()
                if dev >= 2 else None)
    rstd_dump2 = (nc.dram_tensor("rstd_dump2", [TOWN, 1], F32, kind="ExternalOutput").ap()
                  if dev >= 2 else None)

    S = Sched()
    es = contextlib.ExitStack()
    with es:
        E = es.enter_context
        sems = {n: E(nc.semaphore(n)) for n in
                ["pe", "ve", "se", "ge", "dS", "dA", "dG", "dL0", "dL1", "cc"]}
        block = E(nc.Block())

        def DS(instr):
            ATTACH["dS"] += 16
            return instr.then_inc(sems["dS"], 16)

        def DA(instr):
            ATTACH["dA"] += 16
            return instr.then_inc(sems["dA"], 16)

        def DG(instr):
            ATTACH["dG"] += 16
            return instr.then_inc(sems["dG"], 16)

        def DL(instr, pb):
            ATTACH[f"dL{pb}"] += 16
            return instr.then_inc(sems[f"dL{pb}"], 16)

        def PE(instr):
            ATTACH["pe"] += 1
            return instr.then_inc(sems["pe"], 1)

        def VE(instr):
            ATTACH["ve"] += 1
            return instr.then_inc(sems["ve"], 1)

        def SE(instr):
            ATTACH["se"] += 1
            return instr.then_inc(sems["se"], 1)

        def GE(instr):
            ATTACH["ge"] += 1
            return instr.then_inc(sems["ge"], 1)

        c25 = E(nc.sbuf_tensor("c25", [128, 1], F32))
        cn8 = E(nc.sbuf_tensor("cn8", [128, 1], F32))
        ceps1 = E(nc.sbuf_tensor("ceps1", [128, 1], F32))
        ceps2 = E(nc.sbuf_tensor("ceps2", [128, 1], F32))
        ones_rf = E(nc.sbuf_tensor("ones_rf", [1, 128], F32))
        ones_rb = E(nc.sbuf_tensor("ones_rb", [1, 128], BF16))
        ones_kb = E(nc.sbuf_tensor("ones_kb", [128, 1], BF16))
        dmask = E(nc.sbuf_tensor("dmask", [128, 128], F32))
        lam_sb = E(nc.sbuf_tensor("lam_sb", [1, NH], F32))
        lam_sg = E(nc.sbuf_tensor("lam_sg", [1, NH], F32))
        nlam = E(nc.sbuf_tensor("nlam", [128, NH], F32))
        ofs_sb = E(nc.sbuf_tensor("ofs_sb", [1, 1], I32))
        es0 = contextlib.ExitStack()
        lam_ps = es0.enter_context(nc.psum_tensor("lam_ps", [128, NH], F32))

        def p0_consts(v):
            v.memset(c25[:], 0.25)
            v.memset(cn8[:], -8.0)
            v.memset(ceps1[:], 1e-5)
            v.memset(ceps2[:], 1e-6)
            v.memset(ones_rf[:], 1.0)
            v.memset(ones_rb[:], 1.0)
            VE(v.memset(ones_kb[:], 1.0))
        tk_consts = S.add("vector", p0_consts, incs=[("ve", 1)])

        def p0_mask(g):
            g.memset(dmask[:], 0.0)
            GE(g.affine_select(out=dmask[:], in_=dmask[:], pattern=[[1, 128]],
                               compare_op=OP.is_ge, fill=-1e30, base=0,
                               channel_multiplier=-1))
        tk_mask = S.add("gpsimd", p0_mask, incs=[("ge", 1)])
        tk_ofs = S.add("gpsimd", lambda g: DG(g.dma_start(out=ofs_sb[:], in_=ofs)),
                       incs=[("dG", 16)])
        tk_lam = S.add("gpsimd", lambda g: DG(g.dma_start(out=lam_sb[:], in_=lam)),
                       incs=[("dG", 16)])
        tk_sig = S.add("scalar", lambda s: SE(s.activation(
            out=lam_sg[:], in_=lam_sb[:], func=AF.Sigmoid)),
            deps=[tk_lam], incs=[("se", 1)])
        tk_lmm = S.add("tensor", lambda t: PE(t.matmul(
            lam_ps[:], ones_rf[:], lam_sg[:], start=True, stop=True)),
            deps=[tk_sig, tk_consts], incs=[("pe", 1)])
        tk_nlam = S.add("vector", lambda v: VE(v.tensor_scalar(
            out=nlam[:], in0=lam_ps[:], scalar1=-1.0, scalar2=None, op0=OP.mult)),
            deps=[tk_lmm], incs=[("ve", 1)])
        es0.close()

        dq_in = [E(nc.sbuf_tensor(f"dq_in{i}", [128, 512], U16))
                 for i in range(2)]
        dq_din = E(nc.sbuf_tensor("dq_din", [128, 1408], U16))
        dq_nib = E(nc.sbuf_tensor("dq_nib", [128, 2048], U16))
        dq_sf = E(nc.sbuf_tensor("dq_sf", [128, 2048], U16))
        dq_corr = E(nc.sbuf_tensor("dq_corr", [128, 2048], BF16))
        dq_mag = E(nc.sbuf_tensor("dq_mag", [128, 2048], BF16))
        dq_w = [E(nc.sbuf_tensor(f"dq_w{i}", [128, 2048], BF16))
                for i in range(2)]
        dq_scf = E(nc.sbuf_tensor("dq_scf", [128, 44], F16))
        dq_scb = E(nc.sbuf_tensor("dq_scb", [128, 44], BF16))
        dq_s32 = E(nc.sbuf_tensor("dq_s32", [128, 44], U32))

        tk_dq_store = {"gu": [None] * 88, "dn": [[None] * 4 for _ in range(16)]}
        dqs = {"wbuf": 0, "inbuf": 0, "last_w_store": [None, None],
               "last_v": None, "last_din_user": None, "last_sc_user": None}

        def emit_gu_chunk(ci):
            wb = dqs["wbuf"]; dqs["wbuf"] ^= 1
            ib = dqs["inbuf"]; dqs["inbuf"] ^= 1
            tk_ldp = S.add("gpsimd", lambda g, ci=ci, ib=ib: DG(g.dma_start(
                out=dq_in[ib][:], in_=gu16[ci * 128:(ci + 1) * 128, :])),
                deps=[dqs["last_v"]], incs=[("dG", 16)])
            tk_lds = S.add("gpsimd", lambda g, ci=ci: DG(g.dma_start(
                out=dq_scf[:, 0:16], in_=gus[ci * 128:(ci + 1) * 128, :])),
                deps=[dqs["last_sc_user"]], incs=[("dG", 16)])
            tk_scb = S.add("vector", lambda v: VE(v.tensor_scalar(
                out=dq_scb[:, 0:16], in0=dq_scf[:, 0:16], scalar1=4.0,
                scalar2=None, op0=OP.mult)), deps=[tk_lds], incs=[("ve", 1)])

            def s32f(g):
                lo = dq_s32[:, 0:16].bitcast(U16).rearrange(
                    "p (f two) -> p f two", two=2)[:, :, 0]
                hi = dq_s32[:, 0:16].bitcast(U16).rearrange(
                    "p (f two) -> p f two", two=2)[:, :, 1]
                g.tensor_copy(lo, dq_scb[:, 0:16].bitcast(U16))
                GE(g.tensor_copy(hi, dq_scb[:, 0:16].bitcast(U16)))
            tk_s32 = S.add("gpsimd", s32f, deps=[tk_scb], incs=[("ge", 1)])

            def vex(v, ib=ib):
                u32i = dq_in[ib][:].bitcast(U32)
                for k in range(4):
                    m, sh, left = MAG_MASKS[k]
                    v.tensor_scalar(
                        out=dq_nib[:, k * 512:(k + 1) * 512].bitcast(U32),
                        in0=u32i, scalar1=m | (m << 16), scalar2=sh,
                        op0=OP.bitwise_and,
                        op1=OP.logical_shift_left if left else OP.logical_shift_right)
                for k in range(4):
                    m, sh = SGN_MASKS[k]
                    v.tensor_scalar(
                        out=dq_sf[:, k * 512:(k + 1) * 512].bitcast(U32),
                        in0=u32i, scalar1=m | (m << 16), scalar2=sh,
                        op0=OP.bitwise_and, op1=OP.logical_shift_left)
                VE(v.tensor_scalar(out=dq_nib[:].bitcast(U32),
                                   in0=dq_nib[:].bitcast(U32),
                                   scalar1=0x3E003E00, scalar2=None,
                                   op0=OP.bitwise_or))
            tk_ext = S.add("vector", vex, deps=[tk_ldp], incs=[("ve", 1)])
            tk_cor = tk_ext

            def vfin(v, wb=wb):
                v.tensor_scalar(out=dq_corr[:], in0=dq_nib[:].bitcast(BF16),
                                scalar1=2.0, scalar2=-0.25,
                                op0=OP.mult, op1=OP.add)
                v.tensor_tensor(out=dq_mag[:], in0=dq_corr[:],
                                in1=dq_nib[:].bitcast(BF16), op=OP.min)
                sc4 = bass.AP(dq_s32[:].tensor, dq_s32[:].offset,
                              [dq_s32[:].ap[0], [0, 4], [1, 16], [0, 16]])
                v.tensor_tensor(
                    out=dq_sf[:].bitcast(U32).rearrange("p (a g r) -> p a g r",
                                                        a=4, r=16),
                    in0=dq_sf[:].bitcast(U32).rearrange("p (a g r) -> p a g r",
                                                        a=4, r=16),
                    in1=sc4, op=OP.bitwise_or)
                VE(v.tensor_tensor(out=dq_w[wb][:], in0=dq_mag[:],
                                   in1=dq_sf[:].bitcast(BF16), op=OP.mult))
            tk_fin = S.add("vector", vfin,
                           deps=[tk_cor, tk_s32, dqs["last_w_store"][wb]],
                           incs=[("ve", 1)])
            dqs["last_v"] = tk_fin
            dqs["last_sc_user"] = tk_fin
            tk_stw = S.add("gpsimd", lambda g, ci=ci, wb=wb: DG(g.dma_start(
                out=wgu_d[ci * 128:(ci + 1) * 128, :], in_=dq_w[wb][:])),
                deps=[tk_fin], incs=[("dG", 16)])
            dqs["last_w_store"][wb] = tk_stw
            tk_dq_store["gu"][ci] = tk_stw

        def emit_dn_block(db):
            tk_ldp = S.add("gpsimd", lambda g, db=db: DG(g.dma_start(
                out=dq_din[:], in_=dn16[db * 128:(db + 1) * 128, :])),
                deps=[dqs["last_din_user"]], incs=[("dG", 16)])
            tk_lds = S.add("gpsimd", lambda g, db=db: DG(g.dma_start(
                out=dq_scf[:], in_=dns[db * 128:(db + 1) * 128, :])),
                deps=[dqs["last_sc_user"]], incs=[("dG", 16)])
            tk_scb = S.add("vector", lambda v: VE(v.tensor_scalar(
                out=dq_scb[:], in0=dq_scf[:], scalar1=4.0, scalar2=None,
                op0=OP.mult)), deps=[tk_lds], incs=[("ve", 1)])

            def s32f(g):
                lo = dq_s32[:].bitcast(U16).rearrange(
                    "p (f two) -> p f two", two=2)[:, :, 0]
                hi = dq_s32[:].bitcast(U16).rearrange(
                    "p (f two) -> p f two", two=2)[:, :, 1]
                g.tensor_copy(lo, dq_scb[:].bitcast(U16))
                GE(g.tensor_copy(hi, dq_scb[:].bitcast(U16)))
            tk_s32 = S.add("gpsimd", s32f, deps=[tk_scb], incs=[("ge", 1)])

            last_fin = None
            for kp4 in range(4):
                wb = dqs["wbuf"]; dqs["wbuf"] ^= 1
                m, sh, left = MAG_MASKS[kp4]
                ms, shs = SGN_MASKS[kp4]

                def vex(v, m=m, sh=sh, left=left, ms=ms, shs=shs):
                    u32i = dq_din[:].bitcast(U32)
                    v.tensor_scalar(out=dq_nib[:, 0:1408].bitcast(U32), in0=u32i,
                                    scalar1=m | (m << 16), scalar2=sh,
                                    op0=OP.bitwise_and,
                                    op1=OP.logical_shift_left if left
                                    else OP.logical_shift_right)
                    v.tensor_scalar(out=dq_sf[:, 0:1408].bitcast(U32), in0=u32i,
                                    scalar1=ms | (ms << 16), scalar2=shs,
                                    op0=OP.bitwise_and, op1=OP.logical_shift_left)
                    VE(v.tensor_scalar(out=dq_nib[:, 0:1408].bitcast(U32),
                                       in0=dq_nib[:, 0:1408].bitcast(U32),
                                       scalar1=0x3E003E00, scalar2=None,
                                       op0=OP.bitwise_or))
                tk_ext = S.add("vector", vex, deps=[tk_ldp], incs=[("ve", 1)])
                tk_cor = tk_ext

                def vfin(v, wb=wb):
                    v.tensor_scalar(out=dq_corr[:, 0:1408],
                                    in0=dq_nib[:, 0:1408].bitcast(BF16),
                                    scalar1=2.0, scalar2=-0.25,
                                    op0=OP.mult, op1=OP.add)
                    v.tensor_tensor(out=dq_mag[:, 0:1408],
                                    in0=dq_corr[:, 0:1408],
                                    in1=dq_nib[:, 0:1408].bitcast(BF16),
                                    op=OP.min)
                    sc4 = bass.AP(dq_s32[:].tensor, dq_s32[:].offset,
                                  [dq_s32[:].ap[0], [1, 44], [0, 16]])
                    v.tensor_tensor(
                        out=dq_sf[:, 0:1408].bitcast(U32).rearrange(
                            "p (g r) -> p g r", g=44),
                        in0=dq_sf[:, 0:1408].bitcast(U32).rearrange(
                            "p (g r) -> p g r", g=44),
                        in1=sc4, op=OP.bitwise_or)
                    VE(v.tensor_tensor(out=dq_w[wb][:, 0:1408],
                                       in0=dq_mag[:, 0:1408],
                                       in1=dq_sf[:, 0:1408].bitcast(BF16),
                                       op=OP.mult))
                tk_fin = S.add("vector", vfin,
                               deps=[tk_cor, tk_s32, dqs["last_w_store"][wb]],
                               incs=[("ve", 1)])
                dqs["last_v"] = tk_fin
                dqs["last_sc_user"] = tk_fin
                last_fin = tk_fin
                tk_stw = S.add("gpsimd", lambda g, db=db, kp4=kp4, wb=wb: DG(
                    g.dma_start(out=wdn_d[db * 128:(db + 1) * 128,
                                          kp4 * 1408:(kp4 + 1) * 1408],
                                in_=dq_w[wb][:, 0:1408])),
                    deps=[tk_fin], incs=[("dG", 16)])
                dqs["last_w_store"][wb] = tk_stw
                tk_dq_store["dn"][db][kp4] = tk_stw
            dqs["last_din_user"] = last_fin

        dq_units = [lambda ci=ci: emit_gu_chunk(ci) for ci in range(88)]
        dq_units += [lambda db=db: emit_dn_block(db) for db in range(16)]
        dq_iter = iter(dq_units)
        n_dq_per_slot = (len(dq_units) + 63) // 64

        with contextlib.ExitStack() as es1:
            E1 = es1.enter_context
            xf = [E1(nc.sbuf_tensor(f"xf{i}", [128, D], F32)) for i in range(2)]
            xnb = [E1(nc.sbuf_tensor(f"xnb{i}", [128, D], BF16)) for i in range(2)]
            negm = [E1(nc.sbuf_tensor(f"negm{i}", [128, 1], F32)) for i in range(2)]
            rstd1 = [E1(nc.sbuf_tensor(f"rstd1{i}", [128, 1], F32)) for i in range(2)]
            ssq1 = [E1(nc.sbuf_tensor(f"ssq1{i}", [128, 1], F32)) for i in range(2)]
            sxs = [E1(nc.sbuf_tensor(f"sxs{i}", [128, 1], F32)) for i in range(2)]
            vart = [E1(nc.sbuf_tensor(f"vart{i}", [128, 1], F32)) for i in range(2)]
            sqd = E1(nc.sbuf_tensor("sqd", [128, D], BF16))

            regs = {}

            def p1_reg(g):
                regs["rt"] = g.alloc_register("rt")
                g.reg_load(regs["rt"], ofs_sb[:1, :1])
            S.add("gpsimd", p1_reg, deps=[tk_ofs])

            tk_xn = [None] * NT
            tk_st = [None] * NT
            for t in range(NT):
                b = t % 2
                tk_ld = S.add("sync", lambda sy, t=t, b=b: DS(sy.dma_start(
                    out=xf[b][:], in_=x[t * 128:(t + 1) * 128, :])),
                    deps=[tk_xn[t - 2] if t >= 2 else None], incs=[("dS", 16)])

                tk_red = S.add("vector", lambda v, b=b: VE(
                    v.reduce_sum(sxs[b][:], xf[b][:], axis=AX)),
                    deps=[tk_ld], incs=[("ve", 1)])
                tk_sum = S.add("vector", lambda v, b=b: VE(
                    v.tensor_scalar(out=negm[b][:], in0=sxs[b][:],
                                    scalar1=-1.0 / D, scalar2=None, op0=OP.mult)),
                    deps=[tk_red], incs=[("ve", 1)])

                def p1_sq(s, b=b):
                    s.memzero(ssq1[b][:])
                    SE(s.activation(out=sqd[:], in_=xf[b][:], func=AF.Square,
                                    accum_out=ssq1[b][:]))
                tk_sq = S.add("scalar", p1_sq, deps=[tk_ld], incs=[("se", 1)])

                tk_va = S.add("vector", lambda v, b=b: VE(
                    v.tensor_scalar(out=vart[b][:], in0=ssq1[b][:],
                                    scalar1=1.0 / D, scalar2=None, op0=OP.mult)),
                    deps=[tk_sq], incs=[("ve", 1)])
                tk_vb = S.add("vector", lambda v, b=b: VE(
                    v.tensor_tensor(out=sxs[b][:], in0=negm[b][:],
                                    in1=negm[b][:], op=OP.mult)),
                    deps=[tk_sum], incs=[("ve", 1)])
                tk_var = S.add("vector", lambda v, b=b: VE(
                    v.scalar_tensor_tensor(out=vart[b][:], in0=sxs[b][:],
                                           scalar=-1.0, in1=vart[b][:],
                                           op0=OP.mult, op1=OP.add)),
                    deps=[tk_va, tk_vb], incs=[("ve", 1)])

                tk_ln1 = S.add("scalar", lambda s, b=b: SE(
                    s.activation(out=ssq1[b][:], in_=vart[b][:], func=AF.Ln,
                                 bias=ceps1[:], scale=1.0)),
                    deps=[tk_var], incs=[("se", 1)])
                tk_rstd = S.add("scalar", lambda s, b=b: SE(
                    s.activation(out=rstd1[b][:], in_=ssq1[b][:], func=AF.Exp,
                                 scale=-0.5)),
                    deps=[tk_ln1], incs=[("se", 1)])

                def p1_xn(v, b=b):
                    xo = bass.AP(xnb[b][:].tensor, xnb[b][:].offset,
                                 [xnb[b][:].ap[0], [1, 512], [512, 4]])
                    xi = bass.AP(xf[b][:].tensor, xf[b][:].offset,
                                 [xf[b][:].ap[0], [4, 512], [1, 4]])
                    VE(v.tensor_scalar(out=xo, in0=xi, scalar1=negm[b][:],
                                       scalar2=rstd1[b][:], op0=OP.add, op1=OP.mult))
                tk_xn[t] = S.add("vector", p1_xn,
                                 deps=[tk_rstd, tk_st[t - 2] if t >= 2 else None],
                                 incs=[("ve", 1)])

                def p1_st(g, t=t, b=b):
                    DG(g.dma_start(out=xn_d[t * 128:(t + 1) * 128, :], in_=xnb[b][:]))
                tk_st[t] = S.add("gpsimd", p1_st, deps=[tk_xn[t]],
                                 incs=[("dG", 16)])
                for _ in range(2):
                    u = next(dq_iter, None)
                    if u is not None:
                        u()

            def p1_xn2(g):
                dyn = bass.AP(xn2_d.tensor, regs["rt"], [[D, T], [1, D]])
                DG(g.dma_start(out=dyn, in_=xn_d))
            tk_p1_done = S.add("gpsimd", p1_xn2, deps=[tk_st[NT - 1]],
                               incs=[("dG", 16)])
            S.flush(block, sems, "P1")

        with contextlib.ExitStack() as es2:
            E2 = es2.enter_context
            wkb = E2(nc.sbuf_tensor("wkb", [128, 16 * DK], BF16))
            wvb = E2(nc.sbuf_tensor("wvb", [128, 16 * DK], BF16))
            wqb = E2(nc.sbuf_tensor("wqb", [128, 16 * DK], BF16))
            tk_w = {}
            for nm, src, dst in [("k", wk, wkb), ("v", wv, wvb), ("q", wq, wqb)]:
                tk_w[nm] = S.add("gpsimd", lambda g, src=src, dst=dst: DG(
                    g.dma_start(
                        out=dst[:].rearrange("p (a n) -> p a n", a=16),
                        in_=src.rearrange("(a p) n -> p a n", p=128))),
                    incs=[("dG", 16)])
            xnt = [E2(nc.sbuf_tensor(f"xnt{i}", [128, 16 * 512], BF16))
                   for i in range(2)]
            evb = {k: [E2(nc.sbuf_tensor(f"{k}ev{i}", [128, 512], BF16))
                       for i in range(2)] for k in "kqv"}
            pp = {k: [E2(nc.psum_tensor(f"{k}p{i}", [128, 512], F32))
                      for i in range(2)] for k in "kqv"}

            tk_chunk = [None] * 8
            tk_ev = {}
            tk_stc = {}
            for c in range(8):
                cb = c % 2

                def p2_ldx(sy, c=c, cb=cb):
                    for dt in range(16):
                        DS(sy.dma_start(
                            out=xnt[cb][:, dt * 512:(dt + 1) * 512],
                            in_=xn_d[c * 512:(c + 1) * 512, dt * 128:(dt + 1) * 128],
                            transpose=True))
                tk_ldx = S.add("sync", p2_ldx,
                               deps=[tk_p1_done, tk_chunk[c - 2] if c >= 2 else None],
                               incs=[("dS", 256)])

                for kind, wsb, dst in [("k", wkb, k_d), ("q", wqb, q_d),
                                       ("v", wvb, v_d)]:
                    for j in range(8):
                        jb = j % 2
                        key = (kind, c, j)
                        prev = (kind, c, j - 2) if j >= 2 else (kind, c - 1, 6 + j)
                        dep_bank = tk_ev.get(prev)

                        def mms(t, j=j, jb=jb, wsb=wsb, kind=kind, cb=cb):
                            ps = pp[kind][jb]
                            for dt in range(16):
                                if kind in ("k", "q"):
                                    lhs = wsb[:, dt * DK + j * 128:
                                              dt * DK + (j + 1) * 128]
                                    rhs = xnt[cb][:, dt * 512:(dt + 1) * 512]
                                else:
                                    ts4, dc = divmod(j, 2)
                                    lhs = xnt[cb][:, dt * 512 + ts4 * 128:
                                                  dt * 512 + (ts4 + 1) * 128]
                                    rhs = wsb[:, dt * DK + dc * 512:
                                              dt * DK + (dc + 1) * 512]
                                ins = t.matmul(ps[:], lhs, rhs,
                                               start=(dt == 0), stop=(dt == 15))
                            PE(ins)
                        tk_mm = S.add("tensor", mms,
                                      deps=[tk_ldx, dep_bank, tk_w[kind]],
                                      incs=[("pe", 1)])
                        dep_stb = tk_stc.get(prev)

                        def ev(se_, jb=jb, kind=kind):
                            SE(se_.copy(evb[kind][jb][:], pp[kind][jb][:]))
                        tk_ev[key] = S.add("scalar", ev, deps=[tk_mm, dep_stb],
                                           incs=[("se", 1)])

                        def st(s2, j=j, jb=jb, kind=kind, c=c, dst=dst):
                            if kind in ("k", "q"):
                                d = dst[j * 128:(j + 1) * 128, c * 512:(c + 1) * 512]
                            else:
                                ts4, dc = divmod(j, 2)
                                d = dst[c * 512 + ts4 * 128:
                                        c * 512 + (ts4 + 1) * 128,
                                        dc * 512:(dc + 1) * 512]
                            DA(s2.dma_start(out=d, in_=evb[kind][jb][:]))
                        tk_stc[key] = S.add("scalar", st, deps=[tk_ev[key]],
                                            incs=[("dA", 16)])
                        u = next(dq_iter, None)
                        if u is not None:
                            u()
                tk_chunk[c] = tk_ev[("v", c, 7)]
            for u in dq_iter:
                u()
            S.flush(block, sems, "P2")
            p2_gate = S.snap()

        with contextlib.ExitStack() as es4:
            E4 = es4.enter_context
            k_sb = E4(nc.sbuf_tensor("k_sb", [128, NH * T], BF16))
            v_sb = E4(nc.sbuf_tensor("v_sb", [128, NT * DK], BF16))
            qm_sb = [E4(nc.sbuf_tensor(f"qm_sb{i}", [128, NH * 512], BF16))
                     for i in range(2)]
            p12 = [E4(nc.sbuf_tensor(f"p12_{i}", [128, 1024], BF16))
                   for i in range(2)]
            oh_sb = E4(nc.sbuf_tensor("oh_sb", [128, NH * 512], BF16))
            rl_f = E4(nc.sbuf_tensor("rl_f", [1, 1024], F32))
            rl_b = E4(nc.sbuf_tensor("rl_b", [1, 1024], BF16))
            rlb1 = E4(nc.sbuf_tensor("rlb1", [128, 512], BF16))
            rlb2 = E4(nc.sbuf_tensor("rlb2", [128, 512], BF16))
            tcmb = [E4(nc.sbuf_tensor(f"tcmb{i}", [128, 512], BF16))
                    for i in range(2)]
            s_ps = [E4(nc.psum_tensor(f"s_ps{i}", [128, 1024], F32))
                    for i in range(2)]
            o_ps = [E4(nc.psum_tensor(f"o_ps{i}", [128, 512], F32))
                    for i in range(2)]
            l_ps = E4(nc.psum_tensor("l_ps", [128, 1024], F32))

            def p4_ldk(sy):
                for h in range(NH):
                    DS(sy.dma_start(out=k_sb[:, h * T:(h + 1) * T],
                                    in_=k_d[h * 128:(h + 1) * 128, :]))
            tk_ldk = S.add("sync", p4_ldk, deps=[p2_gate], incs=[("dS", 128)])

            def p4_ldv(sy):
                for tt in range(NT):
                    DS(sy.dma_start(out=v_sb[:, tt * DK:(tt + 1) * DK],
                                    in_=v_d[tt * 128:(tt + 1) * 128, :]))
            tk_ldv = S.add("sync", p4_ldv, deps=[p2_gate], incs=[("dS", 512)])

            tk_ldq = [None] * 8
            tk_qm_done = [None] * 8

            def emit_ldq(qm):
                def f(sy, qm=qm):
                    src = bass.AP(q_d.tensor, qm * 512,
                                  [[T, 128], [128 * T, NH], [1, 512]])
                    DS(sy.dma_start(out=qm_sb[qm % 2][:], in_=src))
                tk_ldq[qm] = S.add(
                    "sync", f,
                    deps=[p2_gate, tk_qm_done[qm - 2] if qm >= 2 else None],
                    incs=[("dS", 16)])

            emit_ldq(0)
            emit_ldq(1)

            tk_exp = {}
            tk_avl = {}
            tk_cmb_prev = None
            tk_bce_prev = None
            tk_bc_prev = None
            tk_oh_dma = {}
            for qm in range(8):
                n_kt = 4 * qm + 4
                for hi in range(NH):
                    def emit_skt(kt, hi=hi, qm=qm):
                        par = kt % 2
                        qoff = max(0, (kt - 4 * qm) * 128)

                        def smm(t, hi=hi, kt=kt, qm=qm, par=par, qoff=qoff):
                            l1 = k_sb[0:64, hi * T + kt * 128:
                                      hi * T + (kt + 1) * 128]
                            r1 = qm_sb[qm % 2][0:64, hi * 512 + qoff:
                                               (hi + 1) * 512]
                            t.matmul(s_ps[par][:, qoff:512], l1, r1,
                                     start=True, stop=True)
                            l2 = k_sb[64:128, hi * T + kt * 128:
                                      hi * T + (kt + 1) * 128]
                            r2 = qm_sb[qm % 2][64:128, hi * 512 + qoff:
                                               (hi + 1) * 512]
                            PE(t.matmul(s_ps[par][:, 512 + qoff:1024], l2, r2,
                                        start=True, stop=True))
                        tk_s = S.add("tensor", smm,
                                     deps=[tk_ldq[qm], tk_ldk,
                                           tk_exp.get((qm, hi, kt - 2)),
                                           tk_cmb_prev if kt <= 1 else None,
                                           tk_bce_prev if kt <= 1 else None],
                                     incs=[("pe", 1)])

                        if kt >= 4 * qm:
                            def msk(v, par=par, qoff=qoff):
                                v.tensor_tensor(
                                    out=s_ps[par][:, qoff:qoff + 128],
                                    in0=s_ps[par][:, qoff:qoff + 128],
                                    in1=dmask[:], op=OP.add)
                                VE(v.tensor_tensor(
                                    out=s_ps[par][:, 512 + qoff:512 + qoff + 128],
                                    in0=s_ps[par][:, 512 + qoff:512 + qoff + 128],
                                    in1=dmask[:], op=OP.add))
                            tk_m = S.add("vector", msk, deps=[tk_s, tk_mask],
                                         incs=[("ve", 1)])
                        else:
                            tk_m = tk_s

                        def expf(s, par=par, qoff=qoff, kt=kt):
                            pb = p12[kt % 2]
                            if qoff == 0:
                                SE(s.activation(out=pb[:], in_=s_ps[par][:],
                                                func=AF.Exp, bias=cn8[:],
                                                scale=0.125))
                            else:
                                s.activation(out=pb[:, qoff:512],
                                             in_=s_ps[par][:, qoff:512],
                                             func=AF.Exp, bias=cn8[:],
                                             scale=0.125)
                                SE(s.activation(
                                    out=pb[:, 512 + qoff:1024],
                                    in_=s_ps[par][:, 512 + qoff:1024],
                                    func=AF.Exp, bias=cn8[:], scale=0.125))
                        tk_exp[(qm, hi, kt)] = S.add(
                            "scalar", expf,
                            deps=[tk_m, tk_avl.get((qm, hi, kt - 2))],
                            incs=[("se", 1)])

                    def emit_avkt(kt, hi=hi, qm=qm):
                        qoff = max(0, (kt - 4 * qm) * 128)

                        def avl(t, hi=hi, kt=kt, qoff=qoff, qm=qm):
                            pb = p12[kt % 2]
                            vt = v_sb[:, kt * DK + hi * 128:
                                      kt * DK + (hi + 1) * 128]
                            st0 = (kt == 0)
                            sp0 = (kt == 4 * qm + 3)
                            t.matmul(o_ps[0][:, qoff:512], vt, pb[:, qoff:512],
                                     start=st0, stop=sp0, skip_group_check=True)
                            t.matmul(o_ps[1][:, qoff:512], vt,
                                     pb[:, 512 + qoff:1024],
                                     start=st0, stop=sp0, skip_group_check=True)
                            t.matmul(l_ps[0:1, qoff:512], ones_kb[:],
                                     pb[:, qoff:512],
                                     start=st0, stop=sp0, skip_group_check=True)
                            PE(t.matmul(l_ps[0:1, 512 + qoff:1024], ones_kb[:],
                                        pb[:, 512 + qoff:1024],
                                        start=st0, stop=sp0,
                                        skip_group_check=True))
                        tk_avl[(qm, hi, kt)] = S.add(
                            "tensor", avl,
                            deps=[tk_exp[(qm, hi, kt)], tk_ldv,
                                  tk_cmb_prev if kt <= 1 else None],
                            incs=[("pe", 1)])

                    for kt in range(n_kt + 1):
                        if kt < n_kt:
                            emit_skt(kt)
                        if kt >= 1:
                            emit_avkt(kt - 1)

                    last_av = tk_avl[(qm, hi, n_kt - 1)]

                    tk_rla = S.add("scalar", lambda s: SE(
                        s.activation(out=rl_f[:], in_=l_ps[0:1, :], func=AF.Ln)),
                        deps=[last_av], incs=[("se", 1)])
                    tk_rl = S.add("scalar", lambda s: SE(
                        s.activation(out=rl_f[:], in_=rl_f[:], func=AF.Exp,
                                     scale=-1.0)),
                        deps=[tk_rla], incs=[("se", 1)])

                    tk_rn = S.add("vector", lambda v, hi=hi: VE(
                        v.tensor_scalar(out=rl_f[0:1, 512:1024],
                                        in0=rl_f[0:1, 512:1024],
                                        scalar1=nlam[0:1, hi:hi + 1],
                                        scalar2=None, op0=OP.mult)),
                        deps=[tk_rl, tk_nlam], incs=[("ve", 1)])
                    tk_rlb = S.add("vector", lambda v: VE(
                        v.tensor_copy(rl_b[:], rl_f[:])),
                        deps=[tk_rn, tk_bc_prev], incs=[("ve", 1)])

                    def bc(t):
                        t.matmul(s_ps[0][:, 0:512], ones_rb[:],
                                 rl_b[0:1, 0:512], start=True, stop=True)
                        PE(t.matmul(s_ps[0][:, 512:1024], ones_rb[:],
                                    rl_b[0:1, 512:1024], start=True, stop=True))
                    tk_bc = S.add("tensor", bc,
                                  deps=[tk_rlb, tk_exp[(qm, hi, n_kt - 1)]],
                                  incs=[("pe", 1)])
                    tk_bc_prev = tk_bc

                    def bce(s):
                        s.copy(rlb1[:], s_ps[0][:, 0:512])
                        SE(s.copy(rlb2[:], s_ps[0][:, 512:1024]))
                    tk_bce = S.add("scalar", bce, deps=[tk_bc], incs=[("se", 1)])
                    tk_bce_prev = tk_bce

                    tk_c1 = S.add("vector", lambda v: VE(
                        v.tensor_tensor(out=tcmb[0][:], in0=o_ps[0][:],
                                        in1=rlb1[:], op=OP.mult)),
                        deps=[tk_bce, last_av], incs=[("ve", 1)])
                    tk_c2 = S.add("vector", lambda v: VE(
                        v.tensor_tensor(out=tcmb[1][:], in0=o_ps[1][:],
                                        in1=rlb2[:], op=OP.mult)),
                        deps=[tk_bce, last_av], incs=[("ve", 1)])
                    tk_cmb = S.add("vector", lambda v, hi=hi: VE(
                        v.tensor_tensor(out=oh_sb[:, hi * 512:(hi + 1) * 512],
                                        in0=tcmb[0][:], in1=tcmb[1][:],
                                        op=OP.add)),
                        deps=[tk_c1, tk_c2, tk_oh_dma.get((qm - 1, hi))],
                        incs=[("ve", 1)])
                    tk_cmb_prev = tk_cmb
                    tk_oh_dma[(qm, hi)] = S.add(
                        "scalar", lambda s2, hi=hi, qm=qm: DA(s2.dma_start(
                            out=o_d[hi * 128:(hi + 1) * 128,
                                    qm * 512:(qm + 1) * 512],
                            in_=oh_sb[:, hi * 512:(hi + 1) * 512])),
                        deps=[tk_cmb], incs=[("dA", 16)])
                tk_qm_done[qm] = tk_cmb_prev
                if qm + 2 < 8:
                    emit_ldq(qm + 2)
            S.flush(block, sems, "P4")
            p4_gate = S.snap()

        es56 = contextlib.ExitStack()
        xnt6 = es56.enter_context(nc.sbuf_tensor("xnt6", [128, 16 * TOWN], BF16))
        with contextlib.ExitStack() as es5:
            E5 = es5.enter_context
            wob = E5(nc.sbuf_tensor("wob", [128, NH * D], BF16))
            otl = [E5(nc.sbuf_tensor(f"otl{i}", [128, NH * 128], BF16))
                   for i in range(2)]
            att_sb = [E5(nc.sbuf_tensor(f"att_sb{i}", [128, D], BF16))
                      for i in range(2)]
            ap_ps = [E5(nc.psum_tensor(f"ap_ps{i}", [128, D], F32))
                     for i in range(2)]

            tk_wo = S.add("gpsimd", lambda g: DG(g.dma_start(
                out=wob[:].rearrange("p (a n) -> p a n", a=NH),
                in_=wo.rearrange("(a p) n -> p a n", p=128))),
                deps=[p4_gate], incs=[("dG", 16)])

            tk_p5 = {}
            tk_ev5 = {}
            tk_attst = {}
            for tt in range(NT):
                tb = tt % 2

                def ldo(sy, tt=tt, tb=tb):
                    src = bass.AP(o_d.tensor, tt * 128,
                                  [[T, 128], [128 * T, NH], [1, 128]])
                    DS(sy.dma_start(out=otl[tb][:], in_=src))
                tk_ldo = S.add("sync", ldo, deps=[p4_gate, tk_p5.get(tt - 2)],
                               incs=[("dS", 16)])

                def p5mm(t, tb=tb):
                    for dv in range(NH):
                        for ch in range(4):
                            ins = t.matmul(
                                ap_ps[tb][:, ch * 512:(ch + 1) * 512],
                                otl[tb][:, dv * 128:(dv + 1) * 128],
                                wob[:, dv * D + ch * 512: dv * D + (ch + 1) * 512],
                                start=(dv == 0), stop=(dv == NH - 1))
                    PE(ins)
                tk_p5[tt] = S.add("tensor", p5mm,
                                  deps=[tk_ldo, tk_wo, tk_ev5.get(tt - 2)],
                                  incs=[("pe", 1)])
                tk_ev5[tt] = S.add("vector", lambda v, tb=tb: VE(
                    v.tensor_copy(att_sb[tb][:], ap_ps[tb][:])),
                    deps=[tk_p5[tt], tk_attst.get(tt - 2)], incs=[("ve", 1)])

                def stc(s2, tt=tt, tb=tb):
                    DA(s2.dma_start(out=cc_in[tt * 128:(tt + 1) * 128, :],
                                    in_=att_sb[tb][:]))
                    if dev >= 2:
                        DA(s2.dma_start(
                            out=att_dump[tt * 128:(tt + 1) * 128, :],
                            in_=att_sb[tb][:]))
                tk_attst[tt] = S.add("scalar", stc, deps=[tk_ev5[tt]],
                                     incs=[("dA", 32 if dev >= 2 else 16)])

            def _ccfn(g):
                ATTACH["cc"] += 1
                return g.collective_compute(
                "ReduceScatter", OP.add,
                replica_groups=[[0, 1], [2, 3], [4, 5], [6, 7]],
                    ins=[cc_in.opt()], outs=[cc_out.opt()]).then_inc(sems["cc"], 1)
            tk_cc = S.add("gpsimd", _ccfn,
                          deps=[tk_attst[NT - 1], {"dA": S.c["dA"]}], incs=[("cc", 1)])

            def ldxn6(sy):
                for dt in range(16):
                    DS(sy.dma_start(out=xnt6[:, dt * TOWN:(dt + 1) * TOWN],
                                    in_=xn2_d[TOWN:TOWN + TOWN,
                                              dt * 128:(dt + 1) * 128],
                                    transpose=True))
            tk_xn6 = S.add("sync", ldxn6, deps=[p4_gate, tk_p1_done],
                           incs=[("dS", 256)])
            if dev >= 2:
                S.add("gpsimd", lambda g: DG(g.dma_start(out=cc_dump, in_=cc_out)),
                      deps=[tk_cc], incs=[("dG", 16)])
            S.flush(block, sems, "P5")
            p5_gate = S.snap()

        with contextlib.ExitStack() as es6:
            E6 = es6.enter_context
            wgt = [E6(nc.sbuf_tensor(f"wgt{i}", [128, 16 * 512], BF16))
                   for i in range(2)]
            wup = [E6(nc.sbuf_tensor(f"wup{i}", [128, 16 * 512], BF16))
                   for i in range(2)]
            gsil = E6(nc.sbuf_tensor("gsil", [128, TOWN], BF16))
            h_sb = [E6(nc.sbuf_tensor(f"h_sb{i}", [128, TOWN], BF16))
                    for i in range(2)]
            gA = E6(nc.psum_tensor("gA", [128, TOWN], F32))
            gB = E6(nc.psum_tensor("gB", [128, TOWN], F32))

            tk_h = {}
            tk_hv = {}
            tk_sil = {}
            tk_blk = [None, None]
            for blk in range(11):
                bb = blk % 2

                def ldw(sy, blk=blk, bb=bb):
                    for dt in range(16):
                        DS(sy.dma_start(
                            out=wgt[bb][:, dt * 512:(dt + 1) * 512],
                            in_=wgu_d[blk * 512:(blk + 1) * 512,
                                      dt * 128:(dt + 1) * 128], transpose=True))
                    for dt in range(16):
                        DS(sy.dma_start(
                            out=wup[bb][:, dt * 512:(dt + 1) * 512],
                            in_=wgu_d[II + blk * 512: II + (blk + 1) * 512,
                                      dt * 128:(dt + 1) * 128], transpose=True))
                tk_ldw = S.add("sync", ldw,
                               deps=[tk_dq_store["gu"][4 * blk + 3],
                                     tk_dq_store["gu"][44 + 4 * blk + 3],
                                     tk_blk[bb], p5_gate],
                               incs=[("dS", 512)])

                for pos in range(4):
                    ot = blk * 4 + pos

                    def gmm(t, bb=bb, pos=pos):
                        for dt in range(16):
                            lhs = wgt[bb][:, dt * 512 + pos * 128:
                                          dt * 512 + (pos + 1) * 128]
                            for ch in range(4):
                                ins = t.matmul(
                                    gA[:, ch * 512:(ch + 1) * 512], lhs,
                                    xnt6[:, dt * TOWN + ch * 512:
                                         dt * TOWN + (ch + 1) * 512],
                                    start=(dt == 0), stop=(dt == 15))
                        PE(ins)
                    tk_g = S.add("tensor", gmm,
                                 deps=[tk_ldw, tk_xn6, tk_sil.get(ot - 1)],
                                 incs=[("pe", 1)])

                    def umm(t, bb=bb, pos=pos):
                        for dt in range(16):
                            lhs = wup[bb][:, dt * 512 + pos * 128:
                                          dt * 512 + (pos + 1) * 128]
                            for ch in range(4):
                                ins = t.matmul(
                                    gB[:, ch * 512:(ch + 1) * 512], lhs,
                                    xnt6[:, dt * TOWN + ch * 512:
                                         dt * TOWN + (ch + 1) * 512],
                                    start=(dt == 0), stop=(dt == 15))
                        PE(ins)
                    tk_u = S.add("tensor", umm,
                                 deps=[tk_ldw, tk_xn6, tk_hv.get(ot - 1)],
                                 incs=[("pe", 1)])

                    def sil(s):
                        for ch in range(4):
                            ins = s.activation(
                                out=gsil[:, ch * 512:(ch + 1) * 512],
                                in_=gA[:, ch * 512:(ch + 1) * 512], func=AF.Silu)
                        SE(ins)
                    tk_sil[ot] = S.add("scalar", sil, deps=[tk_g],
                                       incs=[("se", 1)])
                    hb = ot % 2

                    def hv(v, hb=hb):
                        for ch in range(4):
                            ins = v.tensor_tensor(
                                out=h_sb[hb][:, ch * 512:(ch + 1) * 512],
                                in0=gB[:, ch * 512:(ch + 1) * 512],
                                in1=gsil[:, ch * 512:(ch + 1) * 512], op=OP.mult)
                        VE(ins)
                    tk_hv[ot] = S.add("vector", hv,
                                      deps=[tk_u, tk_sil[ot], tk_h.get(ot - 2)],
                                      incs=[("ve", 1)])
                    tk_h[ot] = S.add("scalar", lambda s2, ot=ot, hb=hb: DA(
                        s2.dma_start(out=h_d[ot * 128:(ot + 1) * 128, :],
                                     in_=h_sb[hb][:])),
                        deps=[tk_hv[ot]], incs=[("dA", 16)])
                tk_blk[bb] = tk_h[blk * 4 + 3]
            S.flush(block, sems, "P6")
            p6_gate = S.snap()
        es56.close()

        with contextlib.ExitStack() as es7:
            E7 = es7.enter_context
            wdt = E7(nc.sbuf_tensor("wdt", [128, 44 * 1024], BF16))
            h7 = [E7(nc.sbuf_tensor(f"h7_{i}", [128, 44 * 128], BF16))
                  for i in range(2)]
            y0st = [E7(nc.sbuf_tensor(f"y0st{i}", [128, 1024], F32))
                    for i in range(2)]
            att7 = [E7(nc.sbuf_tensor(f"att7_{i}", [128, D], BF16))
                    for i in range(2)]
            xf7 = [E7(nc.sbuf_tensor(f"xf7_{i}", [128, D], F32)) for i in range(2)]
            y0l = [E7(nc.sbuf_tensor(f"y0l{i}", [128, 1024], F32))
                   for i in range(2)]
            outb = [E7(nc.sbuf_tensor(f"outb{i}", [128, D], F32))
                    for i in range(2)]
            ssqa = E7(nc.sbuf_tensor("ssqa", [128, NTO], F32))
            ssqh = [E7(nc.sbuf_tensor(f"ssqh{i}", [128, 1], F32)) for i in range(2)]
            ssqt = [E7(nc.sbuf_tensor(f"ssqt{i}", [128, 1], F32)) for i in range(2)]
            lnv7 = [E7(nc.sbuf_tensor(f"lnv7_{i}", [128, 1], F32))
                    for i in range(2)]
            rstd7 = [E7(nc.sbuf_tensor(f"rstd7_{i}", [128, 1], F32))
                     for i in range(2)]
            sq7 = E7(nc.sbuf_tensor("sq7", [128, 1024], BF16))
            y_ps = [E7(nc.psum_tensor(f"y_ps{i}", [128, 1024], F32))
                    for i in range(2)]

            tk_z = S.add("scalar", lambda s: SE(s.memzero(ssqa[:])),
                         deps=[p6_gate], incs=[("se", 1)])
            tk_ob0 = S.add("vector", lambda v: (
                v.memset(outb[0][:], 0.0),
                VE(v.memset(outb[1][:], 0.0)))[-1],
                deps=[p6_gate], incs=[("ve", 1)])

            tk_out = {}
            tk_y0 = {}
            tk_pe7 = {}
            for half in range(2):
                def ldwd(sy, half=half):
                    for it in range(44):
                        DS(sy.dma_start(
                            out=wdt[:, it * 1024:(it + 1) * 1024],
                            in_=wdn_d[half * 1024:(half + 1) * 1024,
                                      it * 128:(it + 1) * 128], transpose=True))
                dep_wd = [p6_gate] if half == 0 else [tk_pe7[(0, NTO - 1)]]
                dn_deps = [tk_dq_store["dn"][dbi][3] for dbi in range(16)]
                tk_ldwd = S.add("sync", ldwd, deps=dep_wd + dn_deps,
                                incs=[("dS", 704)])

                order = list(range(NTO))
                for pos, tt in enumerate(order):
                    tb = pos % 2
                    prev2 = order[pos - 2] if pos >= 2 else None

                    def ldh(sy, tt=tt, tb=tb):
                        for ggg in range(11):
                            src = bass.AP(h_d.tensor,
                                          ggg * 4 * 128 * TOWN + tt * 128,
                                          [[TOWN, 128], [128 * TOWN, 4], [1, 128]])
                            DS(sy.dma_start(
                                out=h7[tb][:, ggg * 512:(ggg + 1) * 512], in_=src))
                    if half == 0:
                        dep_h7 = tk_pe7.get((0, prev2)) if prev2 is not None else None
                    else:
                        dep_h7 = (tk_pe7.get((1, prev2)) if prev2 is not None
                                  else tk_pe7.get((0, NTO - 2 + pos)))
                    tk_ldh = S.add("sync", ldh, deps=[p6_gate, dep_h7],
                                   incs=[("dS", 176)])

                    def dmm(t, tb=tb):
                        for it in range(44):
                            for ch in range(2):
                                ins = t.matmul(
                                    y_ps[tb][:, ch * 512:(ch + 1) * 512],
                                    h7[tb][:, it * 128:(it + 1) * 128],
                                    wdt[:, it * 1024 + ch * 512:
                                        it * 1024 + (ch + 1) * 512],
                                    start=(it == 0), stop=(it == 43))
                        PE(ins)
                    if half == 0:
                        dep_bank = tk_y0.get(prev2) if prev2 is not None else None
                    else:
                        dep_bank = (tk_out.get(prev2) if prev2 is not None
                                    else tk_y0.get(NTO - 2 + pos))
                    tk_pe7[(half, tt)] = S.add("tensor", dmm,
                                               deps=[tk_ldh, tk_ldwd, dep_bank],
                                               incs=[("pe", 1)])
                    def sqacc(s, tb=tb):
                        s.memzero(ssqh[tb][:])
                        SE(s.activation(out=sq7[:], in_=y_ps[tb][:],
                                        func=AF.Square, accum_out=ssqh[tb][:]))
                    tk_ssq = S.add("scalar", sqacc,
                                   deps=[tk_pe7[(half, tt)], tk_z],
                                   incs=[("se", 1)])

                    if half == 0:
                        tk_ssqc = S.add("scalar", lambda s, tb=tb, tt=tt: SE(
                            s.copy(ssqa[:, tt:tt + 1], ssqh[tb][:])),
                            deps=[tk_ssq], incs=[("se", 1)])
                        tk_ev7 = S.add("vector", lambda v, tb=tb: VE(
                            v.tensor_copy(y0st[tb][:], y_ps[tb][:])),
                            deps=[tk_pe7[(half, tt)], tk_y0.get(tt - 2), tk_ssq],
                            incs=[("ve", 1)])
                        tk_y0[tt] = S.add("scalar", lambda s2, tt=tt, tb=tb: DA(
                            s2.dma_start(out=y0_d[tt * 128:(tt + 1) * 128, :],
                                         in_=y0st[tb][:])),
                            deps=[tk_ev7], incs=[("dA", 16)])
                    else:
                        tk_tot = S.add("vector", lambda v, tb=tb, tt=tt: VE(
                            v.tensor_tensor(out=ssqt[tb][:],
                                            in0=ssqa[:, tt:tt + 1],
                                            in1=ssqh[tb][:], op=OP.add)),
                            deps=[tk_ssq], incs=[("ve", 1)])
                        tk_l7 = S.add("scalar", lambda s, tb=tb: SE(
                            s.activation(out=lnv7[tb][:], in_=ssqt[tb][:],
                                         func=AF.Ln, bias=ceps2[:],
                                         scale=1.0 / D)),
                            deps=[tk_tot], incs=[("se", 1)])
                        tk_rs7 = S.add("scalar", lambda s, tb=tb: SE(
                            s.activation(out=rstd7[tb][:], in_=lnv7[tb][:],
                                         func=AF.Exp, scale=-0.5)),
                            deps=[tk_l7], incs=[("se", 1)])
                        if dev >= 2:
                            tk_y1e = S.add("vector", lambda v, tb=tb: VE(
                                v.tensor_copy(y0st[tb][:], y_ps[tb][:])),
                                deps=[tk_pe7[(half, tt)], tk_rs7],
                                incs=[("ve", 1)])
                            S.add("scalar", lambda s2, tt=tt, tb=tb: DA(
                                s2.dma_start(out=y1_dump[tt * 128:(tt + 1) * 128, :],
                                             in_=y0st[tb][:])),
                                deps=[tk_y1e], incs=[("dA", 16)])
                        dep_prev_out = (tk_out.get(prev2)
                                        if prev2 is not None else None)
                        pb = tb

                        def ld3(g, tt=tt, tb=tb, pb=pb):
                            DL(g.dma_start(
                                out=att7[tb][:],
                                in_=cc_out[tt * 128:(tt + 1) * 128, :]), pb)
                            DL(g.dma_start(
                                out=xf7[tb][:],
                                in_=xown[tt * 128:(tt + 1) * 128, :]), pb)
                            DL(g.dma_start(
                                out=y0l[tb][:],
                                in_=y0_d[tt * 128:(tt + 1) * 128, :]), pb)
                        tk_ld3 = S.add("gpsimd", ld3,
                                       deps=[tk_cc, tk_y0[tt], dep_prev_out],
                                       incs=[(f"dL{pb}", 48)])
                        tk_lda = tk_ldxo = tk_ldy0 = tk_ld3

                        deps_f = [tk_rs7, tk_lda, tk_ldxo, tk_ldy0,
                                  tk_pe7[(half, tt)], dep_prev_out]
                        tk_f1 = S.add("vector", lambda v, tb=tb: VE(
                            v.scalar_tensor_tensor(
                                out=outb[tb][:, 1024:2048], in0=y_ps[tb][:],
                                scalar=rstd7[tb][:], in1=att7[tb][:, 1024:2048],
                                op0=OP.mult, op1=OP.add)),
                            deps=deps_f, incs=[("ve", 1)])
                        tk_f2 = S.add("vector", lambda v, tb=tb: VE(
                            v.tensor_tensor(out=outb[tb][:, 1024:2048],
                                            in0=outb[tb][:, 1024:2048],
                                            in1=xf7[tb][:, 1024:2048],
                                            op=OP.add)),
                            deps=[tk_f1], incs=[("ve", 1)])
                        tk_f3 = S.add("vector", lambda v, tb=tb: VE(
                            v.scalar_tensor_tensor(
                                out=outb[tb][:, 0:1024], in0=y0l[tb][:],
                                scalar=rstd7[tb][:], in1=att7[tb][:, 0:1024],
                                op0=OP.mult, op1=OP.add)),
                            deps=deps_f, incs=[("ve", 1)])
                        tk_fin = S.add("vector", lambda v, tb=tb: VE(
                            v.tensor_tensor(out=outb[tb][:, 0:1024],
                                            in0=outb[tb][:, 0:1024],
                                            in1=xf7[tb][:, 0:1024], op=OP.add)),
                            deps=[tk_f2, tk_f3], incs=[("ve", 1)])
                        if dev >= 2:
                            def dumps7(s2, tt=tt, tb=tb):
                                DA(s2.dma_start(
                                    out=att7_dump[tt * 128:(tt + 1) * 128, :],
                                    in_=att7[tb][:]))
                                DA(s2.dma_start(
                                    out=y0l_dump[tt * 128:(tt + 1) * 128, :],
                                    in_=y0l[tb][:]))
                                DA(s2.dma_start(
                                    out=rstd_dump2[tt * 128:(tt + 1) * 128, :],
                                    in_=rstd7[tb][:]))
                            S.add("scalar", dumps7,
                                  deps=[tk_fin], incs=[("dA", 48)])
                        tk_out[tt] = S.add("scalar", lambda s2, tt=tt, tb=tb: DA(
                            s2.dma_start(out=out[tt * 128:(tt + 1) * 128, :],
                                         in_=outb[tb][:])),
                            deps=[tk_fin], incs=[("dA", 16)])

            S.add("sync", lambda sy: sy.nop(),
                  deps=[{"dA": S.c["dA"], "dG": S.c["dG"], "dS": S.c["dS"]}])
            S.flush(block, sems, "P7")
    return nc


_NC_CACHE = {}


def _get_nc(dev=0):
    if dev not in _NC_CACHE:
        _NC_CACHE[dev] = build(dev)
    return _NC_CACHE[dev]


def make_in_maps(L0_attn_x, L0_attn_Wq, L0_attn_Wk, L0_attn_Wv, L0_attn_Wo,
                 L0_attn_lambda_init, L0_ffn_gate_up_packed, L0_ffn_gate_up_scales,
                 L0_ffn_down_packed, L0_ffn_down_scales):
    pi_h = _pi4(D)
    pi_i = _pi4(II)
    asc = np.ascontiguousarray

    xq = np.asarray(L0_attn_x)
    wq_p = asc(np.asarray(L0_attn_Wq)[pi_h])
    wk_p = asc(np.asarray(L0_attn_Wk)[pi_h])
    wv_p = asc(np.asarray(L0_attn_Wv)[pi_h])
    wo_a = np.asarray(L0_attn_Wo)
    lam_a = np.asarray(L0_attn_lambda_init)
    gu_rows = np.concatenate([pi_i, II + pi_i])
    gu16 = asc(np.asarray(L0_ffn_gate_up_packed).view(np.uint16)[gu_rows])
    gus_t = asc(np.asarray(L0_ffn_gate_up_scales).T[gu_rows])
    dn16 = asc(np.asarray(L0_ffn_down_packed).view(np.uint16))
    dns_t = asc(np.asarray(L0_ffn_down_scales).T)

    in_maps = []
    for c in range(8):
        b, hg = divmod(c, 2)
        R = hg * TOWN
        in_maps.append({
            "x": asc(xq[b]),
            "xown": asc(xq[b][R:R + TOWN]),
            "wq": asc(wq_p[:, hg * DK:(hg + 1) * DK]),
            "wk": asc(wk_p[:, hg * DK:(hg + 1) * DK]),
            "wv": asc(wv_p[:, hg * DK:(hg + 1) * DK]),
            "wo": asc(wo_a[hg * DK:(hg + 1) * DK, :]),
            "lam": asc(lam_a[hg * NH:(hg + 1) * NH].reshape(1, NH)),
            "gu16": gu16,
            "dn16": dn16,
            "gus": gus_t,
            "dns": dns_t,
            "ofs": np.array([[(TOWN - R) * D]], dtype=np.int32),
        })
    return in_maps


def kernel(L0_attn_x, L0_attn_Wq, L0_attn_Wk, L0_attn_Wv, L0_attn_Wo,
           L0_attn_lambda_init, L0_ffn_gate_up_packed, L0_ffn_gate_up_scales,
           L0_ffn_down_packed, L0_ffn_down_scales):
    in_maps = make_in_maps(L0_attn_x, L0_attn_Wq, L0_attn_Wk, L0_attn_Wv,
                           L0_attn_Wo, L0_attn_lambda_init,
                           L0_ffn_gate_up_packed, L0_ffn_gate_up_scales,
                           L0_ffn_down_packed, L0_ffn_down_scales)
    nc = _get_nc(0)
    res = run_bass_kernel_spmd(nc, in_maps, core_ids=list(range(8)))
    outp = np.empty((4, T, D), dtype=np.float32)
    for c in range(8):
        b, hg = divmod(c, 2)
        R = hg * TOWN
        outp[b, R:R + TOWN, :] = res.results[c]["out"]
    return outp


# version 06
# speedup vs baseline: 1.1351x; 1.0019x over previous
import contextlib
import numpy as np
import ml_dtypes
import concourse.bass as bass
import concourse.mybir as mybir
from concourse.bass_utils import run_bass_kernel_spmd
from concourse.alu_op_type import AluOpType as OP

F32 = mybir.dt.float32
BF16 = mybir.dt.bfloat16
F16 = mybir.dt.float16
U16 = mybir.dt.uint16
U32 = mybir.dt.uint32
I32 = mybir.dt.int32
AF = mybir.ActivationFunctionType
AX = mybir.AxisListType.X

T = 4096
D = 2048
DK = 1024
NH = 8
II = 5632
I2 = 2 * II
TOWN = 2048
NT = T // 128
NTO = TOWN // 128

MAG_MASKS = [(0x0007, 6, True), (0x0070, 2, True), (0x0700, 2, False), (0x7000, 6, False)]
SGN_MASKS = [(0x0008, 12), (0x0080, 8), (0x0800, 4), (0x8000, 0)]

ENGS = ["sync", "vector", "scalar", "tensor", "gpsimd"]
OWNER = {"pe": "tensor", "ve": "vector", "se": "scalar", "ge": "gpsimd"}


def _pi4(n):
    q = n // 4
    j = np.arange(n)
    return 4 * (j % q) + j // q


class Sched:

    def __init__(self):
        self.ops = {e: [] for e in ENGS}
        self.waited = {}
        self.c = {s: 0 for s in ["pe", "ve", "se", "ge", "dS", "dA", "dG",
                                 "dL0", "dL1", "cc"]}

    def add(self, eng, fn, deps=(), incs=()):
        waits = {}
        for d in deps:
            if d is None:
                continue
            items = d.items() if isinstance(d, dict) else [d]
            for s, v in items:
                waits[s] = max(waits.get(s, 0), v)
        tok = {}
        for s, n in incs:
            self.c[s] += n
            tok[s] = self.c[s]
        self.ops[eng].append((waits, fn, dict(incs)))
        return tok

    def snap(self):
        return dict(self.c)

    def flush(self, block, sems, name=None):
        import contextlib as _cl
        scope = (block.bass.named_scope(name) if name is not None
                 else _cl.nullcontext())
        with scope:
            self._flush_inner(block, sems)

    def _flush_inner(self, block, sems):
        for ename in ENGS:
            oplist = self.ops[ename]
            if not oplist:
                continue

            def body(eng, oplist=oplist, ename=ename):
                waited = self.waited.setdefault(ename, {})
                for waits, fn, dec in oplist:
                    for s in sorted(waits):
                        v = waits[s]
                        if waited.get(s, -1) >= v:
                            continue
                        eng.wait_ge(sems[s], v)
                        waited[s] = v
                    before = dict(ATTACH)
                    fn(eng)
                    for sname, n in dec.items():
                        got = ATTACH[sname] - before[sname]
                        if got != n:
                            import traceback
                            print(f"*** INC MISMATCH {ename} {sname}: declared {n} attached {got} fn={fn}")
                    for sname in ATTACH:
                        if sname not in dec and ATTACH[sname] != before[sname]:
                            print(f"*** UNDECLARED ATTACH {ename} {sname} fn={fn}")
            getattr(block, ename)(body)
        self.ops = {e: [] for e in ENGS}


ATTACH = {s: 0 for s in ["pe", "ve", "se", "ge", "dS", "dA", "dG",
                         "dL0", "dL1", "cc"]}


def build(dev=0):
    for k in ATTACH:
        ATTACH[k] = 0
    nc = bass.Bass("TRN2", target_bir_lowering=False)

    x = nc.dram_tensor("x", [T, D], F32, kind="ExternalInput").ap()
    xown = nc.dram_tensor("xown", [TOWN, D], F32, kind="ExternalInput").ap()
    wq = nc.dram_tensor("wq", [D, DK], F32, kind="ExternalInput").ap()
    wk = nc.dram_tensor("wk", [D, DK], F32, kind="ExternalInput").ap()
    wv = nc.dram_tensor("wv", [D, DK], F32, kind="ExternalInput").ap()
    wo = nc.dram_tensor("wo", [DK, D], F32, kind="ExternalInput").ap()
    lam = nc.dram_tensor("lam", [1, NH], F32, kind="ExternalInput").ap()
    gu16 = nc.dram_tensor("gu16", [I2, D // 4], U16, kind="ExternalInput").ap()
    dn16 = nc.dram_tensor("dn16", [D, II // 4], U16, kind="ExternalInput").ap()
    gus = nc.dram_tensor("gus", [I2, 16], F16, kind="ExternalInput").ap()
    dns = nc.dram_tensor("dns", [D, 44], F16, kind="ExternalInput").ap()
    ofs = nc.dram_tensor("ofs", [1, 1], I32, kind="ExternalInput").ap()
    out = nc.dram_tensor("out", [TOWN, D], F32, kind="ExternalOutput").ap()

    def scratch(name, shape, dt):
        kind = "ExternalOutput" if dev >= 2 else "Internal"
        return nc.dram_tensor(name, shape, dt, kind=kind).ap()

    xn_d = scratch("xn_d", [T, D], BF16)
    xn2_d = scratch("xn2_d", [T + TOWN, D], BF16)
    k_d = scratch("k_d", [DK, T], BF16)
    v_d = scratch("v_d", [T, DK], BF16)
    q_d = scratch("q_d", [DK, T], BF16)
    o_d = scratch("o_d", [DK, T], BF16)
    wgu_d = scratch("wgu_d", [I2, D], BF16)
    wdn_d = scratch("wdn_d", [D, II], BF16)
    h_d = scratch("h_d", [II, TOWN], BF16)
    y0_d = scratch("y0_d", [TOWN, 1024], F32)
    cc_in = nc.dram_tensor("cc_in", [T, D], BF16, kind="Internal").ap()
    cc_out = nc.dram_tensor("cc_out", [TOWN, D], BF16, kind="Internal").ap()
    att_dump = (nc.dram_tensor("att_dump", [T, D], BF16, kind="ExternalOutput").ap()
                if dev >= 2 else None)
    cc_dump = (nc.dram_tensor("cc_dump", [TOWN, D], BF16, kind="ExternalOutput").ap()
               if dev >= 2 else None)
    rstd_dump = (nc.dram_tensor("rstd_dump", [TOWN, 1], F32, kind="ExternalOutput").ap()
                 if dev >= 2 else None)
    y1_dump = (nc.dram_tensor("y1_dump", [TOWN, 1024], F32, kind="ExternalOutput").ap()
               if dev >= 2 else None)
    att7_dump = (nc.dram_tensor("att7_dump", [TOWN, D], BF16, kind="ExternalOutput").ap()
                 if dev >= 2 else None)
    y0l_dump = (nc.dram_tensor("y0l_dump", [TOWN, 1024], F32, kind="ExternalOutput").ap()
                if dev >= 2 else None)
    rstd_dump2 = (nc.dram_tensor("rstd_dump2", [TOWN, 1], F32, kind="ExternalOutput").ap()
                  if dev >= 2 else None)

    S = Sched()
    es = contextlib.ExitStack()
    with es:
        E = es.enter_context
        sems = {n: E(nc.semaphore(n)) for n in
                ["pe", "ve", "se", "ge", "dS", "dA", "dG", "dL0", "dL1", "cc"]}
        block = E(nc.Block())

        def DS(instr):
            ATTACH["dS"] += 16
            return instr.then_inc(sems["dS"], 16)

        def DA(instr):
            ATTACH["dA"] += 16
            return instr.then_inc(sems["dA"], 16)

        def DG(instr):
            ATTACH["dG"] += 16
            return instr.then_inc(sems["dG"], 16)

        def DL(instr, pb):
            ATTACH[f"dL{pb}"] += 16
            return instr.then_inc(sems[f"dL{pb}"], 16)

        def PE(instr):
            ATTACH["pe"] += 1
            return instr.then_inc(sems["pe"], 1)

        def VE(instr):
            ATTACH["ve"] += 1
            return instr.then_inc(sems["ve"], 1)

        def SE(instr):
            ATTACH["se"] += 1
            return instr.then_inc(sems["se"], 1)

        def GE(instr):
            ATTACH["ge"] += 1
            return instr.then_inc(sems["ge"], 1)

        c25 = E(nc.sbuf_tensor("c25", [128, 1], F32))
        cn8 = E(nc.sbuf_tensor("cn8", [128, 1], F32))
        ceps1 = E(nc.sbuf_tensor("ceps1", [128, 1], F32))
        ceps2 = E(nc.sbuf_tensor("ceps2", [128, 1], F32))
        ones_rf = E(nc.sbuf_tensor("ones_rf", [1, 128], F32))
        ones_rb = E(nc.sbuf_tensor("ones_rb", [1, 128], BF16))
        ones_kb = E(nc.sbuf_tensor("ones_kb", [128, 1], BF16))
        dmask = E(nc.sbuf_tensor("dmask", [128, 128], F32))
        lam_sb = E(nc.sbuf_tensor("lam_sb", [1, NH], F32))
        lam_sg = E(nc.sbuf_tensor("lam_sg", [1, NH], F32))
        nlam = E(nc.sbuf_tensor("nlam", [128, NH], F32))
        ofs_sb = E(nc.sbuf_tensor("ofs_sb", [1, 1], I32))
        es0 = contextlib.ExitStack()
        lam_ps = es0.enter_context(nc.psum_tensor("lam_ps", [128, NH], F32))

        def p0_consts(v):
            v.memset(c25[:], 0.25)
            v.memset(cn8[:], -8.0)
            v.memset(ceps1[:], 1e-5)
            v.memset(ceps2[:], 1e-6)
            v.memset(ones_rf[:], 1.0)
            v.memset(ones_rb[:], 1.0)
            VE(v.memset(ones_kb[:], 1.0))
        tk_consts = S.add("vector", p0_consts, incs=[("ve", 1)])

        def p0_mask(g):
            g.memset(dmask[:], 0.0)
            GE(g.affine_select(out=dmask[:], in_=dmask[:], pattern=[[1, 128]],
                               compare_op=OP.is_ge, fill=-1e30, base=0,
                               channel_multiplier=-1))
        tk_mask = S.add("gpsimd", p0_mask, incs=[("ge", 1)])
        tk_ofs = S.add("gpsimd", lambda g: DG(g.dma_start(out=ofs_sb[:], in_=ofs)),
                       incs=[("dG", 16)])
        tk_lam = S.add("gpsimd", lambda g: DG(g.dma_start(out=lam_sb[:], in_=lam)),
                       incs=[("dG", 16)])
        tk_sig = S.add("scalar", lambda s: SE(s.activation(
            out=lam_sg[:], in_=lam_sb[:], func=AF.Sigmoid)),
            deps=[tk_lam], incs=[("se", 1)])
        tk_lmm = S.add("tensor", lambda t: PE(t.matmul(
            lam_ps[:], ones_rf[:], lam_sg[:], start=True, stop=True)),
            deps=[tk_sig, tk_consts], incs=[("pe", 1)])
        tk_nlam = S.add("vector", lambda v: VE(v.tensor_scalar(
            out=nlam[:], in0=lam_ps[:], scalar1=-1.0, scalar2=None, op0=OP.mult)),
            deps=[tk_lmm], incs=[("ve", 1)])
        es0.close()

        dq_in = [E(nc.sbuf_tensor(f"dq_in{i}", [128, 512], U16))
                 for i in range(2)]
        dq_din = E(nc.sbuf_tensor("dq_din", [128, 1408], U16))
        dq_nib = E(nc.sbuf_tensor("dq_nib", [128, 2048], U16))
        dq_sf = E(nc.sbuf_tensor("dq_sf", [128, 2048], U16))
        dq_corr = E(nc.sbuf_tensor("dq_corr", [128, 2048], BF16))
        dq_mag = E(nc.sbuf_tensor("dq_mag", [128, 2048], BF16))
        dq_w = [E(nc.sbuf_tensor(f"dq_w{i}", [128, 2048], BF16))
                for i in range(2)]
        dq_scf = E(nc.sbuf_tensor("dq_scf", [128, 44], F16))
        dq_scb = E(nc.sbuf_tensor("dq_scb", [128, 44], BF16))
        dq_s32 = E(nc.sbuf_tensor("dq_s32", [128, 44], U32))

        tk_dq_store = {"gu": [None] * 88, "dn": [[None] * 4 for _ in range(16)]}
        dqs = {"wbuf": 0, "inbuf": 0, "last_w_store": [None, None],
               "last_v": None, "last_din_user": None, "last_sc_user": None}

        def emit_gu_chunk(ci):
            wb = dqs["wbuf"]; dqs["wbuf"] ^= 1
            ib = dqs["inbuf"]; dqs["inbuf"] ^= 1
            tk_ldp = S.add("gpsimd", lambda g, ci=ci, ib=ib: DG(g.dma_start(
                out=dq_in[ib][:], in_=gu16[ci * 128:(ci + 1) * 128, :])),
                deps=[dqs["last_v"]], incs=[("dG", 16)])
            tk_lds = S.add("gpsimd", lambda g, ci=ci: DG(g.dma_start(
                out=dq_scf[:, 0:16], in_=gus[ci * 128:(ci + 1) * 128, :])),
                deps=[dqs["last_sc_user"]], incs=[("dG", 16)])
            tk_scb = S.add("vector", lambda v: VE(v.tensor_scalar(
                out=dq_scb[:, 0:16], in0=dq_scf[:, 0:16], scalar1=4.0,
                scalar2=None, op0=OP.mult)), deps=[tk_lds], incs=[("ve", 1)])

            def s32f(g):
                lo = dq_s32[:, 0:16].bitcast(U16).rearrange(
                    "p (f two) -> p f two", two=2)[:, :, 0]
                hi = dq_s32[:, 0:16].bitcast(U16).rearrange(
                    "p (f two) -> p f two", two=2)[:, :, 1]
                g.tensor_copy(lo, dq_scb[:, 0:16].bitcast(U16))
                GE(g.tensor_copy(hi, dq_scb[:, 0:16].bitcast(U16)))
            tk_s32 = S.add("gpsimd", s32f, deps=[tk_scb], incs=[("ge", 1)])

            def vex(v, ib=ib):
                u32i = dq_in[ib][:].bitcast(U32)
                for k in range(4):
                    m, sh, left = MAG_MASKS[k]
                    v.tensor_scalar(
                        out=dq_nib[:, k * 512:(k + 1) * 512].bitcast(U32),
                        in0=u32i, scalar1=m | (m << 16), scalar2=sh,
                        op0=OP.bitwise_and,
                        op1=OP.logical_shift_left if left else OP.logical_shift_right)
                for k in range(4):
                    m, sh = SGN_MASKS[k]
                    v.tensor_scalar(
                        out=dq_sf[:, k * 512:(k + 1) * 512].bitcast(U32),
                        in0=u32i, scalar1=m | (m << 16), scalar2=sh,
                        op0=OP.bitwise_and, op1=OP.logical_shift_left)
                VE(v.tensor_scalar(out=dq_nib[:].bitcast(U32),
                                   in0=dq_nib[:].bitcast(U32),
                                   scalar1=0x3E003E00, scalar2=None,
                                   op0=OP.bitwise_or))
            tk_ext = S.add("vector", vex, deps=[tk_ldp], incs=[("ve", 1)])
            tk_cor = tk_ext

            def vfin(v, wb=wb):
                v.tensor_scalar(out=dq_corr[:], in0=dq_nib[:].bitcast(BF16),
                                scalar1=2.0, scalar2=-0.25,
                                op0=OP.mult, op1=OP.add)
                v.tensor_tensor(out=dq_mag[:], in0=dq_corr[:],
                                in1=dq_nib[:].bitcast(BF16), op=OP.min)
                sc4 = bass.AP(dq_s32[:].tensor, dq_s32[:].offset,
                              [dq_s32[:].ap[0], [0, 4], [1, 16], [0, 16]])
                v.tensor_tensor(
                    out=dq_sf[:].bitcast(U32).rearrange("p (a g r) -> p a g r",
                                                        a=4, r=16),
                    in0=dq_sf[:].bitcast(U32).rearrange("p (a g r) -> p a g r",
                                                        a=4, r=16),
                    in1=sc4, op=OP.bitwise_or)
                VE(v.tensor_tensor(out=dq_w[wb][:], in0=dq_mag[:],
                                   in1=dq_sf[:].bitcast(BF16), op=OP.mult))
            tk_fin = S.add("vector", vfin,
                           deps=[tk_cor, tk_s32, dqs["last_w_store"][wb]],
                           incs=[("ve", 1)])
            dqs["last_v"] = tk_fin
            dqs["last_sc_user"] = tk_fin
            tk_stw = S.add("gpsimd", lambda g, ci=ci, wb=wb: DG(g.dma_start(
                out=wgu_d[ci * 128:(ci + 1) * 128, :], in_=dq_w[wb][:])),
                deps=[tk_fin], incs=[("dG", 16)])
            dqs["last_w_store"][wb] = tk_stw
            tk_dq_store["gu"][ci] = tk_stw

        def emit_dn_block(db):
            tk_ldp = S.add("gpsimd", lambda g, db=db: DG(g.dma_start(
                out=dq_din[:], in_=dn16[db * 128:(db + 1) * 128, :])),
                deps=[dqs["last_din_user"]], incs=[("dG", 16)])
            tk_lds = S.add("gpsimd", lambda g, db=db: DG(g.dma_start(
                out=dq_scf[:], in_=dns[db * 128:(db + 1) * 128, :])),
                deps=[dqs["last_sc_user"]], incs=[("dG", 16)])
            tk_scb = S.add("vector", lambda v: VE(v.tensor_scalar(
                out=dq_scb[:], in0=dq_scf[:], scalar1=4.0, scalar2=None,
                op0=OP.mult)), deps=[tk_lds], incs=[("ve", 1)])

            def s32f(g):
                lo = dq_s32[:].bitcast(U16).rearrange(
                    "p (f two) -> p f two", two=2)[:, :, 0]
                hi = dq_s32[:].bitcast(U16).rearrange(
                    "p (f two) -> p f two", two=2)[:, :, 1]
                g.tensor_copy(lo, dq_scb[:].bitcast(U16))
                GE(g.tensor_copy(hi, dq_scb[:].bitcast(U16)))
            tk_s32 = S.add("gpsimd", s32f, deps=[tk_scb], incs=[("ge", 1)])

            last_fin = None
            for kp4 in range(4):
                wb = dqs["wbuf"]; dqs["wbuf"] ^= 1
                m, sh, left = MAG_MASKS[kp4]
                ms, shs = SGN_MASKS[kp4]

                def vex(v, m=m, sh=sh, left=left, ms=ms, shs=shs):
                    u32i = dq_din[:].bitcast(U32)
                    v.tensor_scalar(out=dq_nib[:, 0:1408].bitcast(U32), in0=u32i,
                                    scalar1=m | (m << 16), scalar2=sh,
                                    op0=OP.bitwise_and,
                                    op1=OP.logical_shift_left if left
                                    else OP.logical_shift_right)
                    v.tensor_scalar(out=dq_sf[:, 0:1408].bitcast(U32), in0=u32i,
                                    scalar1=ms | (ms << 16), scalar2=shs,
                                    op0=OP.bitwise_and, op1=OP.logical_shift_left)
                    VE(v.tensor_scalar(out=dq_nib[:, 0:1408].bitcast(U32),
                                       in0=dq_nib[:, 0:1408].bitcast(U32),
                                       scalar1=0x3E003E00, scalar2=None,
                                       op0=OP.bitwise_or))
                tk_ext = S.add("vector", vex, deps=[tk_ldp], incs=[("ve", 1)])
                tk_cor = tk_ext

                def vfin(v, wb=wb):
                    v.tensor_scalar(out=dq_corr[:, 0:1408],
                                    in0=dq_nib[:, 0:1408].bitcast(BF16),
                                    scalar1=2.0, scalar2=-0.25,
                                    op0=OP.mult, op1=OP.add)
                    v.tensor_tensor(out=dq_mag[:, 0:1408],
                                    in0=dq_corr[:, 0:1408],
                                    in1=dq_nib[:, 0:1408].bitcast(BF16),
                                    op=OP.min)
                    sc4 = bass.AP(dq_s32[:].tensor, dq_s32[:].offset,
                                  [dq_s32[:].ap[0], [1, 44], [0, 16]])
                    v.tensor_tensor(
                        out=dq_sf[:, 0:1408].bitcast(U32).rearrange(
                            "p (g r) -> p g r", g=44),
                        in0=dq_sf[:, 0:1408].bitcast(U32).rearrange(
                            "p (g r) -> p g r", g=44),
                        in1=sc4, op=OP.bitwise_or)
                    VE(v.tensor_tensor(out=dq_w[wb][:, 0:1408],
                                       in0=dq_mag[:, 0:1408],
                                       in1=dq_sf[:, 0:1408].bitcast(BF16),
                                       op=OP.mult))
                tk_fin = S.add("vector", vfin,
                               deps=[tk_cor, tk_s32, dqs["last_w_store"][wb]],
                               incs=[("ve", 1)])
                dqs["last_v"] = tk_fin
                dqs["last_sc_user"] = tk_fin
                last_fin = tk_fin
                tk_stw = S.add("gpsimd", lambda g, db=db, kp4=kp4, wb=wb: DG(
                    g.dma_start(out=wdn_d[db * 128:(db + 1) * 128,
                                          kp4 * 1408:(kp4 + 1) * 1408],
                                in_=dq_w[wb][:, 0:1408])),
                    deps=[tk_fin], incs=[("dG", 16)])
                dqs["last_w_store"][wb] = tk_stw
                tk_dq_store["dn"][db][kp4] = tk_stw
            dqs["last_din_user"] = last_fin

        dq_units = [lambda ci=ci: emit_gu_chunk(ci) for ci in range(88)]
        dq_units += [lambda db=db: emit_dn_block(db) for db in range(16)]
        dq_iter = iter(dq_units)
        n_dq_per_slot = (len(dq_units) + 63) // 64

        with contextlib.ExitStack() as es1:
            E1 = es1.enter_context
            xf = [E1(nc.sbuf_tensor(f"xf{i}", [128, D], F32)) for i in range(2)]
            xnb = [E1(nc.sbuf_tensor(f"xnb{i}", [128, D], BF16)) for i in range(2)]
            negm = [E1(nc.sbuf_tensor(f"negm{i}", [128, 1], F32)) for i in range(2)]
            rstd1 = [E1(nc.sbuf_tensor(f"rstd1{i}", [128, 1], F32)) for i in range(2)]
            ssq1 = [E1(nc.sbuf_tensor(f"ssq1{i}", [128, 1], F32)) for i in range(2)]
            sxs = [E1(nc.sbuf_tensor(f"sxs{i}", [128, 1], F32)) for i in range(2)]
            vart = [E1(nc.sbuf_tensor(f"vart{i}", [128, 1], F32)) for i in range(2)]
            sqd = E1(nc.sbuf_tensor("sqd", [128, D], BF16))

            regs = {}

            def p1_reg(g):
                regs["rt"] = g.alloc_register("rt")
                g.reg_load(regs["rt"], ofs_sb[:1, :1])
            S.add("gpsimd", p1_reg, deps=[tk_ofs])

            tk_xn = [None] * NT
            tk_st = [None] * NT
            for t in range(NT):
                b = t % 2
                tk_ld = S.add("sync", lambda sy, t=t, b=b: DS(sy.dma_start(
                    out=xf[b][:], in_=x[t * 128:(t + 1) * 128, :])),
                    deps=[tk_xn[t - 2] if t >= 2 else None], incs=[("dS", 16)])

                tk_red = S.add("vector", lambda v, b=b: VE(
                    v.reduce_sum(sxs[b][:], xf[b][:], axis=AX)),
                    deps=[tk_ld], incs=[("ve", 1)])
                tk_sum = S.add("vector", lambda v, b=b: VE(
                    v.tensor_scalar(out=negm[b][:], in0=sxs[b][:],
                                    scalar1=-1.0 / D, scalar2=None, op0=OP.mult)),
                    deps=[tk_red], incs=[("ve", 1)])

                def p1_sq(s, b=b):
                    s.memzero(ssq1[b][:])
                    SE(s.activation(out=sqd[:], in_=xf[b][:], func=AF.Square,
                                    accum_out=ssq1[b][:]))
                tk_sq = S.add("scalar", p1_sq, deps=[tk_ld], incs=[("se", 1)])

                tk_va = S.add("vector", lambda v, b=b: VE(
                    v.tensor_scalar(out=vart[b][:], in0=ssq1[b][:],
                                    scalar1=1.0 / D, scalar2=None, op0=OP.mult)),
                    deps=[tk_sq], incs=[("ve", 1)])
                tk_vb = S.add("vector", lambda v, b=b: VE(
                    v.tensor_tensor(out=sxs[b][:], in0=negm[b][:],
                                    in1=negm[b][:], op=OP.mult)),
                    deps=[tk_sum], incs=[("ve", 1)])
                tk_var = S.add("vector", lambda v, b=b: VE(
                    v.scalar_tensor_tensor(out=vart[b][:], in0=sxs[b][:],
                                           scalar=-1.0, in1=vart[b][:],
                                           op0=OP.mult, op1=OP.add)),
                    deps=[tk_va, tk_vb], incs=[("ve", 1)])

                tk_ln1 = S.add("scalar", lambda s, b=b: SE(
                    s.activation(out=ssq1[b][:], in_=vart[b][:], func=AF.Ln,
                                 bias=ceps1[:], scale=1.0)),
                    deps=[tk_var], incs=[("se", 1)])
                tk_rstd = S.add("scalar", lambda s, b=b: SE(
                    s.activation(out=rstd1[b][:], in_=ssq1[b][:], func=AF.Exp,
                                 scale=-0.5)),
                    deps=[tk_ln1], incs=[("se", 1)])

                def p1_xn(v, b=b):
                    xo = bass.AP(xnb[b][:].tensor, xnb[b][:].offset,
                                 [xnb[b][:].ap[0], [1, 512], [512, 4]])
                    xi = bass.AP(xf[b][:].tensor, xf[b][:].offset,
                                 [xf[b][:].ap[0], [4, 512], [1, 4]])
                    VE(v.tensor_scalar(out=xo, in0=xi, scalar1=negm[b][:],
                                       scalar2=rstd1[b][:], op0=OP.add, op1=OP.mult))
                tk_xn[t] = S.add("vector", p1_xn,
                                 deps=[tk_rstd, tk_st[t - 2] if t >= 2 else None],
                                 incs=[("ve", 1)])

                def p1_st(g, t=t, b=b):
                    DG(g.dma_start(out=xn_d[t * 128:(t + 1) * 128, :], in_=xnb[b][:]))
                tk_st[t] = S.add("gpsimd", p1_st, deps=[tk_xn[t]],
                                 incs=[("dG", 16)])
                u = next(dq_iter, None)
                if u is not None:
                    u()

            def p1_xn2(g):
                dyn = bass.AP(xn2_d.tensor, regs["rt"], [[D, T], [1, D]])
                DG(g.dma_start(out=dyn, in_=xn_d))
            tk_p1_done = S.add("gpsimd", p1_xn2, deps=[tk_st[NT - 1]],
                               incs=[("dG", 16)])
            S.flush(block, sems, "P1")

        with contextlib.ExitStack() as es2:
            E2 = es2.enter_context
            wkb = E2(nc.sbuf_tensor("wkb", [128, 16 * DK], BF16))
            wvb = E2(nc.sbuf_tensor("wvb", [128, 16 * DK], BF16))
            wqb = E2(nc.sbuf_tensor("wqb", [128, 16 * DK], BF16))
            tk_w = {}
            for nm, src, dst in [("k", wk, wkb), ("v", wv, wvb), ("q", wq, wqb)]:
                tk_w[nm] = S.add("gpsimd", lambda g, src=src, dst=dst: DG(
                    g.dma_start(
                        out=dst[:].rearrange("p (a n) -> p a n", a=16),
                        in_=src.rearrange("(a p) n -> p a n", p=128))),
                    incs=[("dG", 16)])
            xnt = [E2(nc.sbuf_tensor(f"xnt{i}", [128, 16 * 512], BF16))
                   for i in range(2)]
            evb = {k: [E2(nc.sbuf_tensor(f"{k}ev{i}", [128, 512], BF16))
                       for i in range(2)] for k in "kqv"}
            pp = {k: [E2(nc.psum_tensor(f"{k}p{i}", [128, 512], F32))
                      for i in range(2)] for k in "kqv"}

            tk_chunk = [None] * 8
            tk_ev = {}
            tk_stc = {}
            for c in range(8):
                cb = c % 2

                def p2_ldx(sy, c=c, cb=cb):
                    for dt in range(16):
                        DS(sy.dma_start(
                            out=xnt[cb][:, dt * 512:(dt + 1) * 512],
                            in_=xn_d[c * 512:(c + 1) * 512, dt * 128:(dt + 1) * 128],
                            transpose=True))
                tk_ldx = S.add("sync", p2_ldx,
                               deps=[tk_st[4 * c + 3],
                                     tk_chunk[c - 2] if c >= 2 else None],
                               incs=[("dS", 256)])

                for kind, wsb, dst in [("k", wkb, k_d), ("q", wqb, q_d),
                                       ("v", wvb, v_d)]:
                    for j in range(8):
                        jb = j % 2
                        key = (kind, c, j)
                        prev = (kind, c, j - 2) if j >= 2 else (kind, c - 1, 6 + j)
                        dep_bank = tk_ev.get(prev)

                        def mms(t, j=j, jb=jb, wsb=wsb, kind=kind, cb=cb):
                            ps = pp[kind][jb]
                            for dt in range(16):
                                if kind in ("k", "q"):
                                    lhs = wsb[:, dt * DK + j * 128:
                                              dt * DK + (j + 1) * 128]
                                    rhs = xnt[cb][:, dt * 512:(dt + 1) * 512]
                                else:
                                    ts4, dc = divmod(j, 2)
                                    lhs = xnt[cb][:, dt * 512 + ts4 * 128:
                                                  dt * 512 + (ts4 + 1) * 128]
                                    rhs = wsb[:, dt * DK + dc * 512:
                                              dt * DK + (dc + 1) * 512]
                                ins = t.matmul(ps[:], lhs, rhs,
                                               start=(dt == 0), stop=(dt == 15))
                            PE(ins)
                        tk_mm = S.add("tensor", mms,
                                      deps=[tk_ldx, dep_bank, tk_w[kind]],
                                      incs=[("pe", 1)])
                        dep_stb = tk_stc.get(prev)

                        def ev(se_, jb=jb, kind=kind):
                            SE(se_.copy(evb[kind][jb][:], pp[kind][jb][:]))
                        tk_ev[key] = S.add("scalar", ev, deps=[tk_mm, dep_stb],
                                           incs=[("se", 1)])

                        def st(s2, j=j, jb=jb, kind=kind, c=c, dst=dst):
                            if kind in ("k", "q"):
                                d = dst[j * 128:(j + 1) * 128, c * 512:(c + 1) * 512]
                            else:
                                ts4, dc = divmod(j, 2)
                                d = dst[c * 512 + ts4 * 128:
                                        c * 512 + (ts4 + 1) * 128,
                                        dc * 512:(dc + 1) * 512]
                            DA(s2.dma_start(out=d, in_=evb[kind][jb][:]))
                        tk_stc[key] = S.add("scalar", st, deps=[tk_ev[key]],
                                            incs=[("dA", 16)])
                        u = next(dq_iter, None)
                        if u is not None:
                            u()
                tk_chunk[c] = tk_ev[("v", c, 7)]
            for u in dq_iter:
                u()
            S.flush(block, sems, "P2")
            p2_gate = S.snap()

        with contextlib.ExitStack() as es4:
            E4 = es4.enter_context
            k_sb = E4(nc.sbuf_tensor("k_sb", [128, NH * T], BF16))
            v_sb = E4(nc.sbuf_tensor("v_sb", [128, NT * DK], BF16))
            qm_sb = [E4(nc.sbuf_tensor(f"qm_sb{i}", [128, NH * 512], BF16))
                     for i in range(2)]
            p12 = [E4(nc.sbuf_tensor(f"p12_{i}", [128, 1024], BF16))
                   for i in range(2)]
            oh_sb = E4(nc.sbuf_tensor("oh_sb", [128, NH * 512], BF16))
            rl_f = E4(nc.sbuf_tensor("rl_f", [1, 1024], F32))
            rl_b = E4(nc.sbuf_tensor("rl_b", [1, 1024], BF16))
            rlb1 = E4(nc.sbuf_tensor("rlb1", [128, 512], BF16))
            rlb2 = E4(nc.sbuf_tensor("rlb2", [128, 512], BF16))
            tcmb = [E4(nc.sbuf_tensor(f"tcmb{i}", [128, 512], BF16))
                    for i in range(2)]
            s_ps = [E4(nc.psum_tensor(f"s_ps{i}", [128, 1024], F32))
                    for i in range(2)]
            o_ps = [E4(nc.psum_tensor(f"o_ps{i}", [128, 512], F32))
                    for i in range(2)]
            l_ps = E4(nc.psum_tensor("l_ps", [128, 1024], F32))

            def p4_ldk(sy):
                for h in range(NH):
                    DS(sy.dma_start(out=k_sb[:, h * T:(h + 1) * T],
                                    in_=k_d[h * 128:(h + 1) * 128, :]))
            tk_ldk = S.add("sync", p4_ldk, deps=[p2_gate], incs=[("dS", 128)])

            def p4_ldv(sy):
                for tt in range(NT):
                    DS(sy.dma_start(out=v_sb[:, tt * DK:(tt + 1) * DK],
                                    in_=v_d[tt * 128:(tt + 1) * 128, :]))
            tk_ldv = S.add("sync", p4_ldv, deps=[p2_gate], incs=[("dS", 512)])

            tk_ldq = [None] * 8
            tk_qm_done = [None] * 8

            def emit_ldq(qm):
                def f(sy, qm=qm):
                    src = bass.AP(q_d.tensor, qm * 512,
                                  [[T, 128], [128 * T, NH], [1, 512]])
                    DS(sy.dma_start(out=qm_sb[qm % 2][:], in_=src))
                tk_ldq[qm] = S.add(
                    "sync", f,
                    deps=[p2_gate, tk_qm_done[qm - 2] if qm >= 2 else None],
                    incs=[("dS", 16)])

            emit_ldq(0)
            emit_ldq(1)

            tk_exp = {}
            tk_avl = {}
            tk_cmb_prev = None
            tk_bce_prev = None
            tk_bc_prev = None
            tk_oh_dma = {}
            for qm in range(8):
                n_kt = 4 * qm + 4
                for hi in range(NH):
                    def emit_skt(kt, hi=hi, qm=qm):
                        par = kt % 2
                        qoff = max(0, (kt - 4 * qm) * 128)

                        def smm(t, hi=hi, kt=kt, qm=qm, par=par, qoff=qoff):
                            l1 = k_sb[0:64, hi * T + kt * 128:
                                      hi * T + (kt + 1) * 128]
                            r1 = qm_sb[qm % 2][0:64, hi * 512 + qoff:
                                               (hi + 1) * 512]
                            t.matmul(s_ps[par][:, qoff:512], l1, r1,
                                     start=True, stop=True)
                            l2 = k_sb[64:128, hi * T + kt * 128:
                                      hi * T + (kt + 1) * 128]
                            r2 = qm_sb[qm % 2][64:128, hi * 512 + qoff:
                                               (hi + 1) * 512]
                            PE(t.matmul(s_ps[par][:, 512 + qoff:1024], l2, r2,
                                        start=True, stop=True))
                        tk_s = S.add("tensor", smm,
                                     deps=[tk_ldq[qm], tk_ldk,
                                           tk_exp.get((qm, hi, kt - 2)),
                                           tk_cmb_prev if kt <= 1 else None,
                                           tk_bce_prev if kt <= 1 else None],
                                     incs=[("pe", 1)])

                        if kt >= 4 * qm:
                            def msk(v, par=par, qoff=qoff):
                                v.tensor_tensor(
                                    out=s_ps[par][:, qoff:qoff + 128],
                                    in0=s_ps[par][:, qoff:qoff + 128],
                                    in1=dmask[:], op=OP.add)
                                VE(v.tensor_tensor(
                                    out=s_ps[par][:, 512 + qoff:512 + qoff + 128],
                                    in0=s_ps[par][:, 512 + qoff:512 + qoff + 128],
                                    in1=dmask[:], op=OP.add))
                            tk_m = S.add("vector", msk, deps=[tk_s, tk_mask],
                                         incs=[("ve", 1)])
                        else:
                            tk_m = tk_s

                        def expf(s, par=par, qoff=qoff, kt=kt):
                            pb = p12[kt % 2]
                            if qoff == 0:
                                SE(s.activation(out=pb[:], in_=s_ps[par][:],
                                                func=AF.Exp, bias=cn8[:],
                                                scale=0.125))
                            else:
                                s.activation(out=pb[:, qoff:512],
                                             in_=s_ps[par][:, qoff:512],
                                             func=AF.Exp, bias=cn8[:],
                                             scale=0.125)
                                SE(s.activation(
                                    out=pb[:, 512 + qoff:1024],
                                    in_=s_ps[par][:, 512 + qoff:1024],
                                    func=AF.Exp, bias=cn8[:], scale=0.125))
                        tk_exp[(qm, hi, kt)] = S.add(
                            "scalar", expf,
                            deps=[tk_m, tk_avl.get((qm, hi, kt - 2))],
                            incs=[("se", 1)])

                    def emit_avkt(kt, hi=hi, qm=qm):
                        qoff = max(0, (kt - 4 * qm) * 128)

                        def avl(t, hi=hi, kt=kt, qoff=qoff, qm=qm):
                            pb = p12[kt % 2]
                            vt = v_sb[:, kt * DK + hi * 128:
                                      kt * DK + (hi + 1) * 128]
                            st0 = (kt == 0)
                            sp0 = (kt == 4 * qm + 3)
                            t.matmul(o_ps[0][:, qoff:512], vt, pb[:, qoff:512],
                                     start=st0, stop=sp0, skip_group_check=True)
                            t.matmul(o_ps[1][:, qoff:512], vt,
                                     pb[:, 512 + qoff:1024],
                                     start=st0, stop=sp0, skip_group_check=True)
                            t.matmul(l_ps[0:1, qoff:512], ones_kb[:],
                                     pb[:, qoff:512],
                                     start=st0, stop=sp0, skip_group_check=True)
                            PE(t.matmul(l_ps[0:1, 512 + qoff:1024], ones_kb[:],
                                        pb[:, 512 + qoff:1024],
                                        start=st0, stop=sp0,
                                        skip_group_check=True))
                        tk_avl[(qm, hi, kt)] = S.add(
                            "tensor", avl,
                            deps=[tk_exp[(qm, hi, kt)], tk_ldv,
                                  tk_cmb_prev if kt <= 1 else None],
                            incs=[("pe", 1)])

                    for kt in range(n_kt + 1):
                        if kt < n_kt:
                            emit_skt(kt)
                        if kt >= 1:
                            emit_avkt(kt - 1)

                    last_av = tk_avl[(qm, hi, n_kt - 1)]

                    tk_rla = S.add("scalar", lambda s: SE(
                        s.activation(out=rl_f[:], in_=l_ps[0:1, :], func=AF.Ln)),
                        deps=[last_av], incs=[("se", 1)])
                    tk_rl = S.add("scalar", lambda s: SE(
                        s.activation(out=rl_f[:], in_=rl_f[:], func=AF.Exp,
                                     scale=-1.0)),
                        deps=[tk_rla], incs=[("se", 1)])

                    tk_rn = S.add("vector", lambda v, hi=hi: VE(
                        v.tensor_scalar(out=rl_f[0:1, 512:1024],
                                        in0=rl_f[0:1, 512:1024],
                                        scalar1=nlam[0:1, hi:hi + 1],
                                        scalar2=None, op0=OP.mult)),
                        deps=[tk_rl, tk_nlam], incs=[("ve", 1)])
                    tk_rlb = S.add("vector", lambda v: VE(
                        v.tensor_copy(rl_b[:], rl_f[:])),
                        deps=[tk_rn, tk_bc_prev], incs=[("ve", 1)])

                    def bc(t):
                        t.matmul(s_ps[0][:, 0:512], ones_rb[:],
                                 rl_b[0:1, 0:512], start=True, stop=True)
                        PE(t.matmul(s_ps[0][:, 512:1024], ones_rb[:],
                                    rl_b[0:1, 512:1024], start=True, stop=True))
                    tk_bc = S.add("tensor", bc,
                                  deps=[tk_rlb, tk_exp[(qm, hi, n_kt - 1)]],
                                  incs=[("pe", 1)])
                    tk_bc_prev = tk_bc

                    def bce(s):
                        s.copy(rlb1[:], s_ps[0][:, 0:512])
                        SE(s.copy(rlb2[:], s_ps[0][:, 512:1024]))
                    tk_bce = S.add("scalar", bce, deps=[tk_bc], incs=[("se", 1)])
                    tk_bce_prev = tk_bce

                    tk_c1 = S.add("vector", lambda v: VE(
                        v.tensor_tensor(out=tcmb[0][:], in0=o_ps[0][:],
                                        in1=rlb1[:], op=OP.mult)),
                        deps=[tk_bce, last_av], incs=[("ve", 1)])
                    tk_c2 = S.add("vector", lambda v: VE(
                        v.tensor_tensor(out=tcmb[1][:], in0=o_ps[1][:],
                                        in1=rlb2[:], op=OP.mult)),
                        deps=[tk_bce, last_av], incs=[("ve", 1)])
                    tk_cmb = S.add("vector", lambda v, hi=hi: VE(
                        v.tensor_tensor(out=oh_sb[:, hi * 512:(hi + 1) * 512],
                                        in0=tcmb[0][:], in1=tcmb[1][:],
                                        op=OP.add)),
                        deps=[tk_c1, tk_c2, tk_oh_dma.get((qm - 1, hi))],
                        incs=[("ve", 1)])
                    tk_cmb_prev = tk_cmb
                    tk_oh_dma[(qm, hi)] = S.add(
                        "scalar", lambda s2, hi=hi, qm=qm: DA(s2.dma_start(
                            out=o_d[hi * 128:(hi + 1) * 128,
                                    qm * 512:(qm + 1) * 512],
                            in_=oh_sb[:, hi * 512:(hi + 1) * 512])),
                        deps=[tk_cmb], incs=[("dA", 16)])
                tk_qm_done[qm] = tk_cmb_prev
                if qm + 2 < 8:
                    emit_ldq(qm + 2)
            S.flush(block, sems, "P4")
            p4_gate = S.snap()

        es56 = contextlib.ExitStack()
        xnt6 = es56.enter_context(nc.sbuf_tensor("xnt6", [128, 16 * TOWN], BF16))
        with contextlib.ExitStack() as es5:
            E5 = es5.enter_context
            wob = E5(nc.sbuf_tensor("wob", [128, NH * D], BF16))
            otl = [E5(nc.sbuf_tensor(f"otl{i}", [128, NH * 128], BF16))
                   for i in range(2)]
            att_sb = [E5(nc.sbuf_tensor(f"att_sb{i}", [128, D], BF16))
                      for i in range(2)]
            ap_ps = [E5(nc.psum_tensor(f"ap_ps{i}", [128, D], F32))
                     for i in range(2)]

            tk_wo = S.add("gpsimd", lambda g: DG(g.dma_start(
                out=wob[:].rearrange("p (a n) -> p a n", a=NH),
                in_=wo.rearrange("(a p) n -> p a n", p=128))),
                deps=[p4_gate], incs=[("dG", 16)])

            tk_p5 = {}
            tk_ev5 = {}
            tk_attst = {}
            for tt in range(NT):
                tb = tt % 2

                def ldo(sy, tt=tt, tb=tb):
                    src = bass.AP(o_d.tensor, tt * 128,
                                  [[T, 128], [128 * T, NH], [1, 128]])
                    DS(sy.dma_start(out=otl[tb][:], in_=src))
                tk_ldo = S.add("sync", ldo, deps=[p4_gate, tk_p5.get(tt - 2)],
                               incs=[("dS", 16)])

                def p5mm(t, tb=tb):
                    for dv in range(NH):
                        for ch in range(4):
                            ins = t.matmul(
                                ap_ps[tb][:, ch * 512:(ch + 1) * 512],
                                otl[tb][:, dv * 128:(dv + 1) * 128],
                                wob[:, dv * D + ch * 512: dv * D + (ch + 1) * 512],
                                start=(dv == 0), stop=(dv == NH - 1))
                    PE(ins)
                tk_p5[tt] = S.add("tensor", p5mm,
                                  deps=[tk_ldo, tk_wo, tk_ev5.get(tt - 2)],
                                  incs=[("pe", 1)])
                tk_ev5[tt] = S.add("vector", lambda v, tb=tb: VE(
                    v.tensor_copy(att_sb[tb][:], ap_ps[tb][:])),
                    deps=[tk_p5[tt], tk_attst.get(tt - 2)], incs=[("ve", 1)])

                def stc(s2, tt=tt, tb=tb):
                    DA(s2.dma_start(out=cc_in[tt * 128:(tt + 1) * 128, :],
                                    in_=att_sb[tb][:]))
                    if dev >= 2:
                        DA(s2.dma_start(
                            out=att_dump[tt * 128:(tt + 1) * 128, :],
                            in_=att_sb[tb][:]))
                tk_attst[tt] = S.add("scalar", stc, deps=[tk_ev5[tt]],
                                     incs=[("dA", 32 if dev >= 2 else 16)])

            def _ccfn(g):
                ATTACH["cc"] += 1
                return g.collective_compute(
                "ReduceScatter", OP.add,
                replica_groups=[[0, 1], [2, 3], [4, 5], [6, 7]],
                    ins=[cc_in.opt()], outs=[cc_out.opt()]).then_inc(sems["cc"], 1)
            tk_cc = S.add("gpsimd", _ccfn,
                          deps=[tk_attst[NT - 1], {"dA": S.c["dA"]}], incs=[("cc", 1)])

            def ldxn6(sy):
                for dt in range(16):
                    DS(sy.dma_start(out=xnt6[:, dt * TOWN:(dt + 1) * TOWN],
                                    in_=xn2_d[TOWN:TOWN + TOWN,
                                              dt * 128:(dt + 1) * 128],
                                    transpose=True))
            tk_xn6 = S.add("sync", ldxn6, deps=[p4_gate, tk_p1_done],
                           incs=[("dS", 256)])
            if dev >= 2:
                S.add("gpsimd", lambda g: DG(g.dma_start(out=cc_dump, in_=cc_out)),
                      deps=[tk_cc], incs=[("dG", 16)])
            S.flush(block, sems, "P5")
            p5_gate = S.snap()

        with contextlib.ExitStack() as es6:
            E6 = es6.enter_context
            wgt = [E6(nc.sbuf_tensor(f"wgt{i}", [128, 16 * 512], BF16))
                   for i in range(2)]
            wup = [E6(nc.sbuf_tensor(f"wup{i}", [128, 16 * 512], BF16))
                   for i in range(2)]
            gsil = E6(nc.sbuf_tensor("gsil", [128, TOWN], BF16))
            h_sb = [E6(nc.sbuf_tensor(f"h_sb{i}", [128, TOWN], BF16))
                    for i in range(2)]
            gA = E6(nc.psum_tensor("gA", [128, TOWN], F32))
            gB = E6(nc.psum_tensor("gB", [128, TOWN], F32))

            tk_h = {}
            tk_hv = {}
            tk_sil = {}
            tk_blk = [None, None]
            for blk in range(11):
                bb = blk % 2

                def ldw(sy, blk=blk, bb=bb):
                    for dt in range(16):
                        DS(sy.dma_start(
                            out=wgt[bb][:, dt * 512:(dt + 1) * 512],
                            in_=wgu_d[blk * 512:(blk + 1) * 512,
                                      dt * 128:(dt + 1) * 128], transpose=True))
                    for dt in range(16):
                        DS(sy.dma_start(
                            out=wup[bb][:, dt * 512:(dt + 1) * 512],
                            in_=wgu_d[II + blk * 512: II + (blk + 1) * 512,
                                      dt * 128:(dt + 1) * 128], transpose=True))
                tk_ldw = S.add("sync", ldw,
                               deps=[tk_dq_store["gu"][4 * blk + 3],
                                     tk_dq_store["gu"][44 + 4 * blk + 3],
                                     tk_blk[bb], p5_gate],
                               incs=[("dS", 512)])

                for pos in range(4):
                    ot = blk * 4 + pos

                    def gmm(t, bb=bb, pos=pos):
                        for dt in range(16):
                            lhs = wgt[bb][:, dt * 512 + pos * 128:
                                          dt * 512 + (pos + 1) * 128]
                            for ch in range(4):
                                ins = t.matmul(
                                    gA[:, ch * 512:(ch + 1) * 512], lhs,
                                    xnt6[:, dt * TOWN + ch * 512:
                                         dt * TOWN + (ch + 1) * 512],
                                    start=(dt == 0), stop=(dt == 15))
                        PE(ins)
                    tk_g = S.add("tensor", gmm,
                                 deps=[tk_ldw, tk_xn6, tk_sil.get(ot - 1)],
                                 incs=[("pe", 1)])

                    def umm(t, bb=bb, pos=pos):
                        for dt in range(16):
                            lhs = wup[bb][:, dt * 512 + pos * 128:
                                          dt * 512 + (pos + 1) * 128]
                            for ch in range(4):
                                ins = t.matmul(
                                    gB[:, ch * 512:(ch + 1) * 512], lhs,
                                    xnt6[:, dt * TOWN + ch * 512:
                                         dt * TOWN + (ch + 1) * 512],
                                    start=(dt == 0), stop=(dt == 15))
                        PE(ins)
                    tk_u = S.add("tensor", umm,
                                 deps=[tk_ldw, tk_xn6, tk_hv.get(ot - 1)],
                                 incs=[("pe", 1)])

                    def sil(s):
                        for ch in range(4):
                            ins = s.activation(
                                out=gsil[:, ch * 512:(ch + 1) * 512],
                                in_=gA[:, ch * 512:(ch + 1) * 512], func=AF.Silu)
                        SE(ins)
                    tk_sil[ot] = S.add("scalar", sil, deps=[tk_g],
                                       incs=[("se", 1)])
                    hb = ot % 2

                    def hv(v, hb=hb):
                        for ch in range(4):
                            ins = v.tensor_tensor(
                                out=h_sb[hb][:, ch * 512:(ch + 1) * 512],
                                in0=gB[:, ch * 512:(ch + 1) * 512],
                                in1=gsil[:, ch * 512:(ch + 1) * 512], op=OP.mult)
                        VE(ins)
                    tk_hv[ot] = S.add("vector", hv,
                                      deps=[tk_u, tk_sil[ot], tk_h.get(ot - 2)],
                                      incs=[("ve", 1)])
                    tk_h[ot] = S.add("scalar", lambda s2, ot=ot, hb=hb: DA(
                        s2.dma_start(out=h_d[ot * 128:(ot + 1) * 128, :],
                                     in_=h_sb[hb][:])),
                        deps=[tk_hv[ot]], incs=[("dA", 16)])
                tk_blk[bb] = tk_h[blk * 4 + 3]
            S.flush(block, sems, "P6")
            p6_gate = S.snap()
        es56.close()

        with contextlib.ExitStack() as es7:
            E7 = es7.enter_context
            wdt = E7(nc.sbuf_tensor("wdt", [128, 44 * 1024], BF16))
            h7 = [E7(nc.sbuf_tensor(f"h7_{i}", [128, 44 * 128], BF16))
                  for i in range(2)]
            y0st = [E7(nc.sbuf_tensor(f"y0st{i}", [128, 1024], F32))
                    for i in range(2)]
            att7 = [E7(nc.sbuf_tensor(f"att7_{i}", [128, D], BF16))
                    for i in range(2)]
            xf7 = [E7(nc.sbuf_tensor(f"xf7_{i}", [128, D], F32)) for i in range(2)]
            y0l = [E7(nc.sbuf_tensor(f"y0l{i}", [128, 1024], F32))
                   for i in range(2)]
            outb = [E7(nc.sbuf_tensor(f"outb{i}", [128, D], F32))
                    for i in range(2)]
            ssqa = E7(nc.sbuf_tensor("ssqa", [128, NTO], F32))
            ssqh = [E7(nc.sbuf_tensor(f"ssqh{i}", [128, 1], F32)) for i in range(2)]
            ssqt = [E7(nc.sbuf_tensor(f"ssqt{i}", [128, 1], F32)) for i in range(2)]
            lnv7 = [E7(nc.sbuf_tensor(f"lnv7_{i}", [128, 1], F32))
                    for i in range(2)]
            rstd7 = [E7(nc.sbuf_tensor(f"rstd7_{i}", [128, 1], F32))
                     for i in range(2)]
            sq7 = E7(nc.sbuf_tensor("sq7", [128, 1024], BF16))
            y_ps = [E7(nc.psum_tensor(f"y_ps{i}", [128, 1024], F32))
                    for i in range(2)]

            tk_z = S.add("scalar", lambda s: SE(s.memzero(ssqa[:])),
                         deps=[p6_gate], incs=[("se", 1)])
            tk_ob0 = S.add("vector", lambda v: (
                v.memset(outb[0][:], 0.0),
                VE(v.memset(outb[1][:], 0.0)))[-1],
                deps=[p6_gate], incs=[("ve", 1)])

            tk_out = {}
            tk_y0 = {}
            tk_pe7 = {}
            for half in range(2):
                def ldwd(sy, half=half):
                    for it in range(44):
                        DS(sy.dma_start(
                            out=wdt[:, it * 1024:(it + 1) * 1024],
                            in_=wdn_d[half * 1024:(half + 1) * 1024,
                                      it * 128:(it + 1) * 128], transpose=True))
                dep_wd = [p6_gate] if half == 0 else [tk_pe7[(0, NTO - 1)]]
                dn_deps = [tk_dq_store["dn"][dbi][3] for dbi in range(16)]
                tk_ldwd = S.add("sync", ldwd, deps=dep_wd + dn_deps,
                                incs=[("dS", 704)])

                order = list(range(NTO))
                for pos, tt in enumerate(order):
                    tb = pos % 2
                    prev2 = order[pos - 2] if pos >= 2 else None

                    def ldh(sy, tt=tt, tb=tb):
                        for ggg in range(11):
                            src = bass.AP(h_d.tensor,
                                          ggg * 4 * 128 * TOWN + tt * 128,
                                          [[TOWN, 128], [128 * TOWN, 4], [1, 128]])
                            DS(sy.dma_start(
                                out=h7[tb][:, ggg * 512:(ggg + 1) * 512], in_=src))
                    if half == 0:
                        dep_h7 = tk_pe7.get((0, prev2)) if prev2 is not None else None
                    else:
                        dep_h7 = (tk_pe7.get((1, prev2)) if prev2 is not None
                                  else tk_pe7.get((0, NTO - 2 + pos)))
                    tk_ldh = S.add("sync", ldh, deps=[p6_gate, dep_h7],
                                   incs=[("dS", 176)])

                    def dmm(t, tb=tb):
                        for it in range(44):
                            for ch in range(2):
                                ins = t.matmul(
                                    y_ps[tb][:, ch * 512:(ch + 1) * 512],
                                    h7[tb][:, it * 128:(it + 1) * 128],
                                    wdt[:, it * 1024 + ch * 512:
                                        it * 1024 + (ch + 1) * 512],
                                    start=(it == 0), stop=(it == 43))
                        PE(ins)
                    if half == 0:
                        dep_bank = tk_y0.get(prev2) if prev2 is not None else None
                    else:
                        dep_bank = (tk_out.get(prev2) if prev2 is not None
                                    else tk_y0.get(NTO - 2 + pos))
                    tk_pe7[(half, tt)] = S.add("tensor", dmm,
                                               deps=[tk_ldh, tk_ldwd, dep_bank],
                                               incs=[("pe", 1)])
                    def sqacc(s, tb=tb):
                        s.memzero(ssqh[tb][:])
                        SE(s.activation(out=sq7[:], in_=y_ps[tb][:],
                                        func=AF.Square, accum_out=ssqh[tb][:]))
                    tk_ssq = S.add("scalar", sqacc,
                                   deps=[tk_pe7[(half, tt)], tk_z],
                                   incs=[("se", 1)])

                    if half == 0:
                        tk_ssqc = S.add("scalar", lambda s, tb=tb, tt=tt: SE(
                            s.copy(ssqa[:, tt:tt + 1], ssqh[tb][:])),
                            deps=[tk_ssq], incs=[("se", 1)])
                        tk_ev7 = S.add("vector", lambda v, tb=tb: VE(
                            v.tensor_copy(y0st[tb][:], y_ps[tb][:])),
                            deps=[tk_pe7[(half, tt)], tk_y0.get(tt - 2), tk_ssq],
                            incs=[("ve", 1)])
                        tk_y0[tt] = S.add("scalar", lambda s2, tt=tt, tb=tb: DA(
                            s2.dma_start(out=y0_d[tt * 128:(tt + 1) * 128, :],
                                         in_=y0st[tb][:])),
                            deps=[tk_ev7], incs=[("dA", 16)])
                    else:
                        tk_tot = S.add("vector", lambda v, tb=tb, tt=tt: VE(
                            v.tensor_tensor(out=ssqt[tb][:],
                                            in0=ssqa[:, tt:tt + 1],
                                            in1=ssqh[tb][:], op=OP.add)),
                            deps=[tk_ssq], incs=[("ve", 1)])
                        tk_l7 = S.add("scalar", lambda s, tb=tb: SE(
                            s.activation(out=lnv7[tb][:], in_=ssqt[tb][:],
                                         func=AF.Ln, bias=ceps2[:],
                                         scale=1.0 / D)),
                            deps=[tk_tot], incs=[("se", 1)])
                        tk_rs7 = S.add("scalar", lambda s, tb=tb: SE(
                            s.activation(out=rstd7[tb][:], in_=lnv7[tb][:],
                                         func=AF.Exp, scale=-0.5)),
                            deps=[tk_l7], incs=[("se", 1)])
                        if dev >= 2:
                            tk_y1e = S.add("vector", lambda v, tb=tb: VE(
                                v.tensor_copy(y0st[tb][:], y_ps[tb][:])),
                                deps=[tk_pe7[(half, tt)], tk_rs7],
                                incs=[("ve", 1)])
                            S.add("scalar", lambda s2, tt=tt, tb=tb: DA(
                                s2.dma_start(out=y1_dump[tt * 128:(tt + 1) * 128, :],
                                             in_=y0st[tb][:])),
                                deps=[tk_y1e], incs=[("dA", 16)])
                        dep_prev_out = (tk_out.get(prev2)
                                        if prev2 is not None else None)
                        pb = tb

                        def ld3(g, tt=tt, tb=tb, pb=pb):
                            DL(g.dma_start(
                                out=att7[tb][:],
                                in_=cc_out[tt * 128:(tt + 1) * 128, :]), pb)
                            DL(g.dma_start(
                                out=xf7[tb][:],
                                in_=xown[tt * 128:(tt + 1) * 128, :]), pb)
                            DL(g.dma_start(
                                out=y0l[tb][:],
                                in_=y0_d[tt * 128:(tt + 1) * 128, :]), pb)
                        tk_ld3 = S.add("gpsimd", ld3,
                                       deps=[tk_cc, tk_y0[tt], dep_prev_out],
                                       incs=[(f"dL{pb}", 48)])
                        tk_lda = tk_ldxo = tk_ldy0 = tk_ld3

                        deps_f = [tk_rs7, tk_lda, tk_ldxo, tk_ldy0,
                                  tk_pe7[(half, tt)], dep_prev_out]
                        tk_f1 = S.add("vector", lambda v, tb=tb: VE(
                            v.scalar_tensor_tensor(
                                out=outb[tb][:, 1024:2048], in0=y_ps[tb][:],
                                scalar=rstd7[tb][:], in1=att7[tb][:, 1024:2048],
                                op0=OP.mult, op1=OP.add)),
                            deps=deps_f, incs=[("ve", 1)])
                        tk_f2 = S.add("vector", lambda v, tb=tb: VE(
                            v.tensor_tensor(out=outb[tb][:, 1024:2048],
                                            in0=outb[tb][:, 1024:2048],
                                            in1=xf7[tb][:, 1024:2048],
                                            op=OP.add)),
                            deps=[tk_f1], incs=[("ve", 1)])
                        tk_f3 = S.add("vector", lambda v, tb=tb: VE(
                            v.scalar_tensor_tensor(
                                out=outb[tb][:, 0:1024], in0=y0l[tb][:],
                                scalar=rstd7[tb][:], in1=att7[tb][:, 0:1024],
                                op0=OP.mult, op1=OP.add)),
                            deps=deps_f, incs=[("ve", 1)])
                        tk_fin = S.add("vector", lambda v, tb=tb: VE(
                            v.tensor_tensor(out=outb[tb][:, 0:1024],
                                            in0=outb[tb][:, 0:1024],
                                            in1=xf7[tb][:, 0:1024], op=OP.add)),
                            deps=[tk_f2, tk_f3], incs=[("ve", 1)])
                        if dev >= 2:
                            def dumps7(s2, tt=tt, tb=tb):
                                DA(s2.dma_start(
                                    out=att7_dump[tt * 128:(tt + 1) * 128, :],
                                    in_=att7[tb][:]))
                                DA(s2.dma_start(
                                    out=y0l_dump[tt * 128:(tt + 1) * 128, :],
                                    in_=y0l[tb][:]))
                                DA(s2.dma_start(
                                    out=rstd_dump2[tt * 128:(tt + 1) * 128, :],
                                    in_=rstd7[tb][:]))
                            S.add("scalar", dumps7,
                                  deps=[tk_fin], incs=[("dA", 48)])
                        tk_out[tt] = S.add("scalar", lambda s2, tt=tt, tb=tb: DA(
                            s2.dma_start(out=out[tt * 128:(tt + 1) * 128, :],
                                         in_=outb[tb][:])),
                            deps=[tk_fin], incs=[("dA", 16)])

            S.add("sync", lambda sy: sy.nop(),
                  deps=[{"dA": S.c["dA"], "dG": S.c["dG"], "dS": S.c["dS"]}])
            S.flush(block, sems, "P7")
    return nc


_NC_CACHE = {}


def _get_nc(dev=0):
    if dev not in _NC_CACHE:
        _NC_CACHE[dev] = build(dev)
    return _NC_CACHE[dev]


def make_in_maps(L0_attn_x, L0_attn_Wq, L0_attn_Wk, L0_attn_Wv, L0_attn_Wo,
                 L0_attn_lambda_init, L0_ffn_gate_up_packed, L0_ffn_gate_up_scales,
                 L0_ffn_down_packed, L0_ffn_down_scales):
    pi_h = _pi4(D)
    pi_i = _pi4(II)
    asc = np.ascontiguousarray

    xq = np.asarray(L0_attn_x)
    wq_p = asc(np.asarray(L0_attn_Wq)[pi_h])
    wk_p = asc(np.asarray(L0_attn_Wk)[pi_h])
    wv_p = asc(np.asarray(L0_attn_Wv)[pi_h])
    wo_a = np.asarray(L0_attn_Wo)
    lam_a = np.asarray(L0_attn_lambda_init)
    gu_rows = np.concatenate([pi_i, II + pi_i])
    gu16 = asc(np.asarray(L0_ffn_gate_up_packed).view(np.uint16)[gu_rows])
    gus_t = asc(np.asarray(L0_ffn_gate_up_scales).T[gu_rows])
    dn16 = asc(np.asarray(L0_ffn_down_packed).view(np.uint16))
    dns_t = asc(np.asarray(L0_ffn_down_scales).T)

    in_maps = []
    for c in range(8):
        b, hg = divmod(c, 2)
        R = hg * TOWN
        in_maps.append({
            "x": asc(xq[b]),
            "xown": asc(xq[b][R:R + TOWN]),
            "wq": asc(wq_p[:, hg * DK:(hg + 1) * DK]),
            "wk": asc(wk_p[:, hg * DK:(hg + 1) * DK]),
            "wv": asc(wv_p[:, hg * DK:(hg + 1) * DK]),
            "wo": asc(wo_a[hg * DK:(hg + 1) * DK, :]),
            "lam": asc(lam_a[hg * NH:(hg + 1) * NH].reshape(1, NH)),
            "gu16": gu16,
            "dn16": dn16,
            "gus": gus_t,
            "dns": dns_t,
            "ofs": np.array([[(TOWN - R) * D]], dtype=np.int32),
        })
    return in_maps


def kernel(L0_attn_x, L0_attn_Wq, L0_attn_Wk, L0_attn_Wv, L0_attn_Wo,
           L0_attn_lambda_init, L0_ffn_gate_up_packed, L0_ffn_gate_up_scales,
           L0_ffn_down_packed, L0_ffn_down_scales):
    in_maps = make_in_maps(L0_attn_x, L0_attn_Wq, L0_attn_Wk, L0_attn_Wv,
                           L0_attn_Wo, L0_attn_lambda_init,
                           L0_ffn_gate_up_packed, L0_ffn_gate_up_scales,
                           L0_ffn_down_packed, L0_ffn_down_scales)
    nc = _get_nc(0)
    res = run_bass_kernel_spmd(nc, in_maps, core_ids=list(range(8)))
    outp = np.empty((4, T, D), dtype=np.float32)
    for c in range(8):
        b, hg = divmod(c, 2)
        R = hg * TOWN
        outp[b, R:R + TOWN, :] = res.results[c]["out"]
    return outp


# version 10 (final)
# speedup vs baseline: 1.1368x; 1.0014x over previous
import contextlib
import numpy as np
import ml_dtypes
import concourse.bass as bass
import concourse.mybir as mybir
from concourse.bass_utils import run_bass_kernel_spmd
from concourse.alu_op_type import AluOpType as OP

F32 = mybir.dt.float32
BF16 = mybir.dt.bfloat16
F16 = mybir.dt.float16
U16 = mybir.dt.uint16
U32 = mybir.dt.uint32
I32 = mybir.dt.int32
AF = mybir.ActivationFunctionType
AX = mybir.AxisListType.X

T = 4096
D = 2048
DK = 1024
NH = 8
II = 5632
I2 = 2 * II
TOWN = 2048
NT = T // 128
NTO = TOWN // 128

MAG_MASKS = [(0x0007, 6, True), (0x0070, 2, True), (0x0700, 2, False), (0x7000, 6, False)]
SGN_MASKS = [(0x0008, 12), (0x0080, 8), (0x0800, 4), (0x8000, 0)]

ENGS = ["sync", "vector", "scalar", "tensor", "gpsimd"]
OWNER = {"pe": "tensor", "ve": "vector", "se": "scalar", "ge": "gpsimd"}


def _pi4(n):
    q = n // 4
    j = np.arange(n)
    return 4 * (j % q) + j // q


class Sched:

    def __init__(self):
        self.ops = {e: [] for e in ENGS}
        self.waited = {}
        self.c = {s: 0 for s in ["pe", "ve", "se", "ge", "dS", "dA", "dG",
                                 "dL0", "dL1", "dK", "dQ0", "dQ1", "dW0", "dW1", "cc"]}

    def add(self, eng, fn, deps=(), incs=()):
        waits = {}
        for d in deps:
            if d is None:
                continue
            items = d.items() if isinstance(d, dict) else [d]
            for s, v in items:
                waits[s] = max(waits.get(s, 0), v)
        tok = {}
        for s, n in incs:
            self.c[s] += n
            tok[s] = self.c[s]
        self.ops[eng].append((waits, fn, dict(incs)))
        return tok

    def snap(self):
        return dict(self.c)

    def flush(self, block, sems, name=None):
        import contextlib as _cl
        scope = (block.bass.named_scope(name) if name is not None
                 else _cl.nullcontext())
        with scope:
            self._flush_inner(block, sems)

    def _flush_inner(self, block, sems):
        for ename in ENGS:
            oplist = self.ops[ename]
            if not oplist:
                continue

            def body(eng, oplist=oplist, ename=ename):
                waited = self.waited.setdefault(ename, {})
                for waits, fn, dec in oplist:
                    for s in sorted(waits):
                        v = waits[s]
                        if waited.get(s, -1) >= v:
                            continue
                        eng.wait_ge(sems[s], v)
                        waited[s] = v
                    before = dict(ATTACH)
                    fn(eng)
                    for sname, n in dec.items():
                        got = ATTACH[sname] - before[sname]
                        if got != n:
                            import traceback
                            print(f"*** INC MISMATCH {ename} {sname}: declared {n} attached {got} fn={fn}")
                    for sname in ATTACH:
                        if sname not in dec and ATTACH[sname] != before[sname]:
                            print(f"*** UNDECLARED ATTACH {ename} {sname} fn={fn}")
            getattr(block, ename)(body)
        self.ops = {e: [] for e in ENGS}


ATTACH = {s: 0 for s in ["pe", "ve", "se", "ge", "dS", "dA", "dG",
                         "dL0", "dL1", "dK", "dQ0", "dQ1", "dW0", "dW1", "cc"]}


def build(dev=0):
    for k in ATTACH:
        ATTACH[k] = 0
    nc = bass.Bass("TRN2", target_bir_lowering=False)

    x = nc.dram_tensor("x", [T, D], F32, kind="ExternalInput").ap()
    xown = nc.dram_tensor("xown", [TOWN, D], F32, kind="ExternalInput").ap()
    wq = nc.dram_tensor("wq", [D, DK], F32, kind="ExternalInput").ap()
    wk = nc.dram_tensor("wk", [D, DK], F32, kind="ExternalInput").ap()
    wv = nc.dram_tensor("wv", [D, DK], F32, kind="ExternalInput").ap()
    wo = nc.dram_tensor("wo", [DK, D], F32, kind="ExternalInput").ap()
    lam = nc.dram_tensor("lam", [1, NH], F32, kind="ExternalInput").ap()
    gu16 = nc.dram_tensor("gu16", [I2, D // 4], U16, kind="ExternalInput").ap()
    dn16 = nc.dram_tensor("dn16", [D, II // 4], U16, kind="ExternalInput").ap()
    gus = nc.dram_tensor("gus", [I2, 16], F16, kind="ExternalInput").ap()
    dns = nc.dram_tensor("dns", [D, 44], F16, kind="ExternalInput").ap()
    ofs = nc.dram_tensor("ofs", [1, 1], I32, kind="ExternalInput").ap()
    out = nc.dram_tensor("out", [TOWN, D], F32, kind="ExternalOutput").ap()

    def scratch(name, shape, dt):
        kind = "ExternalOutput" if dev >= 2 else "Internal"
        return nc.dram_tensor(name, shape, dt, kind=kind).ap()

    xn_d = scratch("xn_d", [T, D], BF16)
    xn2_d = scratch("xn2_d", [T + TOWN, D], BF16)
    k_d = scratch("k_d", [DK, T], BF16)
    v_d = scratch("v_d", [T, DK], BF16)
    q_d = scratch("q_d", [DK, T], BF16)
    o_d = scratch("o_d", [DK, T], BF16)
    wgu_d = scratch("wgu_d", [I2, D], BF16)
    wdn_d = scratch("wdn_d", [D, II], BF16)
    h_d = scratch("h_d", [II, TOWN], BF16)
    y0_d = scratch("y0_d", [TOWN, 1024], F32)
    cc_in = nc.dram_tensor("cc_in", [T, D], BF16, kind="Internal").ap()
    cc_out = nc.dram_tensor("cc_out", [TOWN, D], BF16, kind="Internal").ap()
    cc2_in = nc.dram_tensor("cc2_in", [128, 128], F32, kind="Internal").ap()
    cc2_out = nc.dram_tensor("cc2_out", [128, 128], F32, kind="Internal").ap()
    att_dump = (nc.dram_tensor("att_dump", [T, D], BF16, kind="ExternalOutput").ap()
                if dev >= 2 else None)
    cc_dump = (nc.dram_tensor("cc_dump", [TOWN, D], BF16, kind="ExternalOutput").ap()
               if dev >= 2 else None)
    rstd_dump = (nc.dram_tensor("rstd_dump", [TOWN, 1], F32, kind="ExternalOutput").ap()
                 if dev >= 2 else None)
    y1_dump = (nc.dram_tensor("y1_dump", [TOWN, 1024], F32, kind="ExternalOutput").ap()
               if dev >= 2 else None)
    att7_dump = (nc.dram_tensor("att7_dump", [TOWN, D], BF16, kind="ExternalOutput").ap()
                 if dev >= 2 else None)
    y0l_dump = (nc.dram_tensor("y0l_dump", [TOWN, 1024], F32, kind="ExternalOutput").ap()
                if dev >= 2 else None)
    rstd_dump2 = (nc.dram_tensor("rstd_dump2", [TOWN, 1], F32, kind="ExternalOutput").ap()
                  if dev >= 2 else None)

    S = Sched()
    es = contextlib.ExitStack()
    with es:
        E = es.enter_context
        sems = {n: E(nc.semaphore(n)) for n in
                ["pe", "ve", "se", "ge", "dS", "dA", "dG", "dL0", "dL1",
                 "dK", "dQ0", "dQ1", "dW0", "dW1", "cc"]}
        block = E(nc.Block())

        def DS(instr):
            ATTACH["dS"] += 16
            return instr.then_inc(sems["dS"], 16)

        def DA(instr):
            ATTACH["dA"] += 16
            return instr.then_inc(sems["dA"], 16)

        def DG(instr):
            ATTACH["dG"] += 16
            return instr.then_inc(sems["dG"], 16)

        def DL(instr, pb):
            ATTACH[f"dL{pb}"] += 16
            return instr.then_inc(sems[f"dL{pb}"], 16)

        def DX(instr, nm):
            ATTACH[nm] += 16
            return instr.then_inc(sems[nm], 16)

        def PE(instr):
            ATTACH["pe"] += 1
            return instr.then_inc(sems["pe"], 1)

        def VE(instr):
            ATTACH["ve"] += 1
            return instr.then_inc(sems["ve"], 1)

        def SE(instr):
            ATTACH["se"] += 1
            return instr.then_inc(sems["se"], 1)

        def GE(instr):
            ATTACH["ge"] += 1
            return instr.then_inc(sems["ge"], 1)

        c25 = E(nc.sbuf_tensor("c25", [128, 1], F32))
        cn8 = E(nc.sbuf_tensor("cn8", [128, 1], F32))
        ceps1 = E(nc.sbuf_tensor("ceps1", [128, 1], F32))
        ceps2 = E(nc.sbuf_tensor("ceps2", [128, 1], F32))
        ones_rf = E(nc.sbuf_tensor("ones_rf", [1, 128], F32))
        ones_rb = E(nc.sbuf_tensor("ones_rb", [1, 128], BF16))
        ones_kb = E(nc.sbuf_tensor("ones_kb", [128, 1], BF16))
        dmask = E(nc.sbuf_tensor("dmask", [128, 128], F32))
        lam_sb = E(nc.sbuf_tensor("lam_sb", [1, NH], F32))
        lam_sg = E(nc.sbuf_tensor("lam_sg", [1, NH], F32))
        nlam = E(nc.sbuf_tensor("nlam", [128, NH], F32))
        ofs_sb = E(nc.sbuf_tensor("ofs_sb", [1, 1], I32))
        es0 = contextlib.ExitStack()
        lam_ps = es0.enter_context(nc.psum_tensor("lam_ps", [128, NH], F32))

        def p0_consts(v):
            v.memset(c25[:], 0.25)
            v.memset(cn8[:], -8.0)
            v.memset(ceps1[:], 1e-5)
            v.memset(ceps2[:], 1e-6)
            v.memset(ones_rf[:], 1.0)
            v.memset(ones_rb[:], 1.0)
            VE(v.memset(ones_kb[:], 1.0))
        tk_consts = S.add("vector", p0_consts, incs=[("ve", 1)])

        def p0_mask(g):
            g.memset(dmask[:], 0.0)
            GE(g.affine_select(out=dmask[:], in_=dmask[:], pattern=[[1, 128]],
                               compare_op=OP.is_ge, fill=-1e30, base=0,
                               channel_multiplier=-1))
        tk_mask = S.add("gpsimd", p0_mask, incs=[("ge", 1)])
        tk_ofs = S.add("gpsimd", lambda g: DG(g.dma_start(out=ofs_sb[:], in_=ofs)),
                       incs=[("dG", 16)])
        tk_lam = S.add("gpsimd", lambda g: DG(g.dma_start(out=lam_sb[:], in_=lam)),
                       incs=[("dG", 16)])
        tk_sig = S.add("scalar", lambda s: SE(s.activation(
            out=lam_sg[:], in_=lam_sb[:], func=AF.Sigmoid)),
            deps=[tk_lam], incs=[("se", 1)])
        tk_lmm = S.add("tensor", lambda t: PE(t.matmul(
            lam_ps[:], ones_rf[:], lam_sg[:], start=True, stop=True)),
            deps=[tk_sig, tk_consts], incs=[("pe", 1)])
        tk_nlam = S.add("vector", lambda v: VE(v.tensor_scalar(
            out=nlam[:], in0=lam_ps[:], scalar1=-1.0, scalar2=None, op0=OP.mult)),
            deps=[tk_lmm], incs=[("ve", 1)])
        es0.close()

        dq_in = [E(nc.sbuf_tensor(f"dq_in{i}", [128, 512], U16))
                 for i in range(2)]
        dq_din = E(nc.sbuf_tensor("dq_din", [128, 1408], U16))
        dq_nib = E(nc.sbuf_tensor("dq_nib", [128, 2048], U16))
        dq_sf = E(nc.sbuf_tensor("dq_sf", [128, 2048], U16))
        dq_corr = E(nc.sbuf_tensor("dq_corr", [128, 2048], BF16))
        dq_mag = E(nc.sbuf_tensor("dq_mag", [128, 2048], BF16))
        dq_w = [E(nc.sbuf_tensor(f"dq_w{i}", [128, 2048], BF16))
                for i in range(2)]
        dq_scf = E(nc.sbuf_tensor("dq_scf", [128, 44], F16))
        dq_scb = E(nc.sbuf_tensor("dq_scb", [128, 44], BF16))
        dq_s32 = E(nc.sbuf_tensor("dq_s32", [128, 44], U32))

        tk_dq_store = {"gu": [None] * 88, "dn": [[None] * 4 for _ in range(16)]}
        dqs = {"wbuf": 0, "inbuf": 0, "last_w_store": [None, None],
               "last_v": None, "last_din_user": None, "last_sc_user": None}

        def emit_gu_chunk(ci):
            wb = dqs["wbuf"]; dqs["wbuf"] ^= 1
            ib = dqs["inbuf"]; dqs["inbuf"] ^= 1
            tk_ldp = S.add("gpsimd", lambda g, ci=ci, ib=ib: DG(g.dma_start(
                out=dq_in[ib][:], in_=gu16[ci * 128:(ci + 1) * 128, :])),
                deps=[dqs["last_v"]], incs=[("dG", 16)])
            tk_lds = S.add("gpsimd", lambda g, ci=ci: DG(g.dma_start(
                out=dq_scf[:, 0:16], in_=gus[ci * 128:(ci + 1) * 128, :])),
                deps=[dqs["last_sc_user"]], incs=[("dG", 16)])
            tk_scb = S.add("vector", lambda v: VE(v.tensor_scalar(
                out=dq_scb[:, 0:16], in0=dq_scf[:, 0:16], scalar1=4.0,
                scalar2=None, op0=OP.mult)), deps=[tk_lds], incs=[("ve", 1)])

            def s32f(g):
                lo = dq_s32[:, 0:16].bitcast(U16).rearrange(
                    "p (f two) -> p f two", two=2)[:, :, 0]
                hi = dq_s32[:, 0:16].bitcast(U16).rearrange(
                    "p (f two) -> p f two", two=2)[:, :, 1]
                g.tensor_copy(lo, dq_scb[:, 0:16].bitcast(U16))
                GE(g.tensor_copy(hi, dq_scb[:, 0:16].bitcast(U16)))
            tk_s32 = S.add("gpsimd", s32f, deps=[tk_scb], incs=[("ge", 1)])

            def vex(v, ib=ib):
                u32i = dq_in[ib][:].bitcast(U32)
                for k in range(4):
                    m, sh, left = MAG_MASKS[k]
                    v.tensor_scalar(
                        out=dq_nib[:, k * 512:(k + 1) * 512].bitcast(U32),
                        in0=u32i, scalar1=m | (m << 16), scalar2=sh,
                        op0=OP.bitwise_and,
                        op1=OP.logical_shift_left if left else OP.logical_shift_right)
                for k in range(4):
                    m, sh = SGN_MASKS[k]
                    v.tensor_scalar(
                        out=dq_sf[:, k * 512:(k + 1) * 512].bitcast(U32),
                        in0=u32i, scalar1=m | (m << 16), scalar2=sh,
                        op0=OP.bitwise_and, op1=OP.logical_shift_left)
                VE(v.tensor_scalar(out=dq_nib[:].bitcast(U32),
                                   in0=dq_nib[:].bitcast(U32),
                                   scalar1=0x3E003E00, scalar2=None,
                                   op0=OP.bitwise_or))
            tk_ext = S.add("vector", vex, deps=[tk_ldp], incs=[("ve", 1)])
            tk_cor = tk_ext

            def vfin(v, wb=wb):
                v.tensor_scalar(out=dq_corr[:], in0=dq_nib[:].bitcast(BF16),
                                scalar1=2.0, scalar2=-0.25,
                                op0=OP.mult, op1=OP.add)
                v.tensor_tensor(out=dq_mag[:], in0=dq_corr[:],
                                in1=dq_nib[:].bitcast(BF16), op=OP.min)
                sc4 = bass.AP(dq_s32[:].tensor, dq_s32[:].offset,
                              [dq_s32[:].ap[0], [0, 4], [1, 16], [0, 16]])
                v.tensor_tensor(
                    out=dq_sf[:].bitcast(U32).rearrange("p (a g r) -> p a g r",
                                                        a=4, r=16),
                    in0=dq_sf[:].bitcast(U32).rearrange("p (a g r) -> p a g r",
                                                        a=4, r=16),
                    in1=sc4, op=OP.bitwise_or)
                VE(v.tensor_tensor(out=dq_w[wb][:], in0=dq_mag[:],
                                   in1=dq_sf[:].bitcast(BF16), op=OP.mult))
            tk_fin = S.add("vector", vfin,
                           deps=[tk_cor, tk_s32, dqs["last_w_store"][wb]],
                           incs=[("ve", 1)])
            dqs["last_v"] = tk_fin
            dqs["last_sc_user"] = tk_fin
            tk_stw = S.add("gpsimd", lambda g, ci=ci, wb=wb: DG(g.dma_start(
                out=wgu_d[ci * 128:(ci + 1) * 128, :], in_=dq_w[wb][:])),
                deps=[tk_fin], incs=[("dG", 16)])
            dqs["last_w_store"][wb] = tk_stw
            tk_dq_store["gu"][ci] = tk_stw

        def emit_dn_block(db):
            tk_ldp = S.add("gpsimd", lambda g, db=db: DG(g.dma_start(
                out=dq_din[:], in_=dn16[db * 128:(db + 1) * 128, :])),
                deps=[dqs["last_din_user"]], incs=[("dG", 16)])
            tk_lds = S.add("gpsimd", lambda g, db=db: DG(g.dma_start(
                out=dq_scf[:], in_=dns[db * 128:(db + 1) * 128, :])),
                deps=[dqs["last_sc_user"]], incs=[("dG", 16)])
            tk_scb = S.add("vector", lambda v: VE(v.tensor_scalar(
                out=dq_scb[:], in0=dq_scf[:], scalar1=4.0, scalar2=None,
                op0=OP.mult)), deps=[tk_lds], incs=[("ve", 1)])

            def s32f(g):
                lo = dq_s32[:].bitcast(U16).rearrange(
                    "p (f two) -> p f two", two=2)[:, :, 0]
                hi = dq_s32[:].bitcast(U16).rearrange(
                    "p (f two) -> p f two", two=2)[:, :, 1]
                g.tensor_copy(lo, dq_scb[:].bitcast(U16))
                GE(g.tensor_copy(hi, dq_scb[:].bitcast(U16)))
            tk_s32 = S.add("gpsimd", s32f, deps=[tk_scb], incs=[("ge", 1)])

            last_fin = None
            for kp4 in range(4):
                wb = dqs["wbuf"]; dqs["wbuf"] ^= 1
                m, sh, left = MAG_MASKS[kp4]
                ms, shs = SGN_MASKS[kp4]

                def vex(v, m=m, sh=sh, left=left, ms=ms, shs=shs):
                    u32i = dq_din[:].bitcast(U32)
                    v.tensor_scalar(out=dq_nib[:, 0:1408].bitcast(U32), in0=u32i,
                                    scalar1=m | (m << 16), scalar2=sh,
                                    op0=OP.bitwise_and,
                                    op1=OP.logical_shift_left if left
                                    else OP.logical_shift_right)
                    v.tensor_scalar(out=dq_sf[:, 0:1408].bitcast(U32), in0=u32i,
                                    scalar1=ms | (ms << 16), scalar2=shs,
                                    op0=OP.bitwise_and, op1=OP.logical_shift_left)
                    VE(v.tensor_scalar(out=dq_nib[:, 0:1408].bitcast(U32),
                                       in0=dq_nib[:, 0:1408].bitcast(U32),
                                       scalar1=0x3E003E00, scalar2=None,
                                       op0=OP.bitwise_or))
                tk_ext = S.add("vector", vex, deps=[tk_ldp], incs=[("ve", 1)])
                tk_cor = tk_ext

                def vfin(v, wb=wb):
                    v.tensor_scalar(out=dq_corr[:, 0:1408],
                                    in0=dq_nib[:, 0:1408].bitcast(BF16),
                                    scalar1=2.0, scalar2=-0.25,
                                    op0=OP.mult, op1=OP.add)
                    v.tensor_tensor(out=dq_mag[:, 0:1408],
                                    in0=dq_corr[:, 0:1408],
                                    in1=dq_nib[:, 0:1408].bitcast(BF16),
                                    op=OP.min)
                    sc4 = bass.AP(dq_s32[:].tensor, dq_s32[:].offset,
                                  [dq_s32[:].ap[0], [1, 44], [0, 16]])
                    v.tensor_tensor(
                        out=dq_sf[:, 0:1408].bitcast(U32).rearrange(
                            "p (g r) -> p g r", g=44),
                        in0=dq_sf[:, 0:1408].bitcast(U32).rearrange(
                            "p (g r) -> p g r", g=44),
                        in1=sc4, op=OP.bitwise_or)
                    VE(v.tensor_tensor(out=dq_w[wb][:, 0:1408],
                                       in0=dq_mag[:, 0:1408],
                                       in1=dq_sf[:, 0:1408].bitcast(BF16),
                                       op=OP.mult))
                tk_fin = S.add("vector", vfin,
                               deps=[tk_cor, tk_s32, dqs["last_w_store"][wb]],
                               incs=[("ve", 1)])
                dqs["last_v"] = tk_fin
                dqs["last_sc_user"] = tk_fin
                last_fin = tk_fin
                tk_stw = S.add("gpsimd", lambda g, db=db, kp4=kp4, wb=wb: DG(
                    g.dma_start(out=wdn_d[db * 128:(db + 1) * 128,
                                          kp4 * 1408:(kp4 + 1) * 1408],
                                in_=dq_w[wb][:, 0:1408])),
                    deps=[tk_fin], incs=[("dG", 16)])
                dqs["last_w_store"][wb] = tk_stw
                tk_dq_store["dn"][db][kp4] = tk_stw
            dqs["last_din_user"] = last_fin

        dq_units = [lambda ci=ci: emit_gu_chunk(ci) for ci in range(88)]
        dq_units += [lambda db=db: emit_dn_block(db) for db in range(16)]
        dq_iter = iter(dq_units)
        n_dq_per_slot = (len(dq_units) + 63) // 64

        with contextlib.ExitStack() as es1:
            E1 = es1.enter_context
            xf = [E1(nc.sbuf_tensor(f"xf{i}", [128, D], F32)) for i in range(2)]
            xnb = [E1(nc.sbuf_tensor(f"xnb{i}", [128, D], BF16)) for i in range(2)]
            negm = [E1(nc.sbuf_tensor(f"negm{i}", [128, 1], F32)) for i in range(2)]
            rstd1 = [E1(nc.sbuf_tensor(f"rstd1{i}", [128, 1], F32)) for i in range(2)]
            ssq1 = [E1(nc.sbuf_tensor(f"ssq1{i}", [128, 1], F32)) for i in range(2)]
            sxs = [E1(nc.sbuf_tensor(f"sxs{i}", [128, 1], F32)) for i in range(2)]
            vart = [E1(nc.sbuf_tensor(f"vart{i}", [128, 1], F32)) for i in range(2)]
            sqd = E1(nc.sbuf_tensor("sqd", [128, D], BF16))

            regs = {}

            def p1_reg(g):
                regs["rt"] = g.alloc_register("rt")
                g.reg_load(regs["rt"], ofs_sb[:1, :1])
            S.add("gpsimd", p1_reg, deps=[tk_ofs])

            tk_xn = [None] * NT
            tk_st = [None] * NT
            for t in range(NT):
                b = t % 2
                tk_ld = S.add("sync", lambda sy, t=t, b=b: DX(sy.dma_start(
                    out=xf[b][:], in_=x[t * 128:(t + 1) * 128, :]), f"dL{b}"),
                    deps=[tk_xn[t - 2] if t >= 2 else None],
                    incs=[(f"dL{b}", 16)])

                tk_red = S.add("vector", lambda v, b=b: VE(
                    v.reduce_sum(sxs[b][:], xf[b][:], axis=AX)),
                    deps=[tk_ld], incs=[("ve", 1)])
                tk_sum = S.add("vector", lambda v, b=b: VE(
                    v.tensor_scalar(out=negm[b][:], in0=sxs[b][:],
                                    scalar1=-1.0 / D, scalar2=None, op0=OP.mult)),
                    deps=[tk_red], incs=[("ve", 1)])

                def p1_sq(s, b=b):
                    s.memzero(ssq1[b][:])
                    SE(s.activation(out=sqd[:], in_=xf[b][:], func=AF.Square,
                                    accum_out=ssq1[b][:]))
                tk_sq = S.add("scalar", p1_sq, deps=[tk_ld], incs=[("se", 1)])

                tk_va = S.add("vector", lambda v, b=b: VE(
                    v.tensor_scalar(out=vart[b][:], in0=ssq1[b][:],
                                    scalar1=1.0 / D, scalar2=None, op0=OP.mult)),
                    deps=[tk_sq], incs=[("ve", 1)])
                tk_vb = S.add("vector", lambda v, b=b: VE(
                    v.tensor_tensor(out=sxs[b][:], in0=negm[b][:],
                                    in1=negm[b][:], op=OP.mult)),
                    deps=[tk_sum], incs=[("ve", 1)])
                tk_var = S.add("vector", lambda v, b=b: VE(
                    v.scalar_tensor_tensor(out=vart[b][:], in0=sxs[b][:],
                                           scalar=-1.0, in1=vart[b][:],
                                           op0=OP.mult, op1=OP.add)),
                    deps=[tk_va, tk_vb], incs=[("ve", 1)])

                tk_ln1 = S.add("scalar", lambda s, b=b: SE(
                    s.activation(out=ssq1[b][:], in_=vart[b][:], func=AF.Ln,
                                 bias=ceps1[:], scale=1.0)),
                    deps=[tk_var], incs=[("se", 1)])
                tk_rstd = S.add("scalar", lambda s, b=b: SE(
                    s.activation(out=rstd1[b][:], in_=ssq1[b][:], func=AF.Exp,
                                 scale=-0.5)),
                    deps=[tk_ln1], incs=[("se", 1)])

                def p1_xn(v, b=b):
                    xo = bass.AP(xnb[b][:].tensor, xnb[b][:].offset,
                                 [xnb[b][:].ap[0], [1, 512], [512, 4]])
                    xi = bass.AP(xf[b][:].tensor, xf[b][:].offset,
                                 [xf[b][:].ap[0], [4, 512], [1, 4]])
                    VE(v.tensor_scalar(out=xo, in0=xi, scalar1=negm[b][:],
                                       scalar2=rstd1[b][:], op0=OP.add, op1=OP.mult))
                tk_xn[t] = S.add("vector", p1_xn,
                                 deps=[tk_rstd, tk_st[t - 2] if t >= 2 else None],
                                 incs=[("ve", 1)])

                def p1_st(g, t=t, b=b):
                    DG(g.dma_start(out=xn_d[t * 128:(t + 1) * 128, :], in_=xnb[b][:]))
                tk_st[t] = S.add("gpsimd", p1_st, deps=[tk_xn[t]],
                                 incs=[("dG", 16)])
                u = next(dq_iter, None)
                if u is not None:
                    u()

            def p1_xn2(g):
                dyn = bass.AP(xn2_d.tensor, regs["rt"], [[D, T], [1, D]])
                DG(g.dma_start(out=dyn, in_=xn_d))
            tk_p1_done = S.add("gpsimd", p1_xn2, deps=[tk_st[NT - 1]],
                               incs=[("dG", 16)])
            S.flush(block, sems, "P1")

        with contextlib.ExitStack() as es2:
            E2 = es2.enter_context
            wkb = E2(nc.sbuf_tensor("wkb", [128, 16 * DK], BF16))
            wvb = E2(nc.sbuf_tensor("wvb", [128, 16 * DK], BF16))
            wqb = E2(nc.sbuf_tensor("wqb", [128, 16 * DK], BF16))
            tk_w = {}
            for nm, src, dst in [("k", wk, wkb), ("v", wv, wvb), ("q", wq, wqb)]:
                tk_w[nm] = S.add("gpsimd", lambda g, src=src, dst=dst: DG(
                    g.dma_start(
                        out=dst[:].rearrange("p (a n) -> p a n", a=16),
                        in_=src.rearrange("(a p) n -> p a n", p=128))),
                    incs=[("dG", 16)])
            xnt = [E2(nc.sbuf_tensor(f"xnt{i}", [128, 16 * 512], BF16))
                   for i in range(2)]
            evb = {k: [E2(nc.sbuf_tensor(f"{k}ev{i}", [128, 512], BF16))
                       for i in range(2)] for k in "kqv"}
            pp = {k: [E2(nc.psum_tensor(f"{k}p{i}", [128, 512], F32))
                      for i in range(2)] for k in "kqv"}

            tk_chunk = [None] * 8
            tk_ev = {}
            tk_stc = {}
            for c in range(8):
                cb = c % 2

                def p2_ldx(sy, c=c, cb=cb):
                    for dt in range(16):
                        DX(sy.dma_start(
                            out=xnt[cb][:, dt * 512:(dt + 1) * 512],
                            in_=xn_d[c * 512:(c + 1) * 512, dt * 128:(dt + 1) * 128],
                            transpose=True), f"dQ{cb}")
                tk_ldx = S.add("sync", p2_ldx,
                               deps=[tk_st[4 * c + 3],
                                     tk_chunk[c - 2] if c >= 2 else None],
                               incs=[(f"dQ{cb}", 256)])

                for kind, wsb, dst in [("k", wkb, k_d), ("q", wqb, q_d),
                                       ("v", wvb, v_d)]:
                    for j in range(8):
                        jb = j % 2
                        key = (kind, c, j)
                        prev = (kind, c, j - 2) if j >= 2 else (kind, c - 1, 6 + j)
                        dep_bank = tk_ev.get(prev)

                        def mms(t, j=j, jb=jb, wsb=wsb, kind=kind, cb=cb):
                            ps = pp[kind][jb]
                            for dt in range(16):
                                if kind in ("k", "q"):
                                    lhs = wsb[:, dt * DK + j * 128:
                                              dt * DK + (j + 1) * 128]
                                    rhs = xnt[cb][:, dt * 512:(dt + 1) * 512]
                                else:
                                    ts4, dc = divmod(j, 2)
                                    lhs = xnt[cb][:, dt * 512 + ts4 * 128:
                                                  dt * 512 + (ts4 + 1) * 128]
                                    rhs = wsb[:, dt * DK + dc * 512:
                                              dt * DK + (dc + 1) * 512]
                                ins = t.matmul(ps[:], lhs, rhs,
                                               start=(dt == 0), stop=(dt == 15))
                            PE(ins)
                        tk_mm = S.add("tensor", mms,
                                      deps=[tk_ldx, dep_bank, tk_w[kind]],
                                      incs=[("pe", 1)])
                        dep_stb = tk_stc.get(prev)

                        def ev(se_, jb=jb, kind=kind):
                            SE(se_.copy(evb[kind][jb][:], pp[kind][jb][:]))
                        tk_ev[key] = S.add("scalar", ev, deps=[tk_mm, dep_stb],
                                           incs=[("se", 1)])

                        def st(s2, j=j, jb=jb, kind=kind, c=c, dst=dst):
                            if kind in ("k", "q"):
                                d = dst[j * 128:(j + 1) * 128, c * 512:(c + 1) * 512]
                            else:
                                ts4, dc = divmod(j, 2)
                                d = dst[c * 512 + ts4 * 128:
                                        c * 512 + (ts4 + 1) * 128,
                                        dc * 512:(dc + 1) * 512]
                            DA(s2.dma_start(out=d, in_=evb[kind][jb][:]))
                        tk_stc[key] = S.add("scalar", st, deps=[tk_ev[key]],
                                            incs=[("dA", 16)])
                        u = next(dq_iter, None)
                        if u is not None:
                            u()
                tk_chunk[c] = tk_ev[("v", c, 7)]
            for u in dq_iter:
                u()
            S.flush(block, sems, "P2")
            p2_gate = S.snap()

        with contextlib.ExitStack() as es4:
            E4 = es4.enter_context
            k_sb = E4(nc.sbuf_tensor("k_sb", [128, NH * T], BF16))
            v_sb = E4(nc.sbuf_tensor("v_sb", [128, NT * DK], BF16))
            qm_sb = [E4(nc.sbuf_tensor(f"qm_sb{i}", [128, NH * 512], BF16))
                     for i in range(2)]
            p12 = [E4(nc.sbuf_tensor(f"p12_{i}", [128, 1024], BF16))
                   for i in range(2)]
            oh_sb = E4(nc.sbuf_tensor("oh_sb", [128, NH * 512], BF16))
            o1sb = E4(nc.sbuf_tensor("o1sb", [128, 512], F32))
            o2sb = E4(nc.sbuf_tensor("o2sb", [128, 512], F32))
            l_sb = E4(nc.sbuf_tensor("l_sb", [1, 1024], F32))
            nlrow = E4(nc.sbuf_tensor("nlrow", [1, NH * 128], BF16))
            rl_f = E4(nc.sbuf_tensor("rl_f", [1, 1024], F32))
            rl_b = E4(nc.sbuf_tensor("rl_b", [1, 1024], BF16))
            rlb1 = E4(nc.sbuf_tensor("rlb1", [128, 512], BF16))
            rlb2 = E4(nc.sbuf_tensor("rlb2", [128, 512], BF16))
            tcmb = [E4(nc.sbuf_tensor(f"tcmb{i}", [128, 512], BF16))
                    for i in range(2)]
            s_ps = [E4(nc.psum_tensor(f"s_ps{i}", [128, 1024], F32))
                    for i in range(2)]
            o_ps = [E4(nc.psum_tensor(f"o_ps{i}", [128, 512], F32))
                    for i in range(2)]
            l_ps = E4(nc.psum_tensor("l_ps", [128, 1024], F32))

            def nlr(v):
                src = bass.AP(lam_sg[:].tensor, lam_sg[:].offset,
                              [lam_sg[:].ap[0], [1, NH], [0, 128]])
                VE(v.tensor_scalar(
                    out=nlrow[:].rearrange("o (h j) -> o h j", h=NH),
                    in0=src, scalar1=-1.0, scalar2=None, op0=OP.mult))
            tk_nlr = S.add("vector", nlr, deps=[tk_sig, p2_gate],
                           incs=[("ve", 1)])

            def p4_ldk(sy):
                for h in range(NH):
                    DX(sy.dma_start(out=k_sb[:, h * T:(h + 1) * T],
                                    in_=k_d[h * 128:(h + 1) * 128, :]), "dK")
            tk_ldk = S.add("sync", p4_ldk, deps=[p2_gate], incs=[("dK", 128)])

            def p4_ldv(sy):
                for tt in range(NT):
                    DX(sy.dma_start(out=v_sb[:, tt * DK:(tt + 1) * DK],
                                    in_=v_d[tt * 128:(tt + 1) * 128, :]), "dK")
            tk_ldv = S.add("sync", p4_ldv, deps=[p2_gate], incs=[("dK", 512)])

            tk_ldq = [None] * 8
            tk_qm_done = [None] * 8

            def emit_ldq(qm):
                def f(sy, qm=qm):
                    src = bass.AP(q_d.tensor, qm * 512,
                                  [[T, 128], [128 * T, NH], [1, 512]])
                    DX(sy.dma_start(out=qm_sb[qm % 2][:], in_=src),
                       f"dQ{qm % 2}")
                tk_ldq[qm] = S.add(
                    "sync", f,
                    deps=[p2_gate, tk_qm_done[qm - 2] if qm >= 2 else None],
                    incs=[(f"dQ{qm % 2}", 16)])

            emit_ldq(0)
            emit_ldq(1)

            tk_exp = {}
            tk_avl = {}
            tk_cmb_prev = None
            tk_bce_prev = None
            tk_bc_prev = None
            tk_oev_prev = None
            tk_oh_dma = {}
            for qm in range(8):
                n_kt = 4 * qm + 4
                for hi in range(NH):
                    def emit_skt(kt, hi=hi, qm=qm):
                        par = kt % 2
                        qoff = max(0, (kt - 4 * qm) * 128)

                        def smm(t, hi=hi, kt=kt, qm=qm, par=par, qoff=qoff):
                            l1 = k_sb[0:64, hi * T + kt * 128:
                                      hi * T + (kt + 1) * 128]
                            r1 = qm_sb[qm % 2][0:64, hi * 512 + qoff:
                                               (hi + 1) * 512]
                            t.matmul(s_ps[par][:, qoff:512], l1, r1,
                                     start=True, stop=True)
                            l2 = k_sb[64:128, hi * T + kt * 128:
                                      hi * T + (kt + 1) * 128]
                            r2 = qm_sb[qm % 2][64:128, hi * 512 + qoff:
                                               (hi + 1) * 512]
                            PE(t.matmul(s_ps[par][:, 512 + qoff:1024], l2, r2,
                                        start=True, stop=True))
                        tk_s = S.add("tensor", smm,
                                     deps=[tk_ldq[qm], tk_ldk,
                                           tk_exp.get((qm, hi, kt - 2))],
                                     incs=[("pe", 1)])

                        if kt >= 4 * qm:
                            def msk(v, par=par, qoff=qoff):
                                v.tensor_tensor(
                                    out=s_ps[par][:, qoff:qoff + 128],
                                    in0=s_ps[par][:, qoff:qoff + 128],
                                    in1=dmask[:], op=OP.add)
                                VE(v.tensor_tensor(
                                    out=s_ps[par][:, 512 + qoff:512 + qoff + 128],
                                    in0=s_ps[par][:, 512 + qoff:512 + qoff + 128],
                                    in1=dmask[:], op=OP.add))
                            tk_m = S.add("vector", msk, deps=[tk_s, tk_mask],
                                         incs=[("ve", 1)])
                        else:
                            tk_m = tk_s

                        def expf(s, par=par, qoff=qoff, kt=kt):
                            pb = p12[kt % 2]
                            if qoff == 0:
                                SE(s.activation(out=pb[:], in_=s_ps[par][:],
                                                func=AF.Exp, bias=cn8[:],
                                                scale=0.125))
                            else:
                                s.activation(out=pb[:, qoff:512],
                                             in_=s_ps[par][:, qoff:512],
                                             func=AF.Exp, bias=cn8[:],
                                             scale=0.125)
                                SE(s.activation(
                                    out=pb[:, 512 + qoff:1024],
                                    in_=s_ps[par][:, 512 + qoff:1024],
                                    func=AF.Exp, bias=cn8[:], scale=0.125))
                        tk_exp[(qm, hi, kt)] = S.add(
                            "scalar", expf,
                            deps=[tk_m, tk_avl.get((qm, hi, kt - 2))],
                            incs=[("se", 1)])

                    def emit_avkt(kt, hi=hi, qm=qm):
                        qoff = max(0, (kt - 4 * qm) * 128)

                        def avl(t, hi=hi, kt=kt, qoff=qoff, qm=qm):
                            pb = p12[kt % 2]
                            vt = v_sb[:, kt * DK + hi * 128:
                                      kt * DK + (hi + 1) * 128]
                            st0 = (kt == 0)
                            sp0 = (kt == 4 * qm + 3)
                            t.matmul(o_ps[0][:, qoff:512], vt, pb[:, qoff:512],
                                     start=st0, stop=sp0, skip_group_check=True)
                            t.matmul(o_ps[1][:, qoff:512], vt,
                                     pb[:, 512 + qoff:1024],
                                     start=st0, stop=sp0, skip_group_check=True)
                            t.matmul(l_ps[0:1, qoff:512], ones_kb[:],
                                     pb[:, qoff:512],
                                     start=st0, stop=sp0, skip_group_check=True)
                            PE(t.matmul(l_ps[0:1, 512 + qoff:1024], ones_kb[:],
                                        pb[:, 512 + qoff:1024],
                                        start=st0, stop=sp0,
                                        skip_group_check=True))
                        tk_avl[(qm, hi, kt)] = S.add(
                            "tensor", avl,
                            deps=[tk_exp[(qm, hi, kt)], tk_ldv,
                                  tk_oev_prev if kt <= 1 else None,
                                  tk_bce_prev if kt <= 1 else None],
                            incs=[("pe", 1)])

                    for kt in range(n_kt + 1):
                        if kt < n_kt:
                            emit_skt(kt)
                        if kt >= 1:
                            emit_avkt(kt - 1)

                    last_av = tk_avl[(qm, hi, n_kt - 1)]

                    def oev(se_, hi=hi):
                        se_.copy(o1sb[:], o_ps[0][:])
                        se_.copy(o2sb[:], o_ps[1][:])
                        SE(se_.copy(l_sb[:], l_ps[0:1, :]))
                    tk_oev = S.add("scalar", oev,
                                   deps=[last_av, tk_cmb_prev],
                                   incs=[("se", 1)])
                    tk_oev_prev = tk_oev

                    def rlf(s):
                        s.activation(out=rl_f[:], in_=l_sb[:], func=AF.Ln)
                        SE(s.activation(out=rl_f[:], in_=rl_f[:], func=AF.Exp,
                                        scale=-1.0))
                    tk_rl = S.add("scalar", rlf, deps=[tk_oev], incs=[("se", 1)])
                    tk_rlb = S.add("vector", lambda v: VE(
                        v.tensor_copy(rl_b[:], rl_f[:])),
                        deps=[tk_rl, tk_bc_prev], incs=[("ve", 1)])

                    def bc(t, hi=hi):
                        t.matmul(l_ps[:, 0:512], ones_rb[:],
                                 rl_b[0:1, 0:512], start=True, stop=True)
                        PE(t.matmul(l_ps[:, 512:1024],
                                    nlrow[0:1, hi * 128:(hi + 1) * 128],
                                    rl_b[0:1, 512:1024], start=True, stop=True))
                    tk_bc = S.add("tensor", bc,
                                  deps=[tk_rlb, tk_oev], incs=[("pe", 1)])
                    tk_bc_prev = tk_bc

                    def bce(s):
                        s.copy(rlb1[:], l_ps[:, 0:512])
                        SE(s.copy(rlb2[:], l_ps[:, 512:1024]))
                    tk_bce = S.add("scalar", bce, deps=[tk_bc], incs=[("se", 1)])
                    tk_bce_prev = tk_bce

                    tk_c1 = S.add("vector", lambda v: VE(
                        v.tensor_tensor(out=tcmb[0][:], in0=o1sb[:],
                                        in1=rlb1[:], op=OP.mult)),
                        deps=[tk_bce, tk_oev], incs=[("ve", 1)])
                    tk_c2 = S.add("vector", lambda v: VE(
                        v.tensor_tensor(out=tcmb[1][:], in0=o2sb[:],
                                        in1=rlb2[:], op=OP.mult)),
                        deps=[tk_bce, tk_oev], incs=[("ve", 1)])
                    tk_cmb = S.add("vector", lambda v, hi=hi: VE(
                        v.tensor_tensor(out=oh_sb[:, hi * 512:(hi + 1) * 512],
                                        in0=tcmb[0][:], in1=tcmb[1][:],
                                        op=OP.add)),
                        deps=[tk_c1, tk_c2, tk_oh_dma.get((qm - 1, hi))],
                        incs=[("ve", 1)])
                    tk_cmb_prev = tk_cmb
                    tk_oh_dma[(qm, hi)] = S.add(
                        "scalar", lambda s2, hi=hi, qm=qm: DA(s2.dma_start(
                            out=o_d[hi * 128:(hi + 1) * 128,
                                    qm * 512:(qm + 1) * 512],
                            in_=oh_sb[:, hi * 512:(hi + 1) * 512])),
                        deps=[tk_cmb], incs=[("dA", 16)])
                tk_qm_done[qm] = tk_cmb_prev
                if qm + 2 < 8:
                    emit_ldq(qm + 2)
            S.flush(block, sems, "P4")
            p4_gate = S.snap()

        es56 = contextlib.ExitStack()
        xnt6 = es56.enter_context(nc.sbuf_tensor("xnt6", [128, 16 * TOWN], BF16))
        with contextlib.ExitStack() as es5:
            E5 = es5.enter_context
            wob = E5(nc.sbuf_tensor("wob", [128, NH * D], BF16))
            otl = [E5(nc.sbuf_tensor(f"otl{i}", [128, NH * 128], BF16))
                   for i in range(2)]
            att_sb = [E5(nc.sbuf_tensor(f"att_sb{i}", [128, D], BF16))
                      for i in range(2)]
            ap_ps = [E5(nc.psum_tensor(f"ap_ps{i}", [128, D], F32))
                     for i in range(2)]

            tk_wo = S.add("gpsimd", lambda g: DG(g.dma_start(
                out=wob[:].rearrange("p (a n) -> p a n", a=NH),
                in_=wo.rearrange("(a p) n -> p a n", p=128))),
                deps=[p4_gate], incs=[("dG", 16)])

            tk_p5 = {}
            tk_ev5 = {}
            tk_attst = {}
            for tt in range(NT):
                tb = tt % 2

                def ldo(sy, tt=tt, tb=tb):
                    src = bass.AP(o_d.tensor, tt * 128,
                                  [[T, 128], [128 * T, NH], [1, 128]])
                    DX(sy.dma_start(out=otl[tb][:], in_=src), f"dL{tb}")
                tk_ldo = S.add("sync", ldo, deps=[p4_gate, tk_p5.get(tt - 2)],
                               incs=[(f"dL{tb}", 16)])

                def p5mm(t, tb=tb):
                    for dv in range(NH):
                        for ch in range(4):
                            ins = t.matmul(
                                ap_ps[tb][:, ch * 512:(ch + 1) * 512],
                                otl[tb][:, dv * 128:(dv + 1) * 128],
                                wob[:, dv * D + ch * 512: dv * D + (ch + 1) * 512],
                                start=(dv == 0), stop=(dv == NH - 1))
                    PE(ins)
                tk_p5[tt] = S.add("tensor", p5mm,
                                  deps=[tk_ldo, tk_wo, tk_ev5.get(tt - 2)],
                                  incs=[("pe", 1)])
                tk_ev5[tt] = S.add("vector", lambda v, tb=tb: VE(
                    v.tensor_copy(att_sb[tb][:], ap_ps[tb][:])),
                    deps=[tk_p5[tt], tk_attst.get(tt - 2)], incs=[("ve", 1)])

                def stc(s2, tt=tt, tb=tb):
                    DA(s2.dma_start(out=cc_in[tt * 128:(tt + 1) * 128, :],
                                    in_=att_sb[tb][:]))
                    if dev >= 2:
                        DA(s2.dma_start(
                            out=att_dump[tt * 128:(tt + 1) * 128, :],
                            in_=att_sb[tb][:]))
                tk_attst[tt] = S.add("scalar", stc, deps=[tk_ev5[tt]],
                                     incs=[("dA", 32 if dev >= 2 else 16)])

            def _ccfn(g):
                ATTACH["cc"] += 1
                return g.collective_compute(
                "ReduceScatter", OP.add,
                replica_groups=[[0, 1], [2, 3], [4, 5], [6, 7]],
                    ins=[cc_in.opt()], outs=[cc_out.opt()]).then_inc(sems["cc"], 1)
            tk_cc = S.add("gpsimd", _ccfn,
                          deps=[tk_attst[NT - 1], {"dA": S.c["dA"]}], incs=[("cc", 1)])
            tk_cc2w = S.add("gpsimd", lambda g: DG(g.dma_start(out=cc2_in,
                                                               in_=dmask[:])),
                            deps=[tk_mask], incs=[("dG", 16)])

            def _ccfn2(g):
                ATTACH["cc"] += 1
                return g.collective_compute(
                    "AllReduce", OP.add,
                    replica_groups=[[0, 1], [2, 3], [4, 5], [6, 7]],
                    ins=[cc2_in.opt()], outs=[cc2_out.opt()]).then_inc(
                        sems["cc"], 1)
            tk_cc = S.add("gpsimd", _ccfn2, deps=[tk_cc, tk_cc2w],
                          incs=[("cc", 1)])

            def ldxn6(sy):
                for dt in range(16):
                    DX(sy.dma_start(out=xnt6[:, dt * TOWN:(dt + 1) * TOWN],
                                    in_=xn2_d[TOWN:TOWN + TOWN,
                                              dt * 128:(dt + 1) * 128],
                                    transpose=True), "dK")
            tk_xn6 = S.add("sync", ldxn6, deps=[p4_gate, tk_p1_done],
                           incs=[("dK", 256)])
            if dev >= 2:
                S.add("gpsimd", lambda g: DG(g.dma_start(out=cc_dump, in_=cc_out)),
                      deps=[tk_cc], incs=[("dG", 16)])
            S.flush(block, sems, "P5")
            p5_gate = S.snap()

        with contextlib.ExitStack() as es6:
            E6 = es6.enter_context
            wgt = [E6(nc.sbuf_tensor(f"wgt{i}", [128, 16 * 512], BF16))
                   for i in range(2)]
            wup = [E6(nc.sbuf_tensor(f"wup{i}", [128, 16 * 512], BF16))
                   for i in range(2)]
            gsil = E6(nc.sbuf_tensor("gsil", [128, TOWN], BF16))
            h_sb = [E6(nc.sbuf_tensor(f"h_sb{i}", [128, TOWN], BF16))
                    for i in range(2)]
            gA = E6(nc.psum_tensor("gA", [128, TOWN], F32))
            gB = E6(nc.psum_tensor("gB", [128, TOWN], F32))

            tk_h = {}
            tk_hv = {}
            tk_sil = {}
            tk_blk = [None, None]
            for blk in range(11):
                bb = blk % 2

                def ldw(sy, blk=blk, bb=bb):
                    for dt in range(16):
                        DX(sy.dma_start(
                            out=wgt[bb][:, dt * 512:(dt + 1) * 512],
                            in_=wgu_d[blk * 512:(blk + 1) * 512,
                                      dt * 128:(dt + 1) * 128], transpose=True),
                           f"dW{bb}")
                    for dt in range(16):
                        DX(sy.dma_start(
                            out=wup[bb][:, dt * 512:(dt + 1) * 512],
                            in_=wgu_d[II + blk * 512: II + (blk + 1) * 512,
                                      dt * 128:(dt + 1) * 128], transpose=True),
                           f"dW{bb}")
                tk_ldw = S.add("sync", ldw,
                               deps=[tk_dq_store["gu"][4 * blk + 3],
                                     tk_dq_store["gu"][44 + 4 * blk + 3],
                                     tk_blk[bb], p5_gate],
                               incs=[(f"dW{bb}", 512)])

                for pos in range(4):
                    ot = blk * 4 + pos

                    def gmm(t, bb=bb, pos=pos):
                        for dt in range(16):
                            lhs = wgt[bb][:, dt * 512 + pos * 128:
                                          dt * 512 + (pos + 1) * 128]
                            for ch in range(4):
                                ins = t.matmul(
                                    gA[:, ch * 512:(ch + 1) * 512], lhs,
                                    xnt6[:, dt * TOWN + ch * 512:
                                         dt * TOWN + (ch + 1) * 512],
                                    start=(dt == 0), stop=(dt == 15))
                        PE(ins)
                    tk_g = S.add("tensor", gmm,
                                 deps=[tk_ldw, tk_xn6, tk_sil.get(ot - 1)],
                                 incs=[("pe", 1)])

                    def umm(t, bb=bb, pos=pos):
                        for dt in range(16):
                            lhs = wup[bb][:, dt * 512 + pos * 128:
                                          dt * 512 + (pos + 1) * 128]
                            for ch in range(4):
                                ins = t.matmul(
                                    gB[:, ch * 512:(ch + 1) * 512], lhs,
                                    xnt6[:, dt * TOWN + ch * 512:
                                         dt * TOWN + (ch + 1) * 512],
                                    start=(dt == 0), stop=(dt == 15))
                        PE(ins)
                    tk_u = S.add("tensor", umm,
                                 deps=[tk_ldw, tk_xn6, tk_hv.get(ot - 1)],
                                 incs=[("pe", 1)])

                    def sil(s):
                        for ch in range(4):
                            ins = s.activation(
                                out=gsil[:, ch * 512:(ch + 1) * 512],
                                in_=gA[:, ch * 512:(ch + 1) * 512], func=AF.Silu)
                        SE(ins)
                    tk_sil[ot] = S.add("scalar", sil, deps=[tk_g],
                                       incs=[("se", 1)])
                    hb = ot % 2

                    def hv(v, hb=hb):
                        for ch in range(4):
                            ins = v.tensor_tensor(
                                out=h_sb[hb][:, ch * 512:(ch + 1) * 512],
                                in0=gB[:, ch * 512:(ch + 1) * 512],
                                in1=gsil[:, ch * 512:(ch + 1) * 512], op=OP.mult)
                        VE(ins)
                    tk_hv[ot] = S.add("vector", hv,
                                      deps=[tk_u, tk_sil[ot], tk_h.get(ot - 2)],
                                      incs=[("ve", 1)])
                    tk_h[ot] = S.add("scalar", lambda s2, ot=ot, hb=hb: DA(
                        s2.dma_start(out=h_d[ot * 128:(ot + 1) * 128, :],
                                     in_=h_sb[hb][:])),
                        deps=[tk_hv[ot]], incs=[("dA", 16)])
                tk_blk[bb] = tk_h[blk * 4 + 3]
            S.flush(block, sems, "P6")
            p6_gate = S.snap()
        es56.close()

        with contextlib.ExitStack() as es7:
            E7 = es7.enter_context
            wdt = E7(nc.sbuf_tensor("wdt", [128, 44 * 1024], BF16))
            h7 = [E7(nc.sbuf_tensor(f"h7_{i}", [128, 44 * 128], BF16))
                  for i in range(2)]
            y0st = [E7(nc.sbuf_tensor(f"y0st{i}", [128, 1024], F32))
                    for i in range(2)]
            att7 = [E7(nc.sbuf_tensor(f"att7_{i}", [128, D], BF16))
                    for i in range(2)]
            xf7 = [E7(nc.sbuf_tensor(f"xf7_{i}", [128, D], F32)) for i in range(2)]
            y0l = [E7(nc.sbuf_tensor(f"y0l{i}", [128, 1024], F32))
                   for i in range(2)]
            outb = [E7(nc.sbuf_tensor(f"outb{i}", [128, D], F32))
                    for i in range(2)]
            ssqa = E7(nc.sbuf_tensor("ssqa", [128, NTO], F32))
            ssqh = [E7(nc.sbuf_tensor(f"ssqh{i}", [128, 1], F32)) for i in range(2)]
            ssqt = [E7(nc.sbuf_tensor(f"ssqt{i}", [128, 1], F32)) for i in range(2)]
            lnv7 = [E7(nc.sbuf_tensor(f"lnv7_{i}", [128, 1], F32))
                    for i in range(2)]
            rstd7 = [E7(nc.sbuf_tensor(f"rstd7_{i}", [128, 1], F32))
                     for i in range(2)]
            sq7 = E7(nc.sbuf_tensor("sq7", [128, 1024], BF16))
            y_ps = [E7(nc.psum_tensor(f"y_ps{i}", [128, 1024], F32))
                    for i in range(2)]

            tk_z = S.add("scalar", lambda s: SE(s.memzero(ssqa[:])),
                         deps=[p6_gate], incs=[("se", 1)])
            tk_ob0 = S.add("vector", lambda v: (
                v.memset(outb[0][:], 0.0),
                VE(v.memset(outb[1][:], 0.0)))[-1],
                deps=[p6_gate], incs=[("ve", 1)])

            tk_out = {}
            tk_y0 = {}
            tk_pe7 = {}
            for half in range(2):
                def ldwd(sy, half=half):
                    for it in range(44):
                        DX(sy.dma_start(
                            out=wdt[:, it * 1024:(it + 1) * 1024],
                            in_=wdn_d[half * 1024:(half + 1) * 1024,
                                      it * 128:(it + 1) * 128], transpose=True),
                           f"dW{half}")
                dep_wd = [p6_gate] if half == 0 else [tk_pe7[(0, NTO - 1)]]
                dn_deps = [tk_dq_store["dn"][dbi][3] for dbi in range(16)]
                tk_ldwd = S.add("sync", ldwd, deps=dep_wd + dn_deps,
                                incs=[(f"dW{half}", 704)])

                order = list(range(NTO))
                for pos, tt in enumerate(order):
                    tb = pos % 2
                    prev2 = order[pos - 2] if pos >= 2 else None

                    def ldh(sy, tt=tt, tb=tb):
                        for ggg in range(11):
                            src = bass.AP(h_d.tensor,
                                          ggg * 4 * 128 * TOWN + tt * 128,
                                          [[TOWN, 128], [128 * TOWN, 4], [1, 128]])
                            DX(sy.dma_start(
                                out=h7[tb][:, ggg * 512:(ggg + 1) * 512],
                                in_=src), f"dQ{tb}")
                    if half == 0:
                        dep_h7 = tk_pe7.get((0, prev2)) if prev2 is not None else None
                    else:
                        dep_h7 = (tk_pe7.get((1, prev2)) if prev2 is not None
                                  else tk_pe7.get((0, NTO - 2 + pos)))
                    tk_ldh = S.add("sync", ldh, deps=[p6_gate, dep_h7],
                                   incs=[(f"dQ{tb}", 176)])

                    def dmm(t, tb=tb):
                        for it in range(44):
                            for ch in range(2):
                                ins = t.matmul(
                                    y_ps[tb][:, ch * 512:(ch + 1) * 512],
                                    h7[tb][:, it * 128:(it + 1) * 128],
                                    wdt[:, it * 1024 + ch * 512:
                                        it * 1024 + (ch + 1) * 512],
                                    start=(it == 0), stop=(it == 43))
                        PE(ins)
                    if half == 0:
                        dep_bank = tk_y0.get(prev2) if prev2 is not None else None
                    else:
                        dep_bank = (tk_out.get(prev2) if prev2 is not None
                                    else tk_y0.get(NTO - 2 + pos))
                    tk_pe7[(half, tt)] = S.add("tensor", dmm,
                                               deps=[tk_ldh, tk_ldwd, dep_bank],
                                               incs=[("pe", 1)])
                    def sqacc(s, tb=tb):
                        s.memzero(ssqh[tb][:])
                        SE(s.activation(out=sq7[:], in_=y_ps[tb][:],
                                        func=AF.Square, accum_out=ssqh[tb][:]))
                    tk_ssq = S.add("scalar", sqacc,
                                   deps=[tk_pe7[(half, tt)], tk_z],
                                   incs=[("se", 1)])

                    if half == 0:
                        tk_ssqc = S.add("scalar", lambda s, tb=tb, tt=tt: SE(
                            s.copy(ssqa[:, tt:tt + 1], ssqh[tb][:])),
                            deps=[tk_ssq], incs=[("se", 1)])
                        tk_ev7 = S.add("vector", lambda v, tb=tb: VE(
                            v.tensor_copy(y0st[tb][:], y_ps[tb][:])),
                            deps=[tk_pe7[(half, tt)], tk_y0.get(tt - 2), tk_ssq],
                            incs=[("ve", 1)])
                        tk_y0[tt] = S.add("scalar", lambda s2, tt=tt, tb=tb: DA(
                            s2.dma_start(out=y0_d[tt * 128:(tt + 1) * 128, :],
                                         in_=y0st[tb][:])),
                            deps=[tk_ev7], incs=[("dA", 16)])
                    else:
                        tk_tot = S.add("vector", lambda v, tb=tb, tt=tt: VE(
                            v.tensor_tensor(out=ssqt[tb][:],
                                            in0=ssqa[:, tt:tt + 1],
                                            in1=ssqh[tb][:], op=OP.add)),
                            deps=[tk_ssq], incs=[("ve", 1)])
                        tk_l7 = S.add("scalar", lambda s, tb=tb: SE(
                            s.activation(out=lnv7[tb][:], in_=ssqt[tb][:],
                                         func=AF.Ln, bias=ceps2[:],
                                         scale=1.0 / D)),
                            deps=[tk_tot], incs=[("se", 1)])
                        tk_rs7 = S.add("scalar", lambda s, tb=tb: SE(
                            s.activation(out=rstd7[tb][:], in_=lnv7[tb][:],
                                         func=AF.Exp, scale=-0.5)),
                            deps=[tk_l7], incs=[("se", 1)])
                        if dev >= 2:
                            tk_y1e = S.add("vector", lambda v, tb=tb: VE(
                                v.tensor_copy(y0st[tb][:], y_ps[tb][:])),
                                deps=[tk_pe7[(half, tt)], tk_rs7],
                                incs=[("ve", 1)])
                            S.add("scalar", lambda s2, tt=tt, tb=tb: DA(
                                s2.dma_start(out=y1_dump[tt * 128:(tt + 1) * 128, :],
                                             in_=y0st[tb][:])),
                                deps=[tk_y1e], incs=[("dA", 16)])
                        dep_prev_out = (tk_out.get(prev2)
                                        if prev2 is not None else None)
                        pb = tb

                        def ld3(g, tt=tt, tb=tb, pb=pb):
                            DL(g.dma_start(
                                out=att7[tb][:],
                                in_=cc_out[tt * 128:(tt + 1) * 128, :]), pb)
                            DL(g.dma_start(
                                out=xf7[tb][:],
                                in_=xown[tt * 128:(tt + 1) * 128, :]), pb)
                            DL(g.dma_start(
                                out=y0l[tb][:],
                                in_=y0_d[tt * 128:(tt + 1) * 128, :]), pb)
                        tk_ld3 = S.add("gpsimd", ld3,
                                       deps=[tk_cc, tk_y0[tt], dep_prev_out],
                                       incs=[(f"dL{pb}", 48)])
                        tk_lda = tk_ldxo = tk_ldy0 = tk_ld3

                        deps_f = [tk_rs7, tk_lda, tk_ldxo, tk_ldy0,
                                  tk_pe7[(half, tt)], dep_prev_out]
                        tk_f1 = S.add("vector", lambda v, tb=tb: VE(
                            v.scalar_tensor_tensor(
                                out=outb[tb][:, 1024:2048], in0=y_ps[tb][:],
                                scalar=rstd7[tb][:], in1=att7[tb][:, 1024:2048],
                                op0=OP.mult, op1=OP.add)),
                            deps=deps_f, incs=[("ve", 1)])
                        tk_f2 = S.add("vector", lambda v, tb=tb: VE(
                            v.tensor_tensor(out=outb[tb][:, 1024:2048],
                                            in0=outb[tb][:, 1024:2048],
                                            in1=xf7[tb][:, 1024:2048],
                                            op=OP.add)),
                            deps=[tk_f1], incs=[("ve", 1)])
                        tk_f3 = S.add("vector", lambda v, tb=tb: VE(
                            v.scalar_tensor_tensor(
                                out=outb[tb][:, 0:1024], in0=y0l[tb][:],
                                scalar=rstd7[tb][:], in1=att7[tb][:, 0:1024],
                                op0=OP.mult, op1=OP.add)),
                            deps=deps_f, incs=[("ve", 1)])
                        tk_fin = S.add("vector", lambda v, tb=tb: VE(
                            v.tensor_tensor(out=outb[tb][:, 0:1024],
                                            in0=outb[tb][:, 0:1024],
                                            in1=xf7[tb][:, 0:1024], op=OP.add)),
                            deps=[tk_f2, tk_f3], incs=[("ve", 1)])
                        if dev >= 2:
                            def dumps7(s2, tt=tt, tb=tb):
                                DA(s2.dma_start(
                                    out=att7_dump[tt * 128:(tt + 1) * 128, :],
                                    in_=att7[tb][:]))
                                DA(s2.dma_start(
                                    out=y0l_dump[tt * 128:(tt + 1) * 128, :],
                                    in_=y0l[tb][:]))
                                DA(s2.dma_start(
                                    out=rstd_dump2[tt * 128:(tt + 1) * 128, :],
                                    in_=rstd7[tb][:]))
                            S.add("scalar", dumps7,
                                  deps=[tk_fin], incs=[("dA", 48)])
                        tk_out[tt] = S.add("scalar", lambda s2, tt=tt, tb=tb: DA(
                            s2.dma_start(out=out[tt * 128:(tt + 1) * 128, :],
                                         in_=outb[tb][:])),
                            deps=[tk_fin], incs=[("dA", 16)])

            S.add("sync", lambda sy: sy.nop(),
                  deps=[{"dA": S.c["dA"], "dG": S.c["dG"], "dS": S.c["dS"]}])
            S.flush(block, sems, "P7")
    return nc


_NC_CACHE = {}


def _get_nc(dev=0):
    if dev not in _NC_CACHE:
        _NC_CACHE[dev] = build(dev)
    return _NC_CACHE[dev]


def make_in_maps(L0_attn_x, L0_attn_Wq, L0_attn_Wk, L0_attn_Wv, L0_attn_Wo,
                 L0_attn_lambda_init, L0_ffn_gate_up_packed, L0_ffn_gate_up_scales,
                 L0_ffn_down_packed, L0_ffn_down_scales):
    pi_h = _pi4(D)
    pi_i = _pi4(II)
    asc = np.ascontiguousarray

    xq = np.asarray(L0_attn_x)
    wq_p = asc(np.asarray(L0_attn_Wq)[pi_h])
    wk_p = asc(np.asarray(L0_attn_Wk)[pi_h])
    wv_p = asc(np.asarray(L0_attn_Wv)[pi_h])
    wo_a = np.asarray(L0_attn_Wo)
    lam_a = np.asarray(L0_attn_lambda_init)
    gu_rows = np.concatenate([pi_i, II + pi_i])
    gu16 = asc(np.asarray(L0_ffn_gate_up_packed).view(np.uint16)[gu_rows])
    gus_t = asc(np.asarray(L0_ffn_gate_up_scales).T[gu_rows])
    dn16 = asc(np.asarray(L0_ffn_down_packed).view(np.uint16))
    dns_t = asc(np.asarray(L0_ffn_down_scales).T)

    in_maps = []
    for c in range(8):
        b, hg = divmod(c, 2)
        R = hg * TOWN
        in_maps.append({
            "x": asc(xq[b]),
            "xown": asc(xq[b][R:R + TOWN]),
            "wq": asc(wq_p[:, hg * DK:(hg + 1) * DK]),
            "wk": asc(wk_p[:, hg * DK:(hg + 1) * DK]),
            "wv": asc(wv_p[:, hg * DK:(hg + 1) * DK]),
            "wo": asc(wo_a[hg * DK:(hg + 1) * DK, :]),
            "lam": asc(lam_a[hg * NH:(hg + 1) * NH].reshape(1, NH)),
            "gu16": gu16,
            "dn16": dn16,
            "gus": gus_t,
            "dns": dns_t,
            "ofs": np.array([[(TOWN - R) * D]], dtype=np.int32),
        })
    return in_maps


def kernel(L0_attn_x, L0_attn_Wq, L0_attn_Wk, L0_attn_Wv, L0_attn_Wo,
           L0_attn_lambda_init, L0_ffn_gate_up_packed, L0_ffn_gate_up_scales,
           L0_ffn_down_packed, L0_ffn_down_scales):
    in_maps = make_in_maps(L0_attn_x, L0_attn_Wq, L0_attn_Wk, L0_attn_Wv,
                           L0_attn_Wo, L0_attn_lambda_init,
                           L0_ffn_gate_up_packed, L0_ffn_gate_up_scales,
                           L0_ffn_down_packed, L0_ffn_down_scales)
    nc = _get_nc(0)
    res = run_bass_kernel_spmd(nc, in_maps, core_ids=list(range(8)))
    outp = np.empty((4, T, D), dtype=np.float32)
    for c in range(8):
        b, hg = divmod(c, 2)
        R = hg * TOWN
        outp[b, R:R + TOWN, :] = res.results[c]["out"]
    return outp
